# version 15
# baseline (speedup 1.0000x reference)
import os
import numpy as np
from contextlib import ExitStack
_NO_ACCUM = bool(os.environ.get('MK_NO_ACCUM'))
_B_NE = int(os.environ.get('MK_B_NE', '8'))
_B_NQ = int(os.environ.get('MK_B_NQ', '4'))
_B_GONLY = bool(os.environ.get('MK_B_GONLY'))
_B_MM1ONLY = bool(os.environ.get('MK_B_MM1ONLY'))
_B_NOTR = bool(os.environ.get('MK_B_NOTR'))
_B_NOSTORE = bool(os.environ.get('MK_B_NOSTORE'))

import concourse.bass as bass
import concourse.mybir as mybir
from concourse.bass_utils import run_bass_kernel_spmd

F32 = mybir.dt.float32
I32 = mybir.dt.int32
AX = mybir.AxisListType
OP = mybir.AluOpType
AF = mybir.ActivationFunctionType
IOA = bass.IndirectOffsetOnAxis

T = 4096
D = 1024
HID = 4096
E = 8
C = 1792
NGW = C // 4
SLOT_CH = C // 128
NT = T // 128
DC = D // 128
NH, DH = 16, 64
NSLOT = E * C
LN_EPS = 1e-5
ATT_EPS = 1e-6
BIG = 1e9

_SEMS = (["L", "G", "P", "V", "A", "Gc"]
         + [f"{p}{b}" for b in (0, 1) for p in
            ("xl", "w1", "w2", "ix", "xg", "as", "sc", "ld", "yg", "st", "ts", "qf")])


class MK:
    def __init__(self, debug=False, upto="all"):
        self.debug = debug
        self.upto = upto
        self.nc = nc = bass.Bass("TRN2", target_bir_lowering=False,
                                 detect_race_conditions=False)
        dbg = "ExternalOutput" if debug else "Internal"

        def dt_(name, shape, dtype=F32, kind="Internal"):
            return nc.dram_tensor(name, shape, dtype, kind=kind).ap()

        self.x_in = dt_("x", [T, D], kind="ExternalInput")
        self.wq = dt_("wq", [D, D], kind="ExternalInput")
        self.wk = dt_("wk", [D, D], kind="ExternalInput")
        self.wv = dt_("wv", [D, D], kind="ExternalInput")
        self.wo = dt_("wo", [D, D], kind="ExternalInput")
        self.wr = dt_("wr", [D, E], kind="ExternalInput")
        self.gamma = dt_("gamma", [D], kind="ExternalInput")
        self.beta = dt_("beta", [D], kind="ExternalInput")
        self.br = dt_("brr", [E], kind="ExternalInput")
        self.w1 = dt_("w1", [E, D, HID], kind="ExternalInput")
        self.b1 = dt_("b1", [E, HID], kind="ExternalInput")
        self.w2 = dt_("w2", [E, HID, D], kind="ExternalInput")
        self.b2 = dt_("b2", [E, D], kind="ExternalInput")
        self.out = dt_("out", [T, D], kind="ExternalOutput")

        self.v_d = dt_("v_d", [T, D], kind=dbg)
        self.xn_d = dt_("xn_d", [T, D], kind=dbg)
        self.pr_d = dt_("pr_d", [T, E], kind=dbg)
        self.eid_d = dt_("eid_d", [T, 4], I32, kind=dbg)
        self.adr_d = dt_("adr_d", [T, 4], I32, kind=dbg)
        self.sidx_d = dt_("sidx_d", [NSLOT, 1], I32, kind=dbg)
        self.y_d = dt_("y_d", [NSLOT, D], kind=dbg)
        self.c_d = dt_("c_d", [E, D], kind=dbg)
        self.ct_d = dt_("ct_d", [1, D], kind=dbg)
        self.x1_d = dt_("x1_d", [T, D], kind=dbg)
        self.x2_d = dt_("x2_d", [T, D], kind=dbg)
        self.qf_d = dt_("qf_d", [NT, 128, DC, 128], kind=dbg)

        self.S = {}
        self.N = {}
        self.mk = {}
        self._b_evac_tail = 0

    def w(self, eng, s, v):
        if v > 0:
            eng.wait_ge(self.S[s], v)

    def i(self, instr, s, n):
        self.N[s] += n
        instr.then_inc(self.S[s], n)
        return self.N[s]

    def sec(self, engname, fn):
        getattr(self.block, engname)(fn)

    def vi(self, v, ins, wait=True):
        m = self.i(ins, "V", 1)
        if wait:
            v.wait_ge(self.S["V"], m)
        return m

    def ai(self, s, ins, wait=False):
        m = self.i(ins, "A", 1)
        if wait:
            s.wait_ge(self.S["A"], m)
        return m

    def barrier(self):
        counts = {s: v for s, v in self.N.items() if v > 0}
        for eng in ("sync", "tensor", "vector", "scalar", "gpsimd"):
            def f(e, counts=counts):
                for s, v in counts.items():
                    e.wait_ge(self.S[s], v)
            self.sec(eng, f)

    def bank(self, k, lo=0, n=512):
        ps = self.psA if k < 4 else self.psB
        base = (k % 4) * 512
        return ps[:, base + lo: base + lo + n]

    def prologue(self):
        mk = self.mk

        def g(g):
            ins = g.memset(self.ident[:, :], 0.0)
            m = self.i(ins, "Gc", 1)
            g.wait_ge(self.S["Gc"], m)
            ins = g.affine_select(out=self.ident[:, :], in_=self.ident[:, :],
                                  compare_op=OP.not_equal, fill=1.0, base=0,
                                  pattern=[[-1, 128]], channel_multiplier=1)
            self.i(ins, "Gc", 1)
            ins = g.iota(self.iota8[:, :], pattern=[[1, E]], base=0,
                         channel_multiplier=0, allow_small_or_imprecise_dtypes=True)
            self.i(ins, "Gc", 1)
            ins = g.memset(self.eights[:, :], float(E))
            self.i(ins, "Gc", 1)
            ins = g.memset(self.zeros_i[:, :], 0)
            self.i(ins, "Gc", 1)
            ins = g.memset(self.cst[:, 0:1], LN_EPS)
            self.i(ins, "Gc", 1)
            ins = g.memset(self.cst[:, 1:2], ATT_EPS)
            self.i(ins, "Gc", 1)
            ins = g.memset(self.UT[:, :], 0.0)
            m = self.i(ins, "Gc", 1)
            g.wait_ge(self.S["Gc"], m)
            ins = g.affine_select(out=self.UT[:, :], in_=self.UT[:, :],
                                  compare_op=OP.is_ge, fill=1.0, base=0,
                                  pattern=[[-1, 128]], channel_multiplier=1)
            self.i(ins, "Gc", 1)
            ins = g.memset(self.ones1[:, :], 1.0)
            self.i(ins, "Gc", 1)
            ins = g.memset(self.onesc[:, :], 1.0)
            self.i(ins, "Gc", 1)
            ins = g.iota(self.iota84[:, :], pattern=[[0, 4], [1, E]], base=0,
                         channel_multiplier=0, allow_small_or_imprecise_dtypes=True)
            self.i(ins, "Gc", 1)
            ins = g.memset(self.eights4[:, :], float(E))
            self.i(ins, "Gc", 1)
            i1 = g.dma_start(out=self.gamma_bc[:, :],
                             in_=self.gamma.unsqueeze(0).broadcast_to((128, D)))
            self.i(i1, "G", 16)
            i2 = g.dma_start(out=self.beta_bc[:, :],
                             in_=self.beta.unsqueeze(0).broadcast_to((128, D)))
            self.i(i2, "G", 16)
            i3 = g.dma_start(out=self.br_bc[:, :],
                             in_=self.br.unsqueeze(0).broadcast_to((128, E)))
            self.i(i3, "G", 16)
            b1r = self.b1.rearrange("e (r p) -> (e r) p", p=128)
            i4 = g.dma_start(out=self.b1_rows[:, 0:128], in_=b1r[0:128, :])
            self.i(i4, "G", 16)
            i5 = g.dma_start(out=self.b1_rows[:, 128:256], in_=b1r[128:256, :])
            self.i(i5, "G", 16)
            i6 = g.dma_start(out=self.b2_rows[0:64, :],
                             in_=self.b2.rearrange("e (r p) -> (e r) p", p=128))
            self.i(i6, "G", 16)
            i7 = g.dma_start(out=self.wr_sb[:, :].rearrange("p (c e) -> p c e", c=DC),
                             in_=self.wr.rearrange("(c p) e -> p c e", p=128))
            mk["pro_g"] = self.i(i7, "G", 16)
        self.sec("gpsimd", g)

        def t(t):
            self.w(t, "G", mk["pro_g"])
            t.transpose(self.psA[:, 0:128], self.b1_rows[:, 0:128], self.ident[:, :])
            t.transpose(self.psA[:, 512:640], self.b1_rows[:, 128:256], self.ident[:, :])
            ins = t.transpose(self.psA[:, 1024:1088], self.b2_rows[0:64, :],
                              self.ident[0:64, 0:64])
            mk["pro_t"] = self.i(ins, "P", 1)
        self.sec("tensor", t)

        def v(v):
            self.w(v, "P", mk["pro_t"])
            v.tensor_copy(self.b1_sb[:, 0:128], self.psA[:, 0:128])
            v.tensor_copy(self.b1_sb[:, 128:256], self.psA[:, 512:640])
            ins = v.tensor_copy(self.b2_sb[:, 0:64], self.psA[:, 1024:1088])
            mk["pro_v"] = self.i(ins, "V", 1)
        self.sec("vector", v)

        def s(s):
            self.w(s, "V", mk["pro_v"])
            ins = s.activation(self.rb1_sb[:, :], self.b1_sb[:, :], AF.Relu)
            mk["pro_a"] = self.i(ins, "A", 1)
        self.sec("scalar", s)

    def phase_A(self, call, xin, with_b0=False):
        nc, mk = self.nc, self.mk
        P_ = f"A{call}_"
        BT = 4
        NG = NT // BT
        W = BT * E
        with ExitStack() as es:
            sb = lambda n, sh, dt=F32: es.enter_context(nc.sbuf_tensor(P_ + n, sh, dt))
            if with_b0:
                w2qh = [sb(f"b0w{b}", [128, 4 * D]) for b in range(2)]
                csb = sb("b0c", [128, E])

            def b0_mms(e):
                for q in range(4):
                    for hf in range(2):
                        li = e * 8 + q * 2 + hf

                        def ld(g, e=e, q=q, hf=hf, li=li):
                            if li >= 2:
                                self.w(g, "P", mk[f"b0c_{li-2}"])
                            ins = g.dma_start(
                                out=w2qh[li % 2][:, :]
                                    .rearrange("p (h c) -> p h c", h=4),
                                in_=self.w2[e, q * 1024 + hf * 512:
                                            q * 1024 + (hf + 1) * 512, :]
                                    .rearrange("(h p) c -> p h c", p=128))
                            mk[f"b0l_{li}"] = self.i(ins, f"w2{li % 2}", 16)
                        self.sec("sync", ld)

                        def mm(t, e=e, q=q, hf=hf, li=li):
                            if li == e * 8:
                                self.w(t, "A", mk["pro_a"] if e == 0
                                       else mk[f"b0ev_{e-1}"])
                            self.w(t, f"w2{li % 2}", mk[f"b0l_{li}"])
                            for oc in range(DC):
                                for h4 in range(4):
                                    hq = q * 8 + hf * 4 + h4
                                    ins = t.matmul(
                                        self.bank(2, oc, 1),
                                        w2qh[li % 2][:, h4 * D + oc * 128:
                                                     h4 * D + (oc + 1) * 128],
                                        self.rb1_sb[:, e * 32 + hq:
                                                    e * 32 + hq + 1],
                                        start=(q == 0 and hf == 0 and oc == 0
                                               and h4 == 0),
                                        stop=(q == 3 and hf == 1 and oc == DC - 1
                                              and h4 == 3),
                                        skip_group_check=True)
                            mk[f"b0c_{li}"] = self.i(ins, "P", 1)
                        self.sec("tensor", mm)
                mk[f"b0m_{e}"] = self.N["P"]

            def b0_evac(s, e):
                self.w(s, "P", mk[f"b0m_{e}"])
                if e >= 1:
                    self.w(s, "G", mk[f"b0st_{e-1}"])
                for oc in range(DC):
                    ins = s.activation(csb[:, oc:oc + 1], self.bank(2, oc, 1),
                                       AF.Identity,
                                       bias=self.b2_sb[:, e * 8 + oc:e * 8 + oc + 1])
                    self.i(ins, "A", 1)
                mk[f"b0ev_{e}"] = self.N["A"]

            def b0_store(g, e):
                self.w(g, "A", mk[f"b0ev_{e}"])
                for oc in range(DC):
                    ins = g.dma_start(out=self.c_d[e:e + 1, oc * 128:(oc + 1) * 128],
                                      in_=csb[:, oc:oc + 1])
                    self.i(ins, "G", 16)
                mk[f"b0st_{e}"] = self.N["G"]
            xt = [sb(f"xt{b}", [128, BT * D]) for b in range(2)]
            vt = [sb(f"vt{b}", [128, BT * D]) for b in range(2)]
            xnt = [sb(f"xnt{b}", [128, BT * D]) for b in range(2)]
            xnT = [sb(f"xnT{b}", [128, BT * D]) for b in range(2)]
            tmp = sb("tmp", [128, BT * D])
            sc4 = sb("sc4", [128, 16])
            lg = sb("lg", [128, W])
            prb = [sb(f"prb{b}", [128, W]) for b in range(2)]
            pw = sb("pw", [128, W])
            w8 = sb("w8", [128, W])
            w8i = sb("w8i", [128, W], I32)
            cnd = sb("cnd", [128, W])
            Yt = sb("Yt", [128, W])
            ejs = [sb(f"ejs{b}", [128, BT * 4]) for b in range(2)]
            eidt = [sb(f"eidt{b}", [128, BT * 4], I32) for b in range(2)]
            Pt = sb("Pt", [128, W])
            adrf = [sb(f"adrf{b}", [128, BT * 4]) for b in range(2)]
            adri = [sb(f"adri{b}", [128, BT * 4], I32) for b in range(2)]
            tok = [sb(f"tok{b}", [128, BT], I32) for b in range(2)]
            csrow = sb("csrow", [1, W])
            corr = sb("corr", [1, W])
            base = sb("base", [1, E])

            def bc(ap2d, inner):
                return ap2d.unsqueeze(2).broadcast_to((128, ap2d.shape[1], inner))

            def col4(t_, j):
                return t_[:, :].rearrange("p (a c) -> p a c", a=BT)[:, :, j]

            def col4k(t_, j):
                return t_[:, :].rearrange("p (a c) -> p a c", a=BT)[:, :, j:j + 1]

            def g0(g):
                ins = g.dma_start(
                    out=self.sidx_d.rearrange("(p a) one -> p (a one)", p=128),
                    in_=self.zeros_i[:, 0:NSLOT // 128])
                mk[P_ + "sz"] = self.i(ins, "G", 16)
            self.sec("gpsimd", g0)

            def v00(v):
                ins = v.memset(base[:, :], 0.0)
                mk[P_ + "b0"] = self.vi(v, ins)
            self.sec("vector", v00)

            def s0(s):
                for j in range(2):
                    ins = s.dma_start(
                        out=xt[j][:, :].rearrange("p (a c) -> p a c", a=BT),
                        in_=xin[j * 512:(j + 1) * 512, :]
                            .rearrange("(a p) c -> p a c", p=128))
                    mk[P_ + f"xl{j}"] = self.i(ins, f"xl{j}", 16)
            self.sec("sync", s0)

            for i in range(NG):
                b = i % 2
                x4 = lambda t_: t_[:, :].rearrange("p (a c) -> p a c", a=BT)

                if with_b0 and i >= 1:
                    def ab0(s, i=i):
                        b0_evac(s, i - 1)
                    self.sec("scalar", ab0)

                    def gb0s(g, i=i):
                        b0_store(g, i - 1)
                    self.sec("gpsimd", gb0s)

                def v1(v, i=i, b=b):
                    self.w(v, f"xl{b}", mk[P_ + f"xl{i}"])
                    if i >= 2:
                        self.w(v, f"as{b}", mk[P_ + f"as{i-2}"])
                    self.vi(v, v.tensor_reduce(sc4[:, 0:BT], x4(xt[b]), AX.X, OP.add))
                    self.vi(v, v.tensor_scalar(sc4[:, 0:BT], sc4[:, 0:BT], 1.0 / D,
                                               None, OP.mult))
                    self.vi(v, v.tensor_tensor(x4(vt[b]), x4(xt[b]),
                                               bc(sc4[:, 0:BT], D), OP.subtract),
                            wait=False)
                    self.vi(v, v.tensor_tensor(tmp[:, :], vt[b][:, :], vt[b][:, :],
                                               OP.mult), wait=False)
                    mk[P_ + f"ln1{i}"] = self.vi(
                        v, v.tensor_reduce(sc4[:, 4:4 + BT], x4(tmp), AX.X, OP.add),
                        wait=False)
                self.sec("vector", v1)

                def a1(s, i=i):
                    self.w(s, "V", mk[P_ + f"ln1{i}"])
                    ins = s.activation(sc4[:, 4:4 + BT], sc4[:, 4:4 + BT], AF.Sqrt,
                                       bias=self.cst[:, 0:1], scale=1.0 / D)
                    mk[P_ + f"rs1{i}"] = self.i(ins, "A", 1)
                self.sec("scalar", a1)

                def v2(v, i=i, b=b):
                    self.w(v, "A", mk[P_ + f"rs1{i}"])
                    self.vi(v, v.reciprocal(sc4[:, 4:4 + BT], sc4[:, 4:4 + BT]))
                    self.vi(v, v.tensor_tensor(x4(vt[b]), x4(vt[b]),
                                               bc(sc4[:, 4:4 + BT], D), OP.mult),
                            wait=False)
                    self.vi(v, v.tensor_reduce(sc4[:, 8:8 + BT], x4(vt[b]), AX.X,
                                               OP.add))
                    self.vi(v, v.tensor_scalar(sc4[:, 8:8 + BT], sc4[:, 8:8 + BT],
                                               1.0 / D, None, OP.mult))
                    self.vi(v, v.tensor_tensor(x4(xnt[b]), x4(vt[b]),
                                               bc(sc4[:, 8:8 + BT], D), OP.subtract),
                            wait=False)
                    self.vi(v, v.tensor_tensor(tmp[:, :], xnt[b][:, :], xnt[b][:, :],
                                               OP.mult), wait=False)
                    mk[P_ + f"ln2{i}"] = self.vi(
                        v, v.tensor_reduce(sc4[:, 12:12 + BT], x4(tmp), AX.X, OP.add),
                        wait=False)
                self.sec("vector", v2)

                def a2(s, i=i):
                    self.w(s, "V", mk[P_ + f"ln2{i}"])
                    ins = s.activation(sc4[:, 12:12 + BT], sc4[:, 12:12 + BT], AF.Sqrt,
                                       bias=self.cst[:, 0:1], scale=1.0 / D)
                    mk[P_ + f"rs2{i}"] = self.i(ins, "A", 1)
                self.sec("scalar", a2)

                def v3(v, i=i, b=b):
                    self.w(v, "A", mk[P_ + f"rs2{i}"])
                    self.vi(v, v.reciprocal(sc4[:, 12:12 + BT], sc4[:, 12:12 + BT]))
                    self.vi(v, v.tensor_tensor(x4(xnt[b]), x4(xnt[b]),
                                               bc(sc4[:, 12:12 + BT], D), OP.mult),
                            wait=False)
                    gam = self.gamma_bc[:, :].unsqueeze(1).broadcast_to((128, BT, D))
                    bet = self.beta_bc[:, :].unsqueeze(1).broadcast_to((128, BT, D))
                    self.vi(v, v.tensor_tensor(x4(xnt[b]), x4(xnt[b]), gam, OP.mult),
                            wait=False)
                    mk[P_ + f"xn{i}"] = self.vi(
                        v, v.tensor_tensor(x4(xnt[b]), x4(xnt[b]), bet, OP.add),
                        wait=False)
                self.sec("vector", v3)

                for st in range(BT):
                    def t1(t, i=i, b=b, st=st):
                        if st == 0:
                            self.w(t, "V", mk[P_ + f"xn{i}"])
                            if i >= 1:
                                self.w(t, "V", mk[P_ + f"xnTc{i-1}_{BT-1}"])
                        else:
                            self.w(t, "V", mk[P_ + f"xnTc{i}_{st-1}"])
                        for fc in range(DC):
                            k = 0 if fc % 2 == 0 else 4
                            ins = t.transpose(self.bank(k, (fc // 2) * 128, 128),
                                              xnt[b][:, (st * DC + fc) * 128:
                                                     (st * DC + fc + 1) * 128],
                                              self.ident[:, :])
                        mk[P_ + f"xtr{i}_{st}"] = self.i(ins, "P", 1)
                    self.sec("tensor", t1)

                    def v4(v, i=i, b=b, st=st):
                        self.w(v, "P", mk[P_ + f"xtr{i}_{st}"])
                        for fc in range(DC):
                            k = 0 if fc % 2 == 0 else 4
                            m = self.vi(v, v.tensor_copy(
                                xnT[b][:, (st * DC + fc) * 128:(st * DC + fc + 1) * 128],
                                self.bank(k, (fc // 2) * 128, 128)), wait=False)
                        mk[P_ + f"xnTc{i}_{st}"] = m
                    self.sec("vector", v4)

                def t2(t, i=i, b=b):
                    self.w(t, "V", mk[P_ + f"xnTc{i}_{BT-1}"])
                    if i >= 1:
                        self.w(t, "V", mk[P_ + f"lg{i-1}"])
                    for st in range(BT):
                        for fc in range(DC):
                            ins = t.matmul(self.bank(5, st * E, E),
                                           xnT[b][:, (st * DC + fc) * 128:
                                                  (st * DC + fc + 1) * 128],
                                           self.wr_sb[:, fc * E:(fc + 1) * E],
                                           start=(fc == 0), stop=(fc == DC - 1),
                                           skip_group_check=True)
                    mk[P_ + f"rmm{i}"] = self.i(ins, "P", 1)
                self.sec("tensor", t2)

                def v5(v, i=i, b=b):
                    self.w(v, "P", mk[P_ + f"rmm{i}"])
                    brb = self.br_bc[:, :].unsqueeze(1).broadcast_to((128, BT, E))
                    lg3 = lg[:, :].rearrange("p (a c) -> p a c", a=BT)
                    mk[P_ + f"lg{i}"] = self.vi(
                        v, v.tensor_tensor(lg3, self.bank(5, 0, W)
                                           .rearrange("p (a c) -> p a c", a=BT),
                                           brb, OP.add))
                    self.vi(v, v.reduce_max(sc4[:, 0:BT], lg3, AX.X))
                    self.vi(v, v.tensor_tensor(lg3, lg3, bc(sc4[:, 0:BT], E),
                                               OP.subtract))
                    ins = v.engine_nop()
                    mk[P_ + f"nm{i}"] = self.i(ins, "V", 1)
                self.sec("vector", v5)

                def a3(s, i=i, b=b):
                    self.w(s, "V", mk[P_ + f"nm{i}"])
                    if i >= 2:
                        self.w(s, f"as{b}", mk[P_ + f"as{i-2}"])
                    ins = s.activation(prb[b][:, :], lg[:, :], AF.Exp)
                    mk[P_ + f"ex{i}"] = self.i(ins, "A", 1)
                self.sec("scalar", a3)

                def v6(v, i=i, b=b):
                    self.w(v, "A", mk[P_ + f"ex{i}"])
                    p3 = prb[b][:, :].rearrange("p (a c) -> p a c", a=BT)
                    self.vi(v, v.tensor_reduce(sc4[:, 0:BT], p3, AX.X, OP.add))
                    self.vi(v, v.reciprocal(sc4[:, 0:BT], sc4[:, 0:BT]))
                    self.vi(v, v.tensor_tensor(p3, p3, bc(sc4[:, 0:BT], E), OP.mult))
                    self.vi(v, v.tensor_copy(pw[:, :], prb[b][:, :]))
                    pw3 = pw[:, :].rearrange("p (a c) -> p a c", a=BT)
                    w83 = w8[:, :].rearrange("p (a c) -> p a c", a=BT)
                    cn3 = cnd[:, :].rearrange("p (a c) -> p a c", a=BT)
                    for j in range(3):
                        self.vi(v, v.reduce_max(sc4[:, 4:4 + BT], pw3, AX.X))
                        self.vi(v, v.tensor_tensor(
                            w8i[:, :].rearrange("p (a c) -> p a c", a=BT), pw3,
                            bc(sc4[:, 4:4 + BT], E), OP.is_equal))
                        self.vi(v, v.select(cnd[:, :], w8i[:, :], self.iota84[:, :],
                                            self.eights4[:, :]))
                        self.vi(v, v.tensor_reduce(col4k(ejs[b], j), cn3, AX.X,
                                                   OP.min))
                        ejcol = col4(ejs[b], j)
                        self.vi(v, v.tensor_tensor(w83, self.iota84[:, :]
                                                   .rearrange("p (a c) -> p a c", a=BT),
                                                   bc(ejcol, E), OP.is_equal))
                        if j == 0:
                            self.vi(v, v.tensor_copy(Yt[:, :], w8[:, :]))
                        else:
                            self.vi(v, v.tensor_add(Yt[:, :], Yt[:, :], w8[:, :]))
                        self.vi(v, v.scalar_tensor_tensor(pw[:, :], w8[:, :], -BIG,
                                                          pw[:, :], OP.mult, OP.add))
                    self.vi(v, v.memset(col4(ejs[b], 3), 0.0))
                    mk[P_ + f"top{i}"] = self.vi(
                        v, v.tensor_copy(eidt[b][:, :], ejs[b][:, :]), wait=False)
                self.sec("vector", v6)

                def t3(t, i=i, b=b):
                    self.w(t, "V", mk[P_ + f"top{i}"])
                    if i >= 1:
                        self.w(t, "V", mk[P_ + f"csc{i-1}"])
                    for st in range(BT):
                        t.matmul(self.bank(7, st * E, E), self.UT[:, :],
                                 Yt[:, st * E:(st + 1) * E],
                                 start=(st == 0), stop=False, skip_group_check=True)
                    ins = t.matmul(self.bank(6, 0, W)[0:1, :], self.onesc[:, :],
                                   Yt[:, :], start=True, stop=True,
                                   skip_group_check=True)
                    mk[P_ + f"cs{i}"] = self.i(ins, "P", 1)
                self.sec("tensor", t3)

                def v7(v, i=i):
                    self.w(v, "P", mk[P_ + f"cs{i}"])
                    self.vi(v, v.tensor_copy(csrow[:, :], self.bank(6, 0, W)[0:1, :]))
                    self.vi(v, v.tensor_copy(corr[0:1, 0:E], base[0:1, :]))
                    for st in range(1, BT):
                        self.vi(v, v.tensor_add(
                            corr[0:1, st * E:(st + 1) * E],
                            corr[0:1, (st - 1) * E:st * E],
                            csrow[0:1, (st - 1) * E:st * E]))
                    self.vi(v, v.tensor_add(base[0:1, :],
                                            corr[0:1, (BT - 1) * E:BT * E],
                                            csrow[0:1, (BT - 1) * E:BT * E]))
                    ins = v.engine_nop()
                    mk[P_ + f"csc{i}"] = self.i(ins, "V", 1)
                self.sec("vector", v7)

                def t4(t, i=i):
                    self.w(t, "V", mk[P_ + f"csc{i}"])
                    ins = t.matmul(self.bank(7, 0, W), self.ones1[0:1, 0:128],
                                   corr[0:1, :], start=False, stop=True,
                                   skip_group_check=True)
                    mk[P_ + f"pb{i}"] = self.i(ins, "P", 1)
                self.sec("tensor", t4)

                if with_b0 and i < E:
                    b0_mms(i)

                def v8(v, i=i, b=b):
                    self.w(v, "P", mk[P_ + f"pb{i}"])
                    if i >= 2:
                        self.w(v, f"sc{b}", mk[P_ + f"sct{i-2}"])
                    self.vi(v, v.tensor_copy(Pt[:, :], self.bank(7, 0, W)))
                    pt3 = Pt[:, :].rearrange("p (a c) -> p a c", a=BT)
                    w83 = w8[:, :].rearrange("p (a c) -> p a c", a=BT)
                    for j in range(3):
                        ejcol = col4(ejs[b], j)
                        adcol = col4(adrf[b], j)
                        self.vi(v, v.tensor_tensor(w83, self.iota84[:, :]
                                                   .rearrange("p (a c) -> p a c", a=BT),
                                                   bc(ejcol, E), OP.is_equal))
                        self.vi(v, v.tensor_tensor(w83, w83, pt3, OP.mult))
                        self.vi(v, v.tensor_reduce(col4k(adrf[b], j), w83, AX.X,
                                                   OP.add))
                        self.vi(v, v.tensor_scalar(adcol, adcol, float(C - 1), None,
                                                   OP.min))
                        self.vi(v, v.tensor_scalar(tmp[:, 0:BT], ejcol, float(C),
                                                   None, OP.mult))
                        self.vi(v, v.tensor_tensor(adcol, adcol, tmp[:, 0:BT], OP.add))
                    self.vi(v, v.memset(col4(adrf[b], 3), 0.0))
                    mk[P_ + f"p2v{i}"] = self.vi(
                        v, v.tensor_copy(adri[b][:, :], adrf[b][:, :]), wait=False)
                self.sec("vector", v8)

                def g1(g, i=i, b=b):
                    self.w(g, "V", mk[P_ + f"p2v{i}"])
                    if i == 0:
                        self.w(g, "G", mk[P_ + "sz"])
                    if i >= 2:
                        self.w(g, f"sc{b}", mk[P_ + f"sct{i-2}"])
                    ins = g.iota(tok[b][:, :], pattern=[[128, BT]], base=i * 512,
                                 channel_multiplier=1)
                    m = self.i(ins, "Gc", 1)
                    g.wait_ge(self.S["Gc"], m)
                    rows = slice(i * 512, (i + 1) * 512)
                    ins = g.dma_start(out=self.v_d[rows, :]
                                      .rearrange("(a p) c -> p a c", p=128),
                                      in_=vt[b][:, :].rearrange("p (a c) -> p a c", a=BT))
                    self.i(ins, f"as{b}", 16)
                    ins = g.dma_start(out=self.xn_d[rows, :]
                                      .rearrange("(a p) c -> p a c", p=128),
                                      in_=xnt[b][:, :].rearrange("p (a c) -> p a c", a=BT))
                    self.i(ins, f"as{b}", 16)
                    ins = g.dma_start(out=self.pr_d[rows, :]
                                      .rearrange("(a p) c -> p a c", p=128),
                                      in_=prb[b][:, :].rearrange("p (a c) -> p a c", a=BT))
                    self.i(ins, f"as{b}", 16)
                    ins = g.dma_start(out=self.eid_d[rows, :]
                                      .rearrange("(a p) c -> p a c", p=128),
                                      in_=eidt[b][:, :].rearrange("p (a c) -> p a c", a=BT))
                    mk[P_ + f"as{i}"] = self.i(ins, f"as{b}", 16)
                    ins = g.dma_start(out=self.adr_d[rows, :]
                                      .rearrange("(a p) c -> p a c", p=128),
                                      in_=adri[b][:, :].rearrange("p (a c) -> p a c", a=BT))
                    self.i(ins, f"sc{b}", 16)
                    for st in range(BT):
                        for j in range(3):
                            ins = g.indirect_dma_start(
                                out=self.sidx_d[:, :],
                                out_offset=IOA(adri[b][:, st * 4 + j:st * 4 + j + 1], 0),
                                in_=tok[b][:, st:st + 1], in_offset=None)
                            self.i(ins, f"sc{b}", 16)
                    mk[P_ + f"sct{i}"] = self.N[f"sc{b}"]
                self.sec("gpsimd", g1)

                if i + 2 < NG:
                    def s1(s, i=i, b=b):
                        self.w(s, "V", mk[P_ + f"ln1{i}"])
                        ins = s.dma_start(
                            out=xt[b][:, :].rearrange("p (a c) -> p a c", a=BT),
                            in_=xin[(i + 2) * 512:(i + 3) * 512, :]
                                .rearrange("(a p) c -> p a c", p=128))
                        mk[P_ + f"xl{i+2}"] = self.i(ins, f"xl{b}", 16)
                    self.sec("sync", s1)

            if with_b0:
                def ab0f(s):
                    b0_evac(s, E - 1)
                self.sec("scalar", ab0f)

                def gb0f(g):
                    b0_store(g, E - 1)
                self.sec("gpsimd", gb0f)

    def phase_B0(self):
        nc, mk = self.nc, self.mk
        with ExitStack() as es:
            w2q = [es.enter_context(nc.sbuf_tensor(f"B0w2q{b}", [128, E * D], F32))
                   for b in range(2)]
            csb = es.enter_context(nc.sbuf_tensor("B0csb", [128, E], F32))
            cacc = es.enter_context(nc.sbuf_tensor("B0cacc", [128, E], F32))
            NQ = 4
            for e in range(E):
                for q in range(NQ):
                    b = (e * NQ + q) % 2

                    def s0(s, e=e, q=q, b=b):
                        if e * NQ + q >= 2:
                            self.w(s, "P", mk[f"B0mm{e*NQ+q-2}"])
                        ins = s.dma_start(
                            out=w2q[b][:, :].rearrange("p (h c) -> p h c", h=8),
                            in_=self.w2[e, q * 1024:(q + 1) * 1024, :]
                                .rearrange("(h p) c -> p h c", p=128))
                        mk[f"B0l{e*NQ+q}"] = self.i(ins, f"w2{b}", 16)
                    self.sec("sync", s0)

                    def t0(t, e=e, q=q, b=b):
                        self.w(t, f"w2{b}", mk[f"B0l{e*NQ+q}"])
                        if e == 0 and q == 0:
                            self.w(t, "A", mk["pro_a"])
                        if q == 0 and e > 0:
                            self.w(t, "V", mk[f"B0v{e-1}"])
                        for oc in range(DC):
                            for hc in range(8):
                                ins = t.matmul(
                                    self.bank(2, oc, 1),
                                    w2q[b][:, hc * D + oc * 128:hc * D + (oc + 1) * 128],
                                    self.rb1_sb[:, e * 32 + q * 8 + hc:
                                                e * 32 + q * 8 + hc + 1],
                                    start=(q == 0 and oc == 0 and hc == 0),
                                    stop=(q == NQ - 1 and oc == DC - 1 and hc == 7),
                                    skip_group_check=True)
                        mk[f"B0mm{e*NQ+q}"] = self.i(ins, "P", 1)
                    self.sec("tensor", t0)

                def v0(v, e=e):
                    self.w(v, "P", mk[f"B0mm{e*NQ+NQ-1}"])
                    for oc in range(DC):
                        m = self.vi(v, v.tensor_tensor(csb[:, oc:oc + 1], self.bank(2, oc, 1),
                                                       self.b2_sb[:, e * 8 + oc:e * 8 + oc + 1],
                                                       OP.add), wait=False)
                    v.wait_ge(self.S["V"], m)
                    if e == 0:
                        ins = v.tensor_copy(cacc[:, :], csb[:, :])
                    else:
                        ins = v.tensor_add(cacc[:, :], cacc[:, :], csb[:, :])
                    mk[f"B0v{e}"] = self.vi(v, ins)
                self.sec("vector", v0)

                def g0(g, e=e):
                    self.w(g, "V", mk[f"B0v{e}"])
                    for oc in range(DC):
                        ins = g.dma_start(out=self.c_d[e:e + 1, oc * 128:(oc + 1) * 128],
                                          in_=csb[:, oc:oc + 1])
                        self.i(ins, "G", 16)
                    mk[f"B0st{e}"] = self.N["G"]
                self.sec("gpsimd", g0)

            def gf(g):
                self.w(g, "V", mk[f"B0v{E-1}"])
                self.w(g, "G", mk[f"B0st{E-1}"])
                for oc in range(DC):
                    ins = g.dma_start(out=self.ct_d[0:1, oc * 128:(oc + 1) * 128],
                                      in_=cacc[:, oc:oc + 1])
                    self.i(ins, "G", 16)
                g.wait_ge(self.S["G"], self.N["G"])
                ins = g.dma_start(out=self.ctot_bc[:, :],
                                  in_=self.ct_d.broadcast_to((128, D)))
                mk["B0done"] = self.i(ins, "G", 16)
            self.sec("gpsimd", gf)

    def phase_B(self, call):
        nc, mk = self.nc, self.mk
        P_ = f"B{call}_"
        with ExitStack() as es:
            sb = lambda n, sh, dt=F32: es.enter_context(nc.sbuf_tensor(P_ + n, sh, dt))
            xnT = sb("xnT", [128, DC * C])
            hs = sb("hs", [128, 8 * C])
            w1b = [sb(f"w1b{b}", [128, 1024]) for b in range(2)]
            w2h = [sb(f"w2h{b}", [128, 8 * 512]) for b in range(2)]
            xg = [sb(f"xg{b}", [128, D]) for b in range(2)]
            ixb = [sb(f"ixb{b}", [128, 1], I32) for b in range(2)]
            yst = [sb(f"yst{b}", [128, 512]) for b in range(2)]
            b2bc = sb("b2bc", [128, D])

            self._b_evac_tail = 0
            for e in range(min(E, _B_NE)):
                EP = f"{P_}{e}_"

                def gb2(g, e=e):
                    if e > 0:
                        self.w(g, "V", mk[P_ + f"b2done{e-1}"])
                    ins = g.dma_start(out=b2bc[:, :],
                                      in_=self.b2[e:e + 1, :].broadcast_to((128, D)))
                    mk[P_ + f"b2l{e}"] = self.i(ins, "G", 16)
                self.sec("gpsimd", gb2)
                for sc in range(SLOT_CH):
                    b = sc % 2
                    gi = e * SLOT_CH + sc

                    def gg(g, sc=sc, b=b, gi=gi, e=e):
                        if gi >= 2:
                            self.w(g, "P", mk[P_ + f"gxtr{gi-2}"])
                            self.w(g, f"xg{b}", mk[P_ + f"gxl{gi-2}"])
                        ins = g.dma_start(
                            out=ixb[b][:, :],
                            in_=self.sidx_d[e * C + sc * 128:e * C + (sc + 1) * 128, :])
                        ixm = self.i(ins, f"ix{b}", 16)
                        g.wait_ge(self.S[f"ix{b}"], ixm)
                        ins = g.indirect_dma_start(
                            out=xg[b][:, :], out_offset=None,
                            in_=self.xn_d[:, :], in_offset=IOA(ixb[b][:, :], 0))
                        mk[P_ + f"gxl{gi}"] = self.i(ins, f"xg{b}", 16)
                    self.sec("gpsimd", gg)

                    def tt(t, sc=sc, b=b, gi=gi):
                        self.w(t, f"xg{b}", mk[P_ + f"gxl{gi}"])
                        if sc == 0:
                            self.w(t, "V", self._b_evac_tail)
                        if gi >= 1:
                            self.w(t, "V", mk[P_ + f"gxc{gi-1}"])
                        for fc in range(DC):
                            k = 0 if fc % 2 == 0 else 4
                            ins = t.transpose(self.bank(k, (fc // 2) * 128, 128),
                                              xg[b][:, fc * 128:(fc + 1) * 128],
                                              self.ident[:, :])
                        mk[P_ + f"gxtr{gi}"] = self.i(ins, "P", 1)
                    self.sec("tensor", tt)

                    def vv(v, sc=sc, gi=gi):
                        self.w(v, "P", mk[P_ + f"gxtr{gi}"])
                        for fc in range(DC):
                            k = 0 if fc % 2 == 0 else 4
                            m = self.vi(v, v.tensor_copy(
                                xnT[:, fc * C + sc * 128:fc * C + (sc + 1) * 128],
                                self.bank(k, (fc // 2) * 128, 128)), wait=False)
                        mk[P_ + f"gxc{gi}"] = m
                    self.sec("vector", vv)

                for q in range(0 if _B_GONLY else min(4, _B_NQ)):
                    QP = f"{EP}q{q}_"
                    ts_q0, ts_q1 = self.N["ts0"], self.N["ts1"]

                    for hc in range(8):
                        hq = q * 8 + hc
                        b = hc % 2
                        grp = 0 if hc % 2 == 0 else 4
                        wix = (call * 8 + e) * 32 + hq

                        def sw1(s, e=e, hq=hq, b=b, wix=wix):
                            if wix >= 2:
                                self.w(s, "P", mk[f"mm1g_{wix-2}"])
                            ins = s.dma_start(
                                out=w1b[b][:, :].rearrange("p (a c) -> p a c", a=DC),
                                in_=self.w1[e, :, hq * 128:(hq + 1) * 128]
                                    .rearrange("(a p) c -> p a c", p=128))
                            mk[f"w1l{wix}"] = self.i(ins, f"w1{b}", 16)
                        self.sec("sync", sw1)

                        def tm1(t, e=e, q=q, hc=hc, b=b, hq=hq, grp=grp, wix=wix):
                            self.w(t, f"w1{b}", mk[f"w1l{wix}"])
                            if hc == 0:
                                self.w(t, "V", mk[P_ + f"gxc{e*SLOT_CH+SLOT_CH-1}"])
                            if hc < 2:
                                self.w(t, "V", self._b_evac_tail)
                            if wix >= 2:
                                self.w(t, "A", mk[f"ev1_{wix-2}"])
                            for fc in range(DC):
                                for Ng in range(4):
                                    ins = t.matmul(
                                        self.bank(grp + Ng, 0, NGW),
                                        w1b[b][:, fc * 128:(fc + 1) * 128],
                                        xnT[:, fc * C + Ng * NGW:fc * C + (Ng + 1) * NGW],
                                        start=(fc == 0), stop=(fc == DC - 1),
                                        skip_group_check=True)
                            mk[f"mm1g_{wix}"] = self.i(ins, "P", 1)
                        self.sec("tensor", tm1)

                        def am1(s, e=e, hc=hc, hq=hq, grp=grp, wix=wix):
                            self.w(s, "P", mk[f"mm1g_{wix}"])
                            for Ng in range(4):
                                ins = s.activation(
                                    hs[:, hc * C + Ng * NGW:hc * C + (Ng + 1) * NGW],
                                    self.bank(grp + Ng, 0, NGW), AF.Relu,
                                    bias=self.b1_sb[:, e * 32 + hq:e * 32 + hq + 1])
                                m = self.i(ins, "A", 1)
                            mk[f"ev1_{wix}"] = m
                        self.sec("scalar", am1)

                    for half in range(2):
                        widx = ((call * 8 + e) * 4 + q) * 2 + half
                        wb = widx % 2

                        def sw2(s, e=e, q=q, half=half, widx=widx, wb=wb):
                            if widx >= 2:
                                self.w(s, "P", mk[f"mm2w_{widx-2}"])
                            for hc in range(8):
                                ins = s.dma_start(
                                    out=w2h[wb][:, hc * 512:(hc + 1) * 512],
                                    in_=self.w2[e, q * 1024 + hc * 128:
                                                q * 1024 + (hc + 1) * 128,
                                                half * 512:(half + 1) * 512])
                                self.i(ins, f"w2{wb}", 16)
                            mk[f"w2l{widx}"] = self.N[f"w2{wb}"]
                        self.sec("sync", sw2)

                        for sblock in range(SLOT_CH):
                            gidx = ((call * 8 + e) * 4 + q) * 2 * SLOT_CH \
                                + half * SLOT_CH + sblock
                            bk = gidx % 8
                            tsb = gidx % 2

                            def tm2(t, e=e, q=q, half=half, sblock=sblock,
                                    widx=widx, wb=wb, gidx=gidx, bk=bk):
                                if half == 0 and sblock == 0:
                                    self.w(t, "A", mk[f"ev1_{(call*8+e)*32 + q*8 + 7}"])
                                if sblock == 0:
                                    self.w(t, f"w2{wb}", mk[f"w2l{widx}"])
                                pm = mk.get(f"yev_{gidx-8}", 0)
                                self.w(t, "V", pm)
                                for hc in range(8):
                                    ins = t.matmul(
                                        self.bank(bk, 0, 512),
                                        hs[:, hc * C + sblock * 128:
                                           hc * C + (sblock + 1) * 128],
                                        w2h[wb][:, hc * 512:(hc + 1) * 512],
                                        start=(hc == 0), stop=(hc == 7),
                                        skip_group_check=True)
                                m = self.i(ins, "P", 1)
                                mk[f"mm2n_{gidx}"] = m
                                if sblock == SLOT_CH - 1:
                                    mk[f"mm2w_{widx}"] = m
                            self.sec("tensor", tm2)

                            def ve2(v, e=e, q=q, half=half, sblock=sblock,
                                    gidx=gidx, bk=bk, tsb=tsb):
                                self.w(v, "P", mk[f"mm2n_{gidx}"])
                                self.w(v, f"ts{tsb}", self.N[f"ts{tsb}"])
                                if q == 0 and half == 0 and sblock == 0:
                                    self.w(v, "G", mk[P_ + f"b2l{e}"])
                                if q == 0:
                                    ins = v.tensor_add(yst[tsb][:, :],
                                                       self.bank(bk, 0, 512),
                                                       b2bc[:, half * 512:(half + 1) * 512])
                                else:
                                    ins = v.tensor_copy(yst[tsb][:, :],
                                                        self.bank(bk, 0, 512))
                                m = self.vi(v, ins, wait=False)
                                mk[f"yev_{gidx}"] = m
                                self._b_evac_tail = m
                                if q == 0 and half == 1 and sblock == SLOT_CH - 1:
                                    mk[P_ + f"b2done{e}"] = m
                            self.sec("vector", ve2)

                            def gy(g, e=e, q=q, half=half, sblock=sblock,
                                   gidx=gidx, tsb=tsb, ts_q0=ts_q0, ts_q1=ts_q1):
                                if _B_NOSTORE:
                                    return
                                self.w(g, "V", mk[f"yev_{gidx}"])
                                if q > 0 and half == 0 and sblock == 0:
                                    self.w(g, "ts0", ts_q0)
                                    self.w(g, "ts1", ts_q1)
                                dst = self.y_d[e * C + sblock * 128:
                                               e * C + (sblock + 1) * 128,
                                               half * 512:(half + 1) * 512]
                                ins = g.dma_start(
                                    out=dst, in_=yst[tsb][:, :],
                                    accum_op=(OP.bypass if (q == 0 or _NO_ACCUM)
                                              else OP.add))
                                self.i(ins, f"ts{tsb}", 16)
                            self.sec("gpsimd", gy)

    def phase_C(self, call, xin, xout):
        nc, mk = self.nc, self.mk
        P_ = f"C{call}_"
        BT = 2
        NG = NT // BT
        TW = BT * 128
        with ExitStack() as es:
            sb = lambda n, sh, dt=F32: es.enter_context(nc.sbuf_tensor(P_ + n, sh, dt))
            xt = [sb(f"xt{b}", [128, BT * D]) for b in range(2)]
            vt2 = [sb(f"vt{b}", [128, BT * D]) for b in range(2)]
            prt = [sb(f"prt{b}", [128, BT * E]) for b in range(2)]
            eit = [sb(f"eit{b}", [128, BT * 4], I32) for b in range(2)]
            adt = [sb(f"adt{b}", [128, BT * 4], I32) for b in range(2)]
            ygs = [[sb(f"yg{j}{b}", [128, BT * D]) for b in range(2)] for j in range(3)]
            xo = [sb(f"xo{b}", [128, BT * D]) for b in range(2)]
            ejf = sb("ejf", [128, BT * 4])
            w8 = sb("w8", [128, BT * E])
            uw = [sb(f"uw{b}", [128, BT * E]) for b in range(2)]
            pjs = sb("pjs", [128, BT * 4])
            scw = sb("scw", [128, BT * 8])
            moe = sb("moe", [128, BT * D])
            tmd = sb("tmd", [128, BT * D])
            cS = sb("cS", [8, D])
            uwT = sb("uwT", [8, 2 * 128])

            def bc(ap2d, inner):
                return ap2d.unsqueeze(2).broadcast_to((128, ap2d.shape[1], inner))

            def colk(t_, k, j):
                return t_[:, :].rearrange("p (a c) -> p a c", a=BT)[:, :, j]

            def s00(s):
                ins = s.dma_start(out=cS[:, :], in_=self.c_d[:, :])
                mk[P_ + "cl"] = self.i(ins, "L", 16)
            self.sec("sync", s00)

            for i in range(NG):
                b = i % 2
                rows = slice(i * TW, (i + 1) * TW)

                def s0(s, i=i, b=b, rows=rows):
                    if i >= 2:
                        self.w(s, "V", mk[P_ + f"cmb{i-2}"])
                    for o_, i_ in ((prt[b], self.pr_d[rows, :]),
                                   (eit[b], self.eid_d[rows, :]),
                                   (adt[b], self.adr_d[rows, :]),
                                   (xt[b], xin[rows, :]),
                                   (vt2[b], self.v_d[rows, :])):
                        ins = s.dma_start(
                            out=o_[:, :].rearrange("p (a c) -> p a c", a=BT),
                            in_=i_.rearrange("(a p) c -> p a c", p=128))
                        self.i(ins, f"ld{b}", 16)
                    mk[P_ + f"ld{i}"] = self.N[f"ld{b}"]
                self.sec("sync", s0)

                def g0(g, i=i, b=b):
                    self.w(g, f"ld{b}", mk[P_ + f"ld{i}"])
                    if i >= 2:
                        self.w(g, "V", mk[P_ + f"cmb{i-2}"])
                    for st in range(BT):
                        for j in range(3):
                            ins = g.indirect_dma_start(
                                out=ygs[j][b][:, st * D:(st + 1) * D],
                                out_offset=None,
                                in_=self.y_d[:, :],
                                in_offset=IOA(adt[b][:, st * 4 + j:st * 4 + j + 1], 0))
                            self.i(ins, f"yg{b}", 16)
                    mk[P_ + f"yg{i}"] = self.N[f"yg{b}"]
                self.sec("gpsimd", g0)

                def v0(v, i=i, b=b):
                    self.w(v, f"yg{b}", mk[P_ + f"yg{i}"])
                    if i >= 2:
                        self.w(v, f"st{b}", mk[P_ + f"st{i-2}"])
                        self.w(v, "P", mk[P_ + f"cmm{i-2}"])
                    self.vi(v, v.tensor_copy(ejf[:, :], eit[b][:, :]))
                    w83 = w8[:, :].rearrange("p (a c) -> p a c", a=BT)
                    pr3 = prt[b][:, :].rearrange("p (a c) -> p a c", a=BT)
                    for j in range(3):
                        self.vi(v, v.tensor_tensor(
                            w83, self.iota84[:, 0:BT * E]
                            .rearrange("p (a c) -> p a c", a=BT),
                            bc(colk(ejf, BT, j), E), OP.is_equal))
                        self.vi(v, v.tensor_tensor(w83, w83, pr3, OP.mult))
                        self.vi(v, v.tensor_reduce(
                            pjs[:, :].rearrange("p (a c) -> p a c", a=BT)[:, :, j:j + 1],
                            w83, AX.X, OP.add))
                    p0, p1, p2 = (colk(pjs, BT, j) for j in range(3))
                    c0, c1, c2, c3, c4, c5 = (colk(scw, BT, k) for k in range(6))
                    self.vi(v, v.tensor_tensor(c0, p0, p1, OP.min))
                    self.vi(v, v.tensor_tensor(c0, c0, p2, OP.min))
                    self.vi(v, v.tensor_tensor(c3, p2, c0, OP.is_equal))
                    self.vi(v, v.tensor_scalar(c4, c3, -1.0, 1.0, OP.mult, OP.add))
                    self.vi(v, v.tensor_tensor(c5, p1, c0, OP.is_equal))
                    self.vi(v, v.tensor_tensor(c5, c5, c4, OP.mult))
                    self.vi(v, v.tensor_tensor(c1, p0, c0, OP.is_equal))
                    self.vi(v, v.tensor_tensor(c1, c1, c4, OP.mult))
                    self.vi(v, v.tensor_scalar(c2, c5, -1.0, 1.0, OP.mult, OP.add))
                    self.vi(v, v.tensor_tensor(c1, c1, c2, OP.mult))
                    for pj, dj in ((p0, c1), (p1, c5), (p2, c3)):
                        self.vi(v, v.tensor_tensor(c2, pj, dj, OP.mult))
                        self.vi(v, v.tensor_tensor(pj, pj, c2, OP.subtract))
                    self.vi(v, v.tensor_tensor(c0, p0, p1, OP.add))
                    self.vi(v, v.tensor_tensor(c0, c0, p2, OP.add))
                    uw3 = uw[b][:, :].rearrange("p (a c) -> p a c", a=BT)
                    self.vi(v, v.tensor_tensor(uw3, bc(c0, E), bc(c0, E), OP.min),
                            wait=False)
                    for j in range(3):
                        self.vi(v, v.tensor_tensor(
                            w83, self.iota84[:, 0:BT * E]
                            .rearrange("p (a c) -> p a c", a=BT),
                            bc(colk(ejf, BT, j), E), OP.is_equal))
                        self.vi(v, v.tensor_tensor(w83, w83,
                                                   bc(colk(pjs, BT, j), E), OP.mult))
                        self.vi(v, v.tensor_tensor(uw[b][:, :], uw[b][:, :], w8[:, :],
                                                   OP.subtract))
                    y3 = lambda j: ygs[j][b][:, :].rearrange("p (a c) -> p a c", a=BT)
                    m3 = moe[:, :].rearrange("p (a c) -> p a c", a=BT)
                    t3_ = tmd[:, :].rearrange("p (a c) -> p a c", a=BT)
                    self.vi(v, v.tensor_tensor(m3, y3(0), bc(p0, D), OP.mult),
                            wait=False)
                    for j, pj in ((1, p1), (2, p2)):
                        self.vi(v, v.tensor_tensor(t3_, y3(j), bc(pj, D), OP.mult),
                                wait=False)
                        self.vi(v, v.tensor_add(moe[:, :], moe[:, :], tmd[:, :]),
                                wait=False)
                    mk[P_ + f"cmb_u{i}"] = self.vi(v, v.engine_nop())
                self.sec("vector", v0)

                def t0(t, i=i, b=b):
                    self.w(t, "V", mk[P_ + f"cmb_u{i}"])
                    if i == 0:
                        self.w(t, "L", mk[P_ + "cl"])
                    if i >= 1:
                        self.w(t, "V", mk[P_ + f"cmb{i-1}"])
                    for st in range(BT):
                        ins = t.transpose(self.bank(0, st * 128, 128)[0:E, :],
                                          uw[b][:, st * E:(st + 1) * E],
                                          self.ident[:, :])
                    mk[P_ + f"ctr{i}"] = self.i(ins, "P", 1)
                self.sec("tensor", t0)

                def v1(v, i=i):
                    self.w(v, "P", mk[P_ + f"ctr{i}"])
                    mk[P_ + f"ctc{i}"] = self.vi(
                        v, v.tensor_copy(uwT[0:E, :],
                                         self.bank(0, 0, BT * 128)[0:E, :]))
                self.sec("vector", v1)

                def t1(t, i=i):
                    self.w(t, "V", mk[P_ + f"ctc{i}"])
                    for st in range(BT):
                        for half in range(2):
                            ins = t.matmul(
                                self.bank(1 + st * 2 + half, 0, 512),
                                uwT[0:E, st * 128:(st + 1) * 128],
                                cS[0:E, half * 512:(half + 1) * 512],
                                start=True, stop=True, skip_group_check=True)
                    mk[P_ + f"cmm{i}"] = self.i(ins, "P", 1)
                self.sec("tensor", t1)

                def v2(v, i=i, b=b):
                    self.w(v, "P", mk[P_ + f"cmm{i}"])
                    for st in range(BT):
                        for half in range(2):
                            self.vi(v, v.tensor_add(
                                moe[:, st * D + half * 512:st * D + (half + 1) * 512],
                                moe[:, st * D + half * 512:st * D + (half + 1) * 512],
                                self.bank(1 + st * 2 + half, 0, 512)), wait=False)
                    self.vi(v, v.tensor_add(tmd[:, :], vt2[b][:, :], moe[:, :]),
                            wait=False)
                    self.vi(v, v.tensor_scalar(tmd[:, :], tmd[:, :], 0.5, None,
                                               OP.mult), wait=False)
                    mk[P_ + f"cmb{i}"] = self.vi(
                        v, v.tensor_add(xo[b][:, :], xt[b][:, :], tmd[:, :]),
                        wait=False)
                self.sec("vector", v2)


                def g1(g, i=i, b=b, rows=rows):
                    self.w(g, "V", mk[P_ + f"cmb{i}"])
                    ins = g.dma_start(out=xout[rows, :]
                                      .rearrange("(a p) c -> p a c", p=128),
                                      in_=xo[b][:, :]
                                      .rearrange("p (a c) -> p a c", a=BT))
                    mk[P_ + f"st{i}"] = self.i(ins, f"st{b}", 16)
                self.sec("gpsimd", g1)

    def phase_D(self):
        nc, mk = self.nc, self.mk
        P_ = "D_"
        with ExitStack() as eo:
            kv_sb = eo.enter_context(nc.sbuf_tensor("D_kv", [128, 520], F32))
            with ExitStack() as ea:
                sb = lambda n, sh, dt=F32: ea.enter_context(nc.sbuf_tensor(P_ + n, sh, dt))
                wq_sb = sb("wq", [128, DC * D])
                wk_sb = sb("wk", [128, DC * D])
                wv_sb = sb("wv", [128, DC * D])
                xt = [sb(f"xt{b}", [128, D]) for b in range(2)]
                v2 = [sb(f"v2{b}", [128, D]) for b in range(2)]
                v2T4 = sb("v2T4", [128, DC * 512])
                kf = sb("kf", [128, D])
                kp = sb("kp", [128, D])
                tmpi = sb("tmpi", [128, D], I32)
                ek = sb("ek", [128, D])
                vt_sb = sb("vts", [128, D])
                vh = sb("vh", [128, NH * 65])
                qs = [sb(f"qs{b}", [128, 512]) for b in range(2)]
                qe = [sb(f"qe{b}", [128, 512]) for b in range(2)]
                qf = [sb(f"qf{b}", [128, 512]) for b in range(2)]
                sc1 = sb("sc1", [128, 8])
                tmp = sb("tmp", [128, D])

                def s0(s):
                    for fc in range(DC):
                        for wsb, wsrc in ((wq_sb, self.wq), (wk_sb, self.wk),
                                          (wv_sb, self.wv)):
                            ins = s.dma_start(out=wsb[:, fc * D:(fc + 1) * D],
                                              in_=wsrc[fc * 128:(fc + 1) * 128, :])
                            self.i(ins, "L", 16)
                    mk["D_wl"] = self.N["L"]
                    for j in range(2):
                        ins = s.dma_start(out=xt[j][:, :],
                                          in_=self.x1_d[j * 128:(j + 1) * 128, :])
                        mk[P_ + f"xl{j}"] = self.i(ins, f"xl{j}", 16)
                self.sec("sync", s0)

                zf = sb("zf", [128, 512])

                def gv(g):
                    ins = g.memset(vh[:, :], 1.0)
                    self.i(ins, "Gc", 1)
                    ins = g.memset(zf[:, :], 0.0)
                    mk[P_ + "vh1"] = self.i(ins, "Gc", 1)
                self.sec("gpsimd", gv)

                def tz(t):
                    self.w(t, "Gc", mk[P_ + "vh1"])
                    t.matmul(self.bank(6, 0, 512), self.ident[:, :], zf[:, :],
                             start=True, stop=True, skip_group_check=True)
                    ins = t.matmul(self.bank(7, 0, 512), self.ident[:, :], zf[:, :],
                                   start=True, stop=True, skip_group_check=True)
                    mk[P_ + "kvz"] = self.i(ins, "P", 1)
                self.sec("tensor", tz)

                for i in range(NT):
                    b = i % 2

                    def v1(v, i=i, b=b):
                        self.w(v, f"xl{b}", mk[P_ + f"xl{i}"])
                        self.vi(v, v.tensor_reduce(sc1[:, 0:1], xt[b][:, :], AX.X, OP.add))
                        self.vi(v, v.tensor_scalar(sc1[:, 0:1], sc1[:, 0:1], 1.0 / D, None, OP.mult))
                        self.vi(v, v.tensor_scalar(v2[b][:, :], xt[b][:, :], sc1[:, 0:1], None,
                                                   OP.subtract), wait=False)
                        self.vi(v, v.tensor_tensor(tmp[:, :], v2[b][:, :], v2[b][:, :], OP.mult),
                                wait=False)
                        mk[P_ + f"ln{i}"] = self.vi(v, v.tensor_reduce(sc1[:, 1:2], tmp[:, :],
                                                                       AX.X, OP.add), wait=False)
                    self.sec("vector", v1)

                    def a1(s, i=i):
                        self.w(s, "V", mk[P_ + f"ln{i}"])
                        ins = s.activation(sc1[:, 2:3], sc1[:, 1:2], AF.Sqrt,
                                           bias=self.cst[:, 0:1], scale=1.0 / D)
                        mk[P_ + f"rs{i}"] = self.i(ins, "A", 1)
                    self.sec("scalar", a1)

                    def v2f(v, i=i, b=b):
                        self.w(v, "A", mk[P_ + f"rs{i}"])
                        self.vi(v, v.reciprocal(sc1[:, 2:3], sc1[:, 2:3]))
                        mk[P_ + f"v2{i}"] = self.vi(v, v.tensor_scalar(v2[b][:, :], v2[b][:, :],
                                                                       sc1[:, 2:3], None, OP.mult),
                                                    wait=False)
                    self.sec("vector", v2f)

                    def t1(t, i=i, b=b):
                        self.w(t, "V", mk[P_ + f"v2{i}"])
                        if i >= 1:
                            self.w(t, "V", mk[P_ + f"kf{i-1}"])
                        for fc in range(DC):
                            ins = t.transpose(self.bank(fc % 2, (fc // 2) * 128, 128),
                                              v2[b][:, fc * 128:(fc + 1) * 128],
                                              self.ident[:, :])
                        mk[P_ + f"vtr{i}"] = self.i(ins, "P", 1)
                    self.sec("tensor", t1)

                    def v3(v, i=i):
                        self.w(v, "P", mk[P_ + f"vtr{i}"])
                        if i >= 4:
                            self.w(v, "P", mk[P_ + f"qmm{(i//4 - 1)*8 + 7}"])
                        g = i % 4
                        for fc in range(DC):
                            m = self.vi(v, v.tensor_copy(
                                v2T4[:, fc * 512 + g * 128:fc * 512 + (g + 1) * 128],
                                self.bank(fc % 2, (fc // 2) * 128, 128)), wait=False)
                        mk[P_ + f"vTc{i}"] = m
                    self.sec("vector", v3)

                    def t2(t, i=i):
                        self.w(t, "V", mk[P_ + f"vTc{i}"])
                        if i == 0:
                            self.w(t, "L", mk["D_wl"])
                        if i >= 1:
                            self.w(t, "A", mk[P_ + f"vev{i-1}"])
                        g = i % 4
                        for half in range(2):
                            for fc in range(DC):
                                ins = t.matmul(
                                    self.bank(half, 0, 512),
                                    v2T4[:, fc * 512 + g * 128:fc * 512 + (g + 1) * 128],
                                    wk_sb[:, fc * D + half * 512:fc * D + (half + 1) * 512],
                                    start=(fc == 0), stop=(fc == DC - 1))
                        mk[P_ + f"kmm{i}"] = self.i(ins, "P", 1)
                        for half in range(2):
                            for fc in range(DC):
                                ins = t.matmul(
                                    self.bank(2 + half, 0, 512),
                                    v2T4[:, fc * 512 + g * 128:fc * 512 + (g + 1) * 128],
                                    wv_sb[:, fc * D + half * 512:fc * D + (half + 1) * 512],
                                    start=(fc == 0), stop=(fc == DC - 1))
                        mk[P_ + f"vmm{i}"] = self.i(ins, "P", 1)
                    self.sec("tensor", t2)

                    def a2(s, i=i):
                        self.w(s, "P", mk[P_ + f"kmm{i}"])
                        if i >= 1:
                            self.w(s, "V", mk[P_ + f"vh{i-1}"])
                        ins = s.activation(ek[:, :], self.psA[:, 0:1024], AF.Exp)
                        mk[P_ + f"ekx{i}"] = self.i(ins, "A", 1)
                        self.w(s, "P", mk[P_ + f"vmm{i}"])
                        ins = s.activation(vt_sb[:, :], self.psA[:, 1024:2048], AF.Identity)
                        mk[P_ + f"vev{i}"] = self.i(ins, "A", 1)
                    self.sec("scalar", a2)

                    def v4(v, i=i):
                        self.w(v, "A", mk[P_ + f"ekx{i}"])
                        if i == 0:
                            self.w(v, "Gc", mk[P_ + "vh1"])
                        self.vi(v, v.tensor_scalar(tmpi[:, :], self.psA[:, 0:1024], 0.0,
                                                   None, OP.is_gt), wait=False)
                        self.vi(v, v.tensor_scalar(kp[:, :], self.psA[:, 0:1024], 1.0,
                                                   None, OP.add), wait=False)
                        mk[P_ + f"kf{i}"] = self.vi(v, v.select(kf[:, :], tmpi[:, :],
                                                                kp[:, :], ek[:, :]), wait=False)
                        self.w(v, "A", mk[P_ + f"vev{i}"])
                        for h in range(NH):
                            m = self.vi(v, v.tensor_copy(vh[:, h * 65:h * 65 + 64],
                                                         vt_sb[:, h * 64:(h + 1) * 64]),
                                        wait=False)
                        mk[P_ + f"vh{i}"] = m
                    self.sec("vector", v4)

                    def t3(t, i=i):
                        self.w(t, "V", mk[P_ + f"kf{i}"])
                        self.w(t, "V", mk[P_ + f"vh{i}"])
                        for h in range(NH):
                            kb = 6 + h // 8
                            rem = h % 8
                            ph, cb = rem % 2, rem // 2
                            ps = self.bank(kb, cb * 65, 65)[ph * 64:ph * 64 + 64, :]
                            ins = t.matmul(ps, kf[:, h * 64:(h + 1) * 64],
                                           vh[:, h * 65:(h + 1) * 65],
                                           start=False, stop=(i == NT - 1),
                                           skip_group_check=True)
                        mk[P_ + f"kv{i}"] = self.i(ins, "P", 1)
                    self.sec("tensor", t3)

                    if i % 4 == 3:
                        gq = i // 4
                        for qc in range(DC):
                            kq = 4 + qc % 2
                            bq = qc % 2
                            idx = gq * 8 + qc

                            def tq(t, gq=gq, qc=qc, kq=kq, idx=idx):
                                if qc < 2:
                                    self.w(t, "V", mk[P_ + f"vTc{gq*4+3}"])
                                if idx >= 2:
                                    self.w(t, "A", mk[P_ + f"qcp{idx-2}"])
                                for fc in range(DC):
                                    ins = t.matmul(
                                        self.bank(kq, 0, 512),
                                        wq_sb[:, fc * D + qc * 128:fc * D + (qc + 1) * 128],
                                        v2T4[:, fc * 512:(fc + 1) * 512],
                                        start=(fc == 0), stop=(fc == DC - 1))
                                mk[P_ + f"qmm{idx}"] = self.i(ins, "P", 1)
                            self.sec("tensor", tq)

                            def aq(s, qc=qc, kq=kq, bq=bq, idx=idx):
                                self.w(s, "P", mk[P_ + f"qmm{idx}"])
                                if idx >= 2:
                                    self.w(s, "V", mk[P_ + f"qfv{idx-2}"])
                                self.ai(s, s.activation(qe[bq][:, :], self.bank(kq, 0, 512),
                                                        AF.Exp))
                                mk[P_ + f"qcp{idx}"] = self.ai(
                                    s, s.activation(qs[bq][:, :], self.bank(kq, 0, 512),
                                                    AF.Identity))
                            self.sec("scalar", aq)

                            def vq(v, bq=bq, idx=idx):
                                self.w(v, "A", mk[P_ + f"qcp{idx}"])
                                if idx >= 2:
                                    self.w(v, f"st{bq}", mk[P_ + f"qst{idx-2}"])
                                self.vi(v, v.tensor_scalar(tmpi[:, 0:512], qs[bq][:, :], 0.0,
                                                           None, OP.is_gt), wait=False)
                                self.vi(v, v.tensor_scalar(qs[bq][:, :], qs[bq][:, :], 1.0,
                                                           None, OP.add), wait=False)
                                mk[P_ + f"qfv{idx}"] = self.vi(
                                    v, v.select(qf[bq][:, :], tmpi[:, 0:512],
                                                qs[bq][:, :], qe[bq][:, :]), wait=False)
                            self.sec("vector", vq)

                            def gq_(g, gq=gq, qc=qc, bq=bq, idx=idx):
                                self.w(g, "V", mk[P_ + f"qfv{idx}"])
                                for tt4 in range(4):
                                    ins = g.dma_start(
                                        out=self.qf_d[gq * 4 + tt4, :, qc, :],
                                        in_=qf[bq][:, tt4 * 128:(tt4 + 1) * 128])
                                    self.i(ins, f"st{bq}", 16)
                                mk[P_ + f"qst{idx}"] = self.N[f"st{bq}"]
                            self.sec("gpsimd", gq_)

                    if i + 2 < NT:
                        def s1(s, i=i, b=b):
                            self.w(s, "V", mk[P_ + f"v2{i}"])
                            ins = s.dma_start(out=xt[b][:, :],
                                              in_=self.x1_d[(i + 2) * 128:(i + 3) * 128, :])
                            mk[P_ + f"xl{i+2}"] = self.i(ins, f"xl{b}", 16)
                        self.sec("sync", s1)

                def vkv(v):
                    self.w(v, "P", mk[P_ + f"kv{NT-1}"])
                    self.vi(v, v.tensor_copy(kv_sb[:, 0:260], self.bank(6, 0, 260)), wait=False)
                    mk[P_ + "kvc"] = self.vi(v, v.tensor_copy(kv_sb[:, 260:520],
                                                              self.bank(7, 0, 260)))
                self.sec("vector", vkv)

                self.barrier()

            with ExitStack() as eb:
                sb = lambda n, sh, dt=F32: eb.enter_context(nc.sbuf_tensor("Db" + n, sh, dt))
                wo_sb = sb("wo", [128, DC * D])
                qfs = [sb(f"qfs{b}", [128, DC * 128]) for b in range(2)]
                x1t = [sb(f"x1t{b}", [128, D]) for b in range(2)]
                cat = sb("cat", [128, D])
                catT = sb("catT", [128, D])
                xa = [sb(f"xa{b}", [128, D]) for b in range(2)]
                rd = sb("rd", [128, NH])

                def s0(s):
                    for fc in range(DC):
                        ins = s.dma_start(out=wo_sb[:, fc * D:(fc + 1) * D],
                                          in_=self.wo[fc * 128:(fc + 1) * 128, :])
                        self.i(ins, "L", 16)
                    mk["Db_wl"] = self.N["L"]
                    for j in range(2):
                        ins = s.dma_start(
                            out=qfs[j][:, :].rearrange("p (a c) -> p a c", a=DC),
                            in_=self.qf_d[j, :, :, :])
                        self.i(ins, f"qf{j}", 16)
                        mk[P_ + f"ql{j}"] = self.N[f"qf{j}"]
                        ins = s.dma_start(out=x1t[j][:, :],
                                          in_=self.x1_d[j * 128:(j + 1) * 128, :])
                        mk[P_ + f"x1l{j}"] = self.i(ins, f"ld{j}", 16)
                self.sec("sync", s0)

                for i in range(NT):
                    b = i % 2

                    def t1(t, i=i, b=b):
                        self.w(t, f"qf{b}", mk[P_ + f"ql{i}"])
                        if i == 0:
                            self.w(t, "V", mk[P_ + "kvc"])
                            self.w(t, "L", mk["Db_wl"])
                        if i >= 1:
                            self.w(t, "V", mk[P_ + f"cm{i-1}"])
                        for hj in range(NH):
                            h = (hj % 4) * 4 + hj // 4
                            lhsT = qfs[b][:, (h // 2) * 128:(h // 2 + 1) * 128]
                            lhsT = lhsT[(h % 2) * 64:(h % 2) * 64 + 64, :]
                            rhs = kv_sb[(h % 2) * 64:(h % 2) * 64 + 64,
                                        (h // 8) * 260 + ((h // 2) % 4) * 65:
                                        (h // 8) * 260 + ((h // 2) % 4) * 65 + 65]
                            ins = t.matmul(self.bank(h // 4, (h % 4) * 65, 65),
                                           lhsT, rhs,
                                           start=(hj < 4), stop=True,
                                           skip_group_check=True)
                        mk[P_ + f"hm{i}"] = self.i(ins, "P", 1)
                    self.sec("tensor", t1)

                    def a1(s, i=i):
                        self.w(s, "P", mk[P_ + f"hm{i}"])
                        if i >= 1:
                            self.w(s, "V", mk[P_ + f"cm{i-1}"])
                        for h in range(NH):
                            ins = s.activation(rd[:, h:h + 1],
                                               self.bank(h // 4, (h % 4) * 65 + 64, 1),
                                               AF.Identity, bias=self.cst[:, 1:2])
                            self.i(ins, "A", 1)
                        mk[P_ + f"rc{i}"] = self.N["A"]
                    self.sec("scalar", a1)

                    def v1(v, i=i):
                        self.w(v, "A", mk[P_ + f"rc{i}"])
                        self.vi(v, v.reciprocal(rd[:, 0:NH], rd[:, 0:NH]))
                        for h in range(NH):
                            m = self.vi(v, v.tensor_scalar(
                                cat[:, h * 64:(h + 1) * 64],
                                self.bank(h // 4, (h % 4) * 65, 64),
                                rd[:, h:h + 1], None, OP.mult), wait=False)
                        mk[P_ + f"cm{i}"] = m
                    self.sec("vector", v1)

                    def t2(t, i=i):
                        self.w(t, "V", mk[P_ + f"cm{i}"])
                        if i >= 1:
                            self.w(t, "V", mk[P_ + f"ctc{i-1}"])
                        for fc in range(DC):
                            ins = t.transpose(self.bank(4 + fc % 2, (fc // 2) * 128, 128),
                                              cat[:, fc * 128:(fc + 1) * 128],
                                              self.ident[:, :])
                        mk[P_ + f"ctr{i}"] = self.i(ins, "P", 1)
                    self.sec("tensor", t2)

                    def v2(v, i=i):
                        self.w(v, "P", mk[P_ + f"ctr{i}"])
                        for fc in range(DC):
                            m = self.vi(v, v.tensor_copy(catT[:, fc * 128:(fc + 1) * 128],
                                                         self.bank(4 + fc % 2,
                                                                   (fc // 2) * 128, 128)),
                                        wait=False)
                        mk[P_ + f"ctc{i}"] = m
                    self.sec("vector", v2)

                    def t3(t, i=i):
                        self.w(t, "V", mk[P_ + f"ctc{i}"])
                        if i >= 1:
                            self.w(t, "V", mk[P_ + f"x2v{i-1}"])
                        for half in range(2):
                            for fc in range(DC):
                                ins = t.matmul(
                                    self.bank(6 + half, 0, 512),
                                    catT[:, fc * 128:(fc + 1) * 128],
                                    wo_sb[:, fc * D + half * 512:fc * D + (half + 1) * 512],
                                    start=(fc == 0), stop=(fc == DC - 1))
                        mk[P_ + f"omm{i}"] = self.i(ins, "P", 1)
                    self.sec("tensor", t3)

                    def v3(v, i=i, b=b):
                        self.w(v, "P", mk[P_ + f"omm{i}"])
                        self.w(v, f"ld{b}", mk[P_ + f"x1l{i}"])
                        if i >= 2:
                            self.w(v, f"st{b}", mk[P_ + f"x2st{i-2}"])
                        mk[P_ + f"x2v{i}"] = self.vi(v, v.tensor_add(xa[b][:, :], x1t[b][:, :],
                                                                     self.psB[:, 1024:2048]),
                                                     wait=False)
                    self.sec("vector", v3)

                    def g1(g, i=i, b=b):
                        self.w(g, "V", mk[P_ + f"x2v{i}"])
                        ins = g.dma_start(out=self.x2_d[i * 128:(i + 1) * 128, :],
                                          in_=xa[b][:, :])
                        mk[P_ + f"x2st{i}"] = self.i(ins, f"st{b}", 16)
                    self.sec("gpsimd", g1)

                    if i + 2 < NT:
                        def s1(s, i=i, b=b):
                            self.w(s, "P", mk[P_ + f"hm{i}"])
                            ins = s.dma_start(
                                out=qfs[b][:, :].rearrange("p (a c) -> p a c", a=DC),
                                in_=self.qf_d[i + 2, :, :, :])
                            self.i(ins, f"qf{b}", 16)
                            mk[P_ + f"ql{i+2}"] = self.N[f"qf{b}"]
                            self.w(s, "V", mk[P_ + f"x2v{i}"])
                            ins = s.dma_start(out=x1t[b][:, :],
                                              in_=self.x1_d[(i + 2) * 128:(i + 3) * 128, :])
                            mk[P_ + f"x1l{i+2}"] = self.i(ins, f"ld{b}", 16)
                        self.sec("sync", s1)

    def build(self):
        nc = self.nc
        with ExitStack() as es:
            for s in _SEMS:
                self.S[s] = es.enter_context(nc.semaphore(s))
                self.N[s] = 0
            sb = lambda n, sh, dt=F32: es.enter_context(nc.sbuf_tensor(n, sh, dt))
            self.ident = sb("ident", [128, 128])
            self.iota8 = sb("iota8", [128, E])
            self.eights = sb("eights", [128, E])
            self.zeros_i = sb("zeros_i", [128, 128], I32)
            self.gamma_bc = sb("gamma_bc", [128, D])
            self.beta_bc = sb("beta_bc", [128, D])
            self.br_bc = sb("br_bc", [128, E])
            self.b1_rows = sb("b1_rows", [128, 256])
            self.b2_rows = sb("b2_rows", [64, 128])
            self.b1_sb = sb("b1_sb", [128, 256])
            self.b2_sb = sb("b2_sb", [128, 64])
            self.rb1_sb = sb("rb1_sb", [128, 256])
            self.wr_sb = sb("wr_sb", [128, DC * E])
            self.ctot_bc = sb("ctot_bc", [128, D])
            self.cst = sb("cst", [128, 4])
            self.UT = sb("UT", [128, 128])
            self.ones1 = sb("ones1", [1, 128])
            self.onesc = sb("onesc", [128, 1])
            self.iota84 = sb("iota84", [128, 32])
            self.eights4 = sb("eights4", [128, 32])
            self.psA = es.enter_context(nc.psum_tensor("psA", [128, 2048], F32))
            self.psB = es.enter_context(nc.psum_tensor("psB", [128, 2048], F32))

            self.block = es.enter_context(nc.Block())

            steps = [
                ("pro", lambda: self.prologue()),
                ("A", lambda: self.phase_A(0, self.x_in, with_b0=True)),
                ("B", lambda: self.phase_B(0)),
                ("C", lambda: self.phase_C(0, self.x_in, self.x1_d)),
                ("D", lambda: self.phase_D()),
                ("A2", lambda: self.phase_A(1, self.x2_d)),
                ("B2", lambda: self.phase_B(1)),
                ("C2", lambda: self.phase_C(1, self.x2_d, self.out)),
            ]
            for name, fn in steps:
                fn()
                self.barrier()
                if self.upto == name:
                    break
        return nc


_CACHE = {}


def build_program(debug=False, upto="all"):
    key = (bool(debug), upto)
    if key not in _CACHE:
        _CACHE[key] = MK(debug=debug, upto=upto).build()
    return _CACHE[key]


def kernel(L0_attn_x, L0_attn_Wq, L0_attn_Wk, L0_attn_Wv, L0_attn_Wo,
           L0_ffn_gamma, L0_ffn_beta, L0_ffn_Wr, L0_ffn_br,
           L0_ffn_W1, L0_ffn_b1, L0_ffn_W2, L0_ffn_b2):
    nc = build_program(debug=False)
    x = np.ascontiguousarray(np.asarray(L0_attn_x, dtype=np.float32))
    com = {
        "wq": np.ascontiguousarray(L0_attn_Wq, dtype=np.float32),
        "wk": np.ascontiguousarray(L0_attn_Wk, dtype=np.float32),
        "wv": np.ascontiguousarray(L0_attn_Wv, dtype=np.float32),
        "wo": np.ascontiguousarray(L0_attn_Wo, dtype=np.float32),
        "wr": np.ascontiguousarray(L0_ffn_Wr, dtype=np.float32),
        "gamma": np.ascontiguousarray(L0_ffn_gamma, dtype=np.float32),
        "beta": np.ascontiguousarray(L0_ffn_beta, dtype=np.float32),
        "brr": np.ascontiguousarray(L0_ffn_br, dtype=np.float32),
        "w1": np.ascontiguousarray(L0_ffn_W1, dtype=np.float32),
        "b1": np.ascontiguousarray(L0_ffn_b1, dtype=np.float32),
        "w2": np.ascontiguousarray(L0_ffn_W2, dtype=np.float32),
        "b2": np.ascontiguousarray(L0_ffn_b2, dtype=np.float32),
    }
    in_maps = [dict(com, x=x[b]) for b in range(8)]
    res = run_bass_kernel_spmd(nc, in_maps, core_ids=list(range(8)))
    return np.stack([np.asarray(res.results[b]["out"]) for b in range(8)], axis=0)


# version 16
# speedup vs baseline: 1.0231x; 1.0231x over previous
import os
import numpy as np
from contextlib import ExitStack
_NO_ACCUM = bool(os.environ.get('MK_NO_ACCUM'))
_B_NE = int(os.environ.get('MK_B_NE', '8'))
_B_NQ = int(os.environ.get('MK_B_NQ', '4'))
_B_GONLY = bool(os.environ.get('MK_B_GONLY'))
_B_MM1ONLY = bool(os.environ.get('MK_B_MM1ONLY'))
_B_NOTR = bool(os.environ.get('MK_B_NOTR'))
_B_NOSTORE = bool(os.environ.get('MK_B_NOSTORE'))

import concourse.bass as bass
import concourse.mybir as mybir
from concourse.bass_utils import run_bass_kernel_spmd

F32 = mybir.dt.float32
I32 = mybir.dt.int32
AX = mybir.AxisListType
OP = mybir.AluOpType
AF = mybir.ActivationFunctionType
IOA = bass.IndirectOffsetOnAxis

T = 4096
D = 1024
HID = 4096
E = 8
C = 1792
NGW = C // 4
SLOT_CH = C // 128
NT = T // 128
DC = D // 128
NH, DH = 16, 64
NSLOT = E * C
LN_EPS = 1e-5
ATT_EPS = 1e-6
BIG = 1e9

_SEMS = (["L", "G", "P", "V", "A", "Gc"]
         + [f"{p}{b}" for b in (0, 1) for p in
            ("xl", "w1", "w2", "ix", "xg", "as", "sc", "ld", "yg", "st", "ts", "qf")])


class MK:
    def __init__(self, debug=False, upto="all"):
        self.debug = debug
        self.upto = upto
        self.nc = nc = bass.Bass("TRN2", target_bir_lowering=False,
                                 detect_race_conditions=False)
        dbg = "ExternalOutput" if debug else "Internal"

        def dt_(name, shape, dtype=F32, kind="Internal"):
            return nc.dram_tensor(name, shape, dtype, kind=kind).ap()

        self.x_in = dt_("x", [T, D], kind="ExternalInput")
        self.wq = dt_("wq", [D, D], kind="ExternalInput")
        self.wk = dt_("wk", [D, D], kind="ExternalInput")
        self.wv = dt_("wv", [D, D], kind="ExternalInput")
        self.wo = dt_("wo", [D, D], kind="ExternalInput")
        self.wr = dt_("wr", [D, E], kind="ExternalInput")
        self.gamma = dt_("gamma", [D], kind="ExternalInput")
        self.beta = dt_("beta", [D], kind="ExternalInput")
        self.br = dt_("brr", [E], kind="ExternalInput")
        self.w1 = dt_("w1", [E, D, HID], kind="ExternalInput")
        self.b1 = dt_("b1", [E, HID], kind="ExternalInput")
        self.w2 = dt_("w2", [E, HID, D], kind="ExternalInput")
        self.b2 = dt_("b2", [E, D], kind="ExternalInput")
        self.out = dt_("out", [T, D], kind="ExternalOutput")

        self.v_d = dt_("v_d", [T, D], kind=dbg)
        self.xn_d = dt_("xn_d", [T, D], kind=dbg)
        self.pr_d = dt_("pr_d", [T, E], kind=dbg)
        self.eid_d = dt_("eid_d", [T, 4], I32, kind=dbg)
        self.adr_d = dt_("adr_d", [T, 4], I32, kind=dbg)
        self.sidx_d = dt_("sidx_d", [NSLOT, 1], I32, kind=dbg)
        self.y_d = dt_("y_d", [NSLOT, D], kind=dbg)
        self.c_d = dt_("c_d", [E, D], kind=dbg)
        self.ct_d = dt_("ct_d", [1, D], kind=dbg)
        self.x1_d = dt_("x1_d", [T, D], kind=dbg)
        self.x2_d = dt_("x2_d", [T, D], kind=dbg)
        self.qf_d = dt_("qf_d", [NT, 128, DC, 128], kind=dbg)

        self.S = {}
        self.N = {}
        self.mk = {}
        self._b_evac_tail = 0

    def w(self, eng, s, v):
        if v > 0:
            eng.wait_ge(self.S[s], v)

    def i(self, instr, s, n):
        self.N[s] += n
        instr.then_inc(self.S[s], n)
        return self.N[s]

    def sec(self, engname, fn):
        getattr(self.block, engname)(fn)

    def vi(self, v, ins, wait=True):
        m = self.i(ins, "V", 1)
        if wait:
            v.wait_ge(self.S["V"], m)
        return m

    def ai(self, s, ins, wait=False):
        m = self.i(ins, "A", 1)
        if wait:
            s.wait_ge(self.S["A"], m)
        return m

    def barrier(self):
        counts = {s: v for s, v in self.N.items() if v > 0}
        for eng in ("sync", "tensor", "vector", "scalar", "gpsimd"):
            def f(e, counts=counts):
                for s, v in counts.items():
                    e.wait_ge(self.S[s], v)
            self.sec(eng, f)

    def bank(self, k, lo=0, n=512):
        ps = self.psA if k < 4 else self.psB
        base = (k % 4) * 512
        return ps[:, base + lo: base + lo + n]

    def prologue(self):
        mk = self.mk

        def g(g):
            ins = g.memset(self.ident[:, :], 0.0)
            m = self.i(ins, "Gc", 1)
            g.wait_ge(self.S["Gc"], m)
            ins = g.affine_select(out=self.ident[:, :], in_=self.ident[:, :],
                                  compare_op=OP.not_equal, fill=1.0, base=0,
                                  pattern=[[-1, 128]], channel_multiplier=1)
            self.i(ins, "Gc", 1)
            ins = g.iota(self.iota8[:, :], pattern=[[1, E]], base=0,
                         channel_multiplier=0, allow_small_or_imprecise_dtypes=True)
            self.i(ins, "Gc", 1)
            ins = g.memset(self.eights[:, :], float(E))
            self.i(ins, "Gc", 1)
            ins = g.memset(self.zeros_i[:, :], 0)
            self.i(ins, "Gc", 1)
            ins = g.memset(self.cst[:, 0:1], LN_EPS)
            self.i(ins, "Gc", 1)
            ins = g.memset(self.cst[:, 1:2], ATT_EPS)
            self.i(ins, "Gc", 1)
            ins = g.memset(self.UT[:, :], 0.0)
            m = self.i(ins, "Gc", 1)
            g.wait_ge(self.S["Gc"], m)
            ins = g.affine_select(out=self.UT[:, :], in_=self.UT[:, :],
                                  compare_op=OP.is_ge, fill=1.0, base=0,
                                  pattern=[[-1, 128]], channel_multiplier=1)
            self.i(ins, "Gc", 1)
            ins = g.memset(self.ones1[:, :], 1.0)
            self.i(ins, "Gc", 1)
            ins = g.memset(self.onesc[:, :], 1.0)
            self.i(ins, "Gc", 1)
            ins = g.iota(self.iota84[:, :], pattern=[[0, 4], [1, E]], base=0,
                         channel_multiplier=0, allow_small_or_imprecise_dtypes=True)
            self.i(ins, "Gc", 1)
            ins = g.memset(self.eights4[:, :], float(E))
            self.i(ins, "Gc", 1)
            i1 = g.dma_start(out=self.gamma_bc[:, :],
                             in_=self.gamma.unsqueeze(0).broadcast_to((128, D)))
            self.i(i1, "G", 16)
            i2 = g.dma_start(out=self.beta_bc[:, :],
                             in_=self.beta.unsqueeze(0).broadcast_to((128, D)))
            self.i(i2, "G", 16)
            i3 = g.dma_start(out=self.br_bc[:, :],
                             in_=self.br.unsqueeze(0).broadcast_to((128, E)))
            self.i(i3, "G", 16)
            b1r = self.b1.rearrange("e (r p) -> (e r) p", p=128)
            i4 = g.dma_start(out=self.b1_rows[:, 0:128], in_=b1r[0:128, :])
            self.i(i4, "G", 16)
            i5 = g.dma_start(out=self.b1_rows[:, 128:256], in_=b1r[128:256, :])
            self.i(i5, "G", 16)
            i6 = g.dma_start(out=self.b2_rows[0:64, :],
                             in_=self.b2.rearrange("e (r p) -> (e r) p", p=128))
            self.i(i6, "G", 16)
            i7 = g.dma_start(out=self.wr_sb[:, :].rearrange("p (c e) -> p c e", c=DC),
                             in_=self.wr.rearrange("(c p) e -> p c e", p=128))
            mk["pro_g"] = self.i(i7, "G", 16)
        self.sec("gpsimd", g)

        def t(t):
            self.w(t, "G", mk["pro_g"])
            t.transpose(self.psA[:, 0:128], self.b1_rows[:, 0:128], self.ident[:, :])
            t.transpose(self.psA[:, 512:640], self.b1_rows[:, 128:256], self.ident[:, :])
            ins = t.transpose(self.psA[:, 1024:1088], self.b2_rows[0:64, :],
                              self.ident[0:64, 0:64])
            mk["pro_t"] = self.i(ins, "P", 1)
        self.sec("tensor", t)

        def v(v):
            self.w(v, "P", mk["pro_t"])
            v.tensor_copy(self.b1_sb[:, 0:128], self.psA[:, 0:128])
            v.tensor_copy(self.b1_sb[:, 128:256], self.psA[:, 512:640])
            ins = v.tensor_copy(self.b2_sb[:, 0:64], self.psA[:, 1024:1088])
            mk["pro_v"] = self.i(ins, "V", 1)
        self.sec("vector", v)

        def s(s):
            self.w(s, "V", mk["pro_v"])
            ins = s.activation(self.rb1_sb[:, :], self.b1_sb[:, :], AF.Relu)
            mk["pro_a"] = self.i(ins, "A", 1)
        self.sec("scalar", s)

    def phase_A(self, call, xin, with_b0=False):
        nc, mk = self.nc, self.mk
        P_ = f"A{call}_"
        BT = 4
        NG = NT // BT
        W = BT * E
        with ExitStack() as es:
            sb = lambda n, sh, dt=F32: es.enter_context(nc.sbuf_tensor(P_ + n, sh, dt))
            if with_b0:
                w2qh = [sb(f"b0w{b}", [128, 4 * D]) for b in range(2)]
                csb = sb("b0c", [128, E])

            def b0_mms(e):
                for q in range(4):
                    for hf in range(2):
                        li = e * 8 + q * 2 + hf

                        def ld(g, e=e, q=q, hf=hf, li=li):
                            if li >= 2:
                                self.w(g, "P", mk[f"b0c_{li-2}"])
                            ins = g.dma_start(
                                out=w2qh[li % 2][:, :]
                                    .rearrange("p (h c) -> p h c", h=4),
                                in_=self.w2[e, q * 1024 + hf * 512:
                                            q * 1024 + (hf + 1) * 512, :]
                                    .rearrange("(h p) c -> p h c", p=128))
                            mk[f"b0l_{li}"] = self.i(ins, f"w2{li % 2}", 16)
                        self.sec("sync", ld)

                        def mm(t, e=e, q=q, hf=hf, li=li):
                            if li == e * 8:
                                self.w(t, "A", mk["pro_a"] if e == 0
                                       else mk[f"b0ev_{e-1}"])
                            self.w(t, f"w2{li % 2}", mk[f"b0l_{li}"])
                            for oc in range(DC):
                                for h4 in range(4):
                                    hq = q * 8 + hf * 4 + h4
                                    ins = t.matmul(
                                        self.bank(2, oc, 1),
                                        w2qh[li % 2][:, h4 * D + oc * 128:
                                                     h4 * D + (oc + 1) * 128],
                                        self.rb1_sb[:, e * 32 + hq:
                                                    e * 32 + hq + 1],
                                        start=(q == 0 and hf == 0 and oc == 0
                                               and h4 == 0),
                                        stop=(q == 3 and hf == 1 and oc == DC - 1
                                              and h4 == 3),
                                        skip_group_check=True)
                            mk[f"b0c_{li}"] = self.i(ins, "P", 1)
                        self.sec("tensor", mm)
                mk[f"b0m_{e}"] = self.N["P"]

            def b0_evac(s, e):
                self.w(s, "P", mk[f"b0m_{e}"])
                if e >= 1:
                    self.w(s, "G", mk[f"b0st_{e-1}"])
                for oc in range(DC):
                    ins = s.activation(csb[:, oc:oc + 1], self.bank(2, oc, 1),
                                       AF.Identity,
                                       bias=self.b2_sb[:, e * 8 + oc:e * 8 + oc + 1])
                    self.i(ins, "A", 1)
                mk[f"b0ev_{e}"] = self.N["A"]

            def b0_store(g, e):
                self.w(g, "A", mk[f"b0ev_{e}"])
                for oc in range(DC):
                    ins = g.dma_start(out=self.c_d[e:e + 1, oc * 128:(oc + 1) * 128],
                                      in_=csb[:, oc:oc + 1])
                    self.i(ins, "G", 16)
                mk[f"b0st_{e}"] = self.N["G"]
            xt = [sb(f"xt{b}", [128, BT * D]) for b in range(2)]
            vt = [sb(f"vt{b}", [128, BT * D]) for b in range(2)]
            xnt = [sb(f"xnt{b}", [128, BT * D]) for b in range(2)]
            xnT = [sb(f"xnT{b}", [128, BT * D]) for b in range(2)]
            tmp = sb("tmp", [128, BT * D])
            sc4 = sb("sc4", [128, 16])
            lg = sb("lg", [128, W])
            prb = [sb(f"prb{b}", [128, W]) for b in range(2)]
            pw = sb("pw", [128, W])
            w8 = sb("w8", [128, W])
            w8i = sb("w8i", [128, W], I32)
            cnd = sb("cnd", [128, W])
            Yt = sb("Yt", [128, W])
            ejs = [sb(f"ejs{b}", [128, BT * 4]) for b in range(2)]
            eidt = [sb(f"eidt{b}", [128, BT * 4], I32) for b in range(2)]
            Pt = sb("Pt", [128, W])
            adrf = [sb(f"adrf{b}", [128, BT * 4]) for b in range(2)]
            adri = [sb(f"adri{b}", [128, BT * 4], I32) for b in range(2)]
            tok = [sb(f"tok{b}", [128, BT], I32) for b in range(2)]
            csrow = sb("csrow", [1, W])
            corr = sb("corr", [1, W])
            base = sb("base", [1, E])

            def bc(ap2d, inner):
                return ap2d.unsqueeze(2).broadcast_to((128, ap2d.shape[1], inner))

            def col4(t_, j):
                return t_[:, :].rearrange("p (a c) -> p a c", a=BT)[:, :, j]

            def col4k(t_, j):
                return t_[:, :].rearrange("p (a c) -> p a c", a=BT)[:, :, j:j + 1]

            def g0(g):
                ins = g.dma_start(
                    out=self.sidx_d.rearrange("(p a) one -> p (a one)", p=128),
                    in_=self.zeros_i[:, 0:NSLOT // 128])
                mk[P_ + "sz"] = self.i(ins, "G", 16)
            self.sec("gpsimd", g0)

            def v00(v):
                ins = v.memset(base[:, :], 0.0)
                mk[P_ + "b0"] = self.vi(v, ins)
            self.sec("vector", v00)

            def s0(s):
                for j in range(2):
                    ins = s.dma_start(
                        out=xt[j][:, :].rearrange("p (a c) -> p a c", a=BT),
                        in_=xin[j * 512:(j + 1) * 512, :]
                            .rearrange("(a p) c -> p a c", p=128))
                    mk[P_ + f"xl{j}"] = self.i(ins, f"xl{j}", 16)
            self.sec("sync", s0)

            for i in range(NG):
                b = i % 2
                x4 = lambda t_: t_[:, :].rearrange("p (a c) -> p a c", a=BT)

                if with_b0 and i >= 1:
                    def ab0(s, i=i):
                        b0_evac(s, i - 1)
                    self.sec("scalar", ab0)

                    def gb0s(g, i=i):
                        b0_store(g, i - 1)
                    self.sec("gpsimd", gb0s)

                def v1(v, i=i, b=b):
                    self.w(v, f"xl{b}", mk[P_ + f"xl{i}"])
                    if i >= 2:
                        self.w(v, f"as{b}", mk[P_ + f"as{i-2}"])
                    self.vi(v, v.tensor_reduce(sc4[:, 0:BT], x4(xt[b]), AX.X, OP.add))
                    self.vi(v, v.tensor_scalar(sc4[:, 0:BT], sc4[:, 0:BT], 1.0 / D,
                                               None, OP.mult))
                    self.vi(v, v.tensor_tensor(x4(vt[b]), x4(xt[b]),
                                               bc(sc4[:, 0:BT], D), OP.subtract),
                            wait=False)
                    self.vi(v, v.tensor_tensor(tmp[:, :], vt[b][:, :], vt[b][:, :],
                                               OP.mult), wait=False)
                    mk[P_ + f"ln1{i}"] = self.vi(
                        v, v.tensor_reduce(sc4[:, 4:4 + BT], x4(tmp), AX.X, OP.add),
                        wait=False)
                self.sec("vector", v1)

                def a1(s, i=i):
                    self.w(s, "V", mk[P_ + f"ln1{i}"])
                    ins = s.activation(sc4[:, 4:4 + BT], sc4[:, 4:4 + BT], AF.Sqrt,
                                       bias=self.cst[:, 0:1], scale=1.0 / D)
                    mk[P_ + f"rs1{i}"] = self.i(ins, "A", 1)
                self.sec("scalar", a1)

                def v2(v, i=i, b=b):
                    self.w(v, "A", mk[P_ + f"rs1{i}"])
                    self.vi(v, v.reciprocal(sc4[:, 4:4 + BT], sc4[:, 4:4 + BT]))
                    self.vi(v, v.tensor_tensor(x4(vt[b]), x4(vt[b]),
                                               bc(sc4[:, 4:4 + BT], D), OP.mult),
                            wait=False)
                    self.vi(v, v.tensor_reduce(sc4[:, 8:8 + BT], x4(vt[b]), AX.X,
                                               OP.add))
                    self.vi(v, v.tensor_scalar(sc4[:, 8:8 + BT], sc4[:, 8:8 + BT],
                                               1.0 / D, None, OP.mult))
                    self.vi(v, v.tensor_tensor(x4(xnt[b]), x4(vt[b]),
                                               bc(sc4[:, 8:8 + BT], D), OP.subtract),
                            wait=False)
                    self.vi(v, v.tensor_tensor(tmp[:, :], xnt[b][:, :], xnt[b][:, :],
                                               OP.mult), wait=False)
                    mk[P_ + f"ln2{i}"] = self.vi(
                        v, v.tensor_reduce(sc4[:, 12:12 + BT], x4(tmp), AX.X, OP.add),
                        wait=False)
                self.sec("vector", v2)

                def a2(s, i=i):
                    self.w(s, "V", mk[P_ + f"ln2{i}"])
                    ins = s.activation(sc4[:, 12:12 + BT], sc4[:, 12:12 + BT], AF.Sqrt,
                                       bias=self.cst[:, 0:1], scale=1.0 / D)
                    mk[P_ + f"rs2{i}"] = self.i(ins, "A", 1)
                self.sec("scalar", a2)

                def v3(v, i=i, b=b):
                    self.w(v, "A", mk[P_ + f"rs2{i}"])
                    self.vi(v, v.reciprocal(sc4[:, 12:12 + BT], sc4[:, 12:12 + BT]))
                    self.vi(v, v.tensor_tensor(x4(xnt[b]), x4(xnt[b]),
                                               bc(sc4[:, 12:12 + BT], D), OP.mult),
                            wait=False)
                    gam = self.gamma_bc[:, :].unsqueeze(1).broadcast_to((128, BT, D))
                    bet = self.beta_bc[:, :].unsqueeze(1).broadcast_to((128, BT, D))
                    self.vi(v, v.tensor_tensor(x4(xnt[b]), x4(xnt[b]), gam, OP.mult),
                            wait=False)
                    mk[P_ + f"xn{i}"] = self.vi(
                        v, v.tensor_tensor(x4(xnt[b]), x4(xnt[b]), bet, OP.add),
                        wait=False)
                self.sec("vector", v3)

                for st in range(BT):
                    def t1(t, i=i, b=b, st=st):
                        if st == 0:
                            self.w(t, "V", mk[P_ + f"xn{i}"])
                            if i >= 1:
                                self.w(t, "A", mk[P_ + f"xnTc{i-1}_{BT-1}"])
                        else:
                            self.w(t, "A", mk[P_ + f"xnTc{i}_{st-1}"])
                        for fc in range(DC):
                            k = 0 if fc % 2 == 0 else 4
                            ins = t.transpose(self.bank(k, (fc // 2) * 128, 128),
                                              xnt[b][:, (st * DC + fc) * 128:
                                                     (st * DC + fc + 1) * 128],
                                              self.ident[:, :])
                        mk[P_ + f"xtr{i}_{st}"] = self.i(ins, "P", 1)
                    self.sec("tensor", t1)

                    def a4(s, i=i, b=b, st=st):
                        self.w(s, "P", mk[P_ + f"xtr{i}_{st}"])
                        for fc in range(DC):
                            k = 0 if fc % 2 == 0 else 4
                            ins = s.activation(
                                xnT[b][:, (st * DC + fc) * 128:(st * DC + fc + 1) * 128],
                                self.bank(k, (fc // 2) * 128, 128), AF.Identity)
                            m = self.i(ins, "A", 1)
                        mk[P_ + f"xnTc{i}_{st}"] = m
                    self.sec("scalar", a4)

                def t2(t, i=i, b=b):
                    self.w(t, "A", mk[P_ + f"xnTc{i}_{BT-1}"])
                    if i >= 1:
                        self.w(t, "V", mk[P_ + f"lg{i-1}"])
                    for st in range(BT):
                        for fc in range(DC):
                            ins = t.matmul(self.bank(5, st * E, E),
                                           xnT[b][:, (st * DC + fc) * 128:
                                                  (st * DC + fc + 1) * 128],
                                           self.wr_sb[:, fc * E:(fc + 1) * E],
                                           start=(fc == 0), stop=(fc == DC - 1),
                                           skip_group_check=True)
                    mk[P_ + f"rmm{i}"] = self.i(ins, "P", 1)
                self.sec("tensor", t2)

                def v5(v, i=i, b=b):
                    self.w(v, "P", mk[P_ + f"rmm{i}"])
                    brb = self.br_bc[:, :].unsqueeze(1).broadcast_to((128, BT, E))
                    lg3 = lg[:, :].rearrange("p (a c) -> p a c", a=BT)
                    mk[P_ + f"lg{i}"] = self.vi(
                        v, v.tensor_tensor(lg3, self.bank(5, 0, W)
                                           .rearrange("p (a c) -> p a c", a=BT),
                                           brb, OP.add))
                    self.vi(v, v.reduce_max(sc4[:, 0:BT], lg3, AX.X))
                    self.vi(v, v.tensor_tensor(lg3, lg3, bc(sc4[:, 0:BT], E),
                                               OP.subtract))
                    ins = v.engine_nop()
                    mk[P_ + f"nm{i}"] = self.i(ins, "V", 1)
                self.sec("vector", v5)

                def a3(s, i=i, b=b):
                    self.w(s, "V", mk[P_ + f"nm{i}"])
                    if i >= 2:
                        self.w(s, f"as{b}", mk[P_ + f"as{i-2}"])
                    ins = s.activation(prb[b][:, :], lg[:, :], AF.Exp)
                    mk[P_ + f"ex{i}"] = self.i(ins, "A", 1)
                self.sec("scalar", a3)

                def v6(v, i=i, b=b):
                    self.w(v, "A", mk[P_ + f"ex{i}"])
                    p3 = prb[b][:, :].rearrange("p (a c) -> p a c", a=BT)
                    self.vi(v, v.tensor_reduce(sc4[:, 0:BT], p3, AX.X, OP.add))
                    self.vi(v, v.reciprocal(sc4[:, 0:BT], sc4[:, 0:BT]))
                    self.vi(v, v.tensor_tensor(p3, p3, bc(sc4[:, 0:BT], E), OP.mult))
                    self.vi(v, v.tensor_copy(pw[:, :], prb[b][:, :]))
                    pw3 = pw[:, :].rearrange("p (a c) -> p a c", a=BT)
                    w83 = w8[:, :].rearrange("p (a c) -> p a c", a=BT)
                    cn3 = cnd[:, :].rearrange("p (a c) -> p a c", a=BT)
                    for j in range(3):
                        self.vi(v, v.reduce_max(sc4[:, 4:4 + BT], pw3, AX.X))
                        self.vi(v, v.tensor_tensor(
                            w8i[:, :].rearrange("p (a c) -> p a c", a=BT), pw3,
                            bc(sc4[:, 4:4 + BT], E), OP.is_equal))
                        self.vi(v, v.select(cnd[:, :], w8i[:, :], self.iota84[:, :],
                                            self.eights4[:, :]))
                        self.vi(v, v.tensor_reduce(col4k(ejs[b], j), cn3, AX.X,
                                                   OP.min))
                        ejcol = col4(ejs[b], j)
                        self.vi(v, v.tensor_tensor(w83, self.iota84[:, :]
                                                   .rearrange("p (a c) -> p a c", a=BT),
                                                   bc(ejcol, E), OP.is_equal))
                        if j == 0:
                            self.vi(v, v.tensor_copy(Yt[:, :], w8[:, :]))
                        else:
                            self.vi(v, v.tensor_add(Yt[:, :], Yt[:, :], w8[:, :]))
                        self.vi(v, v.scalar_tensor_tensor(pw[:, :], w8[:, :], -BIG,
                                                          pw[:, :], OP.mult, OP.add))
                    self.vi(v, v.memset(col4(ejs[b], 3), 0.0))
                    mk[P_ + f"top{i}"] = self.vi(
                        v, v.tensor_copy(eidt[b][:, :], ejs[b][:, :]), wait=False)
                self.sec("vector", v6)

                def t3(t, i=i, b=b):
                    self.w(t, "V", mk[P_ + f"top{i}"])
                    if i >= 1:
                        self.w(t, "V", mk[P_ + f"csc{i-1}"])
                    for st in range(BT):
                        t.matmul(self.bank(7, st * E, E), self.UT[:, :],
                                 Yt[:, st * E:(st + 1) * E],
                                 start=(st == 0), stop=False, skip_group_check=True)
                    ins = t.matmul(self.bank(6, 0, W)[0:1, :], self.onesc[:, :],
                                   Yt[:, :], start=True, stop=True,
                                   skip_group_check=True)
                    mk[P_ + f"cs{i}"] = self.i(ins, "P", 1)
                self.sec("tensor", t3)

                def v7(v, i=i):
                    self.w(v, "P", mk[P_ + f"cs{i}"])
                    self.vi(v, v.tensor_copy(csrow[:, :], self.bank(6, 0, W)[0:1, :]))
                    self.vi(v, v.tensor_copy(corr[0:1, 0:E], base[0:1, :]))
                    for st in range(1, BT):
                        self.vi(v, v.tensor_add(
                            corr[0:1, st * E:(st + 1) * E],
                            corr[0:1, (st - 1) * E:st * E],
                            csrow[0:1, (st - 1) * E:st * E]))
                    self.vi(v, v.tensor_add(base[0:1, :],
                                            corr[0:1, (BT - 1) * E:BT * E],
                                            csrow[0:1, (BT - 1) * E:BT * E]))
                    ins = v.engine_nop()
                    mk[P_ + f"csc{i}"] = self.i(ins, "V", 1)
                self.sec("vector", v7)

                def t4(t, i=i):
                    self.w(t, "V", mk[P_ + f"csc{i}"])
                    ins = t.matmul(self.bank(7, 0, W), self.ones1[0:1, 0:128],
                                   corr[0:1, :], start=False, stop=True,
                                   skip_group_check=True)
                    mk[P_ + f"pb{i}"] = self.i(ins, "P", 1)
                self.sec("tensor", t4)

                if with_b0 and i < E:
                    b0_mms(i)

                def v8(v, i=i, b=b):
                    self.w(v, "P", mk[P_ + f"pb{i}"])
                    if i >= 2:
                        self.w(v, f"sc{b}", mk[P_ + f"sct{i-2}"])
                    self.vi(v, v.tensor_copy(Pt[:, :], self.bank(7, 0, W)))
                    pt3 = Pt[:, :].rearrange("p (a c) -> p a c", a=BT)
                    w83 = w8[:, :].rearrange("p (a c) -> p a c", a=BT)
                    for j in range(3):
                        ejcol = col4(ejs[b], j)
                        adcol = col4(adrf[b], j)
                        self.vi(v, v.tensor_tensor(w83, self.iota84[:, :]
                                                   .rearrange("p (a c) -> p a c", a=BT),
                                                   bc(ejcol, E), OP.is_equal))
                        self.vi(v, v.tensor_tensor(w83, w83, pt3, OP.mult))
                        self.vi(v, v.tensor_reduce(col4k(adrf[b], j), w83, AX.X,
                                                   OP.add))
                        self.vi(v, v.tensor_scalar(adcol, adcol, float(C - 1), None,
                                                   OP.min))
                        self.vi(v, v.tensor_scalar(tmp[:, 0:BT], ejcol, float(C),
                                                   None, OP.mult))
                        self.vi(v, v.tensor_tensor(adcol, adcol, tmp[:, 0:BT], OP.add))
                    self.vi(v, v.memset(col4(adrf[b], 3), 0.0))
                    mk[P_ + f"p2v{i}"] = self.vi(
                        v, v.tensor_copy(adri[b][:, :], adrf[b][:, :]), wait=False)
                self.sec("vector", v8)

                def g1(g, i=i, b=b):
                    self.w(g, "V", mk[P_ + f"p2v{i}"])
                    if i == 0:
                        self.w(g, "G", mk[P_ + "sz"])
                    if i >= 2:
                        self.w(g, f"sc{b}", mk[P_ + f"sct{i-2}"])
                    ins = g.iota(tok[b][:, :], pattern=[[128, BT]], base=i * 512,
                                 channel_multiplier=1)
                    m = self.i(ins, "Gc", 1)
                    g.wait_ge(self.S["Gc"], m)
                    rows = slice(i * 512, (i + 1) * 512)
                    ins = g.dma_start(out=self.v_d[rows, :]
                                      .rearrange("(a p) c -> p a c", p=128),
                                      in_=vt[b][:, :].rearrange("p (a c) -> p a c", a=BT))
                    self.i(ins, f"as{b}", 16)
                    ins = g.dma_start(out=self.xn_d[rows, :]
                                      .rearrange("(a p) c -> p a c", p=128),
                                      in_=xnt[b][:, :].rearrange("p (a c) -> p a c", a=BT))
                    self.i(ins, f"as{b}", 16)
                    ins = g.dma_start(out=self.pr_d[rows, :]
                                      .rearrange("(a p) c -> p a c", p=128),
                                      in_=prb[b][:, :].rearrange("p (a c) -> p a c", a=BT))
                    self.i(ins, f"as{b}", 16)
                    ins = g.dma_start(out=self.eid_d[rows, :]
                                      .rearrange("(a p) c -> p a c", p=128),
                                      in_=eidt[b][:, :].rearrange("p (a c) -> p a c", a=BT))
                    mk[P_ + f"as{i}"] = self.i(ins, f"as{b}", 16)
                    ins = g.dma_start(out=self.adr_d[rows, :]
                                      .rearrange("(a p) c -> p a c", p=128),
                                      in_=adri[b][:, :].rearrange("p (a c) -> p a c", a=BT))
                    self.i(ins, f"sc{b}", 16)
                    for st in range(BT):
                        for j in range(3):
                            ins = g.indirect_dma_start(
                                out=self.sidx_d[:, :],
                                out_offset=IOA(adri[b][:, st * 4 + j:st * 4 + j + 1], 0),
                                in_=tok[b][:, st:st + 1], in_offset=None)
                            self.i(ins, f"sc{b}", 16)
                    mk[P_ + f"sct{i}"] = self.N[f"sc{b}"]
                self.sec("gpsimd", g1)

                if i + 2 < NG:
                    def s1(s, i=i, b=b):
                        self.w(s, "V", mk[P_ + f"ln1{i}"])
                        ins = s.dma_start(
                            out=xt[b][:, :].rearrange("p (a c) -> p a c", a=BT),
                            in_=xin[(i + 2) * 512:(i + 3) * 512, :]
                                .rearrange("(a p) c -> p a c", p=128))
                        mk[P_ + f"xl{i+2}"] = self.i(ins, f"xl{b}", 16)
                    self.sec("sync", s1)

            if with_b0:
                def ab0f(s):
                    b0_evac(s, E - 1)
                self.sec("scalar", ab0f)

                def gb0f(g):
                    b0_store(g, E - 1)
                self.sec("gpsimd", gb0f)

    def phase_B0(self):
        nc, mk = self.nc, self.mk
        with ExitStack() as es:
            w2q = [es.enter_context(nc.sbuf_tensor(f"B0w2q{b}", [128, E * D], F32))
                   for b in range(2)]
            csb = es.enter_context(nc.sbuf_tensor("B0csb", [128, E], F32))
            cacc = es.enter_context(nc.sbuf_tensor("B0cacc", [128, E], F32))
            NQ = 4
            for e in range(E):
                for q in range(NQ):
                    b = (e * NQ + q) % 2

                    def s0(s, e=e, q=q, b=b):
                        if e * NQ + q >= 2:
                            self.w(s, "P", mk[f"B0mm{e*NQ+q-2}"])
                        ins = s.dma_start(
                            out=w2q[b][:, :].rearrange("p (h c) -> p h c", h=8),
                            in_=self.w2[e, q * 1024:(q + 1) * 1024, :]
                                .rearrange("(h p) c -> p h c", p=128))
                        mk[f"B0l{e*NQ+q}"] = self.i(ins, f"w2{b}", 16)
                    self.sec("sync", s0)

                    def t0(t, e=e, q=q, b=b):
                        self.w(t, f"w2{b}", mk[f"B0l{e*NQ+q}"])
                        if e == 0 and q == 0:
                            self.w(t, "A", mk["pro_a"])
                        if q == 0 and e > 0:
                            self.w(t, "V", mk[f"B0v{e-1}"])
                        for oc in range(DC):
                            for hc in range(8):
                                ins = t.matmul(
                                    self.bank(2, oc, 1),
                                    w2q[b][:, hc * D + oc * 128:hc * D + (oc + 1) * 128],
                                    self.rb1_sb[:, e * 32 + q * 8 + hc:
                                                e * 32 + q * 8 + hc + 1],
                                    start=(q == 0 and oc == 0 and hc == 0),
                                    stop=(q == NQ - 1 and oc == DC - 1 and hc == 7),
                                    skip_group_check=True)
                        mk[f"B0mm{e*NQ+q}"] = self.i(ins, "P", 1)
                    self.sec("tensor", t0)

                def v0(v, e=e):
                    self.w(v, "P", mk[f"B0mm{e*NQ+NQ-1}"])
                    for oc in range(DC):
                        m = self.vi(v, v.tensor_tensor(csb[:, oc:oc + 1], self.bank(2, oc, 1),
                                                       self.b2_sb[:, e * 8 + oc:e * 8 + oc + 1],
                                                       OP.add), wait=False)
                    v.wait_ge(self.S["V"], m)
                    if e == 0:
                        ins = v.tensor_copy(cacc[:, :], csb[:, :])
                    else:
                        ins = v.tensor_add(cacc[:, :], cacc[:, :], csb[:, :])
                    mk[f"B0v{e}"] = self.vi(v, ins)
                self.sec("vector", v0)

                def g0(g, e=e):
                    self.w(g, "V", mk[f"B0v{e}"])
                    for oc in range(DC):
                        ins = g.dma_start(out=self.c_d[e:e + 1, oc * 128:(oc + 1) * 128],
                                          in_=csb[:, oc:oc + 1])
                        self.i(ins, "G", 16)
                    mk[f"B0st{e}"] = self.N["G"]
                self.sec("gpsimd", g0)

            def gf(g):
                self.w(g, "V", mk[f"B0v{E-1}"])
                self.w(g, "G", mk[f"B0st{E-1}"])
                for oc in range(DC):
                    ins = g.dma_start(out=self.ct_d[0:1, oc * 128:(oc + 1) * 128],
                                      in_=cacc[:, oc:oc + 1])
                    self.i(ins, "G", 16)
                g.wait_ge(self.S["G"], self.N["G"])
                ins = g.dma_start(out=self.ctot_bc[:, :],
                                  in_=self.ct_d.broadcast_to((128, D)))
                mk["B0done"] = self.i(ins, "G", 16)
            self.sec("gpsimd", gf)

    def phase_B(self, call):
        nc, mk = self.nc, self.mk
        P_ = f"B{call}_"
        with ExitStack() as es:
            sb = lambda n, sh, dt=F32: es.enter_context(nc.sbuf_tensor(P_ + n, sh, dt))
            xnT = sb("xnT", [128, DC * C])
            hs = sb("hs", [128, 8 * C])
            w1b = [sb(f"w1b{b}", [128, 1024]) for b in range(2)]
            w2h = [sb(f"w2h{b}", [128, 8 * 512]) for b in range(2)]
            xg = [sb(f"xg{b}", [128, D]) for b in range(2)]
            ixb = [sb(f"ixb{b}", [128, 1], I32) for b in range(2)]
            yst = [sb(f"yst{b}", [128, 512]) for b in range(2)]
            b2bc = sb("b2bc", [128, D])

            self._b_evac_tail = 0
            for e in range(min(E, _B_NE)):
                EP = f"{P_}{e}_"

                def gb2(g, e=e):
                    if e > 0:
                        self.w(g, "V", mk[P_ + f"b2done{e-1}"])
                    ins = g.dma_start(out=b2bc[:, :],
                                      in_=self.b2[e:e + 1, :].broadcast_to((128, D)))
                    mk[P_ + f"b2l{e}"] = self.i(ins, "G", 16)
                self.sec("gpsimd", gb2)
                for sc in range(SLOT_CH):
                    b = sc % 2
                    gi = e * SLOT_CH + sc

                    def gg(g, sc=sc, b=b, gi=gi, e=e):
                        if gi >= 2:
                            self.w(g, "P", mk[P_ + f"gxtr{gi-2}"])
                            self.w(g, f"xg{b}", mk[P_ + f"gxl{gi-2}"])
                        ins = g.dma_start(
                            out=ixb[b][:, :],
                            in_=self.sidx_d[e * C + sc * 128:e * C + (sc + 1) * 128, :])
                        ixm = self.i(ins, f"ix{b}", 16)
                        g.wait_ge(self.S[f"ix{b}"], ixm)
                        ins = g.indirect_dma_start(
                            out=xg[b][:, :], out_offset=None,
                            in_=self.xn_d[:, :], in_offset=IOA(ixb[b][:, :], 0))
                        mk[P_ + f"gxl{gi}"] = self.i(ins, f"xg{b}", 16)
                    self.sec("gpsimd", gg)

                    def tt(t, sc=sc, b=b, gi=gi):
                        self.w(t, f"xg{b}", mk[P_ + f"gxl{gi}"])
                        if sc == 0:
                            self.w(t, "V", self._b_evac_tail)
                        if gi >= 1:
                            self.w(t, "V", mk[P_ + f"gxc{gi-1}"])
                        for fc in range(DC):
                            k = 0 if fc % 2 == 0 else 4
                            ins = t.transpose(self.bank(k, (fc // 2) * 128, 128),
                                              xg[b][:, fc * 128:(fc + 1) * 128],
                                              self.ident[:, :])
                        mk[P_ + f"gxtr{gi}"] = self.i(ins, "P", 1)
                    self.sec("tensor", tt)

                    def vv(v, sc=sc, gi=gi):
                        self.w(v, "P", mk[P_ + f"gxtr{gi}"])
                        for fc in range(DC):
                            k = 0 if fc % 2 == 0 else 4
                            m = self.vi(v, v.tensor_copy(
                                xnT[:, fc * C + sc * 128:fc * C + (sc + 1) * 128],
                                self.bank(k, (fc // 2) * 128, 128)), wait=False)
                        mk[P_ + f"gxc{gi}"] = m
                    self.sec("vector", vv)

                for q in range(0 if _B_GONLY else min(4, _B_NQ)):
                    QP = f"{EP}q{q}_"
                    ts_q0, ts_q1 = self.N["ts0"], self.N["ts1"]

                    for hc in range(8):
                        hq = q * 8 + hc
                        b = hc % 2
                        grp = 0 if hc % 2 == 0 else 4
                        wix = (call * 8 + e) * 32 + hq

                        def sw1(s, e=e, hq=hq, b=b, wix=wix):
                            if wix >= 2:
                                self.w(s, "P", mk[f"mm1g_{wix-2}"])
                            ins = s.dma_start(
                                out=w1b[b][:, :].rearrange("p (a c) -> p a c", a=DC),
                                in_=self.w1[e, :, hq * 128:(hq + 1) * 128]
                                    .rearrange("(a p) c -> p a c", p=128))
                            mk[f"w1l{wix}"] = self.i(ins, f"w1{b}", 16)
                        self.sec("sync", sw1)

                        def tm1(t, e=e, q=q, hc=hc, b=b, hq=hq, grp=grp, wix=wix):
                            self.w(t, f"w1{b}", mk[f"w1l{wix}"])
                            if hc == 0:
                                self.w(t, "V", mk[P_ + f"gxc{e*SLOT_CH+SLOT_CH-1}"])
                            if hc < 2:
                                self.w(t, "V", self._b_evac_tail)
                            if wix >= 2:
                                self.w(t, "A", mk[f"ev1_{wix-2}"])
                            for fc in range(DC):
                                for Ng in range(4):
                                    ins = t.matmul(
                                        self.bank(grp + Ng, 0, NGW),
                                        w1b[b][:, fc * 128:(fc + 1) * 128],
                                        xnT[:, fc * C + Ng * NGW:fc * C + (Ng + 1) * NGW],
                                        start=(fc == 0), stop=(fc == DC - 1),
                                        skip_group_check=True)
                            mk[f"mm1g_{wix}"] = self.i(ins, "P", 1)
                        self.sec("tensor", tm1)

                        def am1(s, e=e, hc=hc, hq=hq, grp=grp, wix=wix):
                            self.w(s, "P", mk[f"mm1g_{wix}"])
                            for Ng in range(4):
                                ins = s.activation(
                                    hs[:, hc * C + Ng * NGW:hc * C + (Ng + 1) * NGW],
                                    self.bank(grp + Ng, 0, NGW), AF.Relu,
                                    bias=self.b1_sb[:, e * 32 + hq:e * 32 + hq + 1])
                                m = self.i(ins, "A", 1)
                            mk[f"ev1_{wix}"] = m
                        self.sec("scalar", am1)

                    for half in range(2):
                        widx = ((call * 8 + e) * 4 + q) * 2 + half
                        wb = widx % 2

                        def sw2(s, e=e, q=q, half=half, widx=widx, wb=wb):
                            if widx >= 2:
                                self.w(s, "P", mk[f"mm2w_{widx-2}"])
                            for hc in range(8):
                                ins = s.dma_start(
                                    out=w2h[wb][:, hc * 512:(hc + 1) * 512],
                                    in_=self.w2[e, q * 1024 + hc * 128:
                                                q * 1024 + (hc + 1) * 128,
                                                half * 512:(half + 1) * 512])
                                self.i(ins, f"w2{wb}", 16)
                            mk[f"w2l{widx}"] = self.N[f"w2{wb}"]
                        self.sec("sync", sw2)

                        for sblock in range(SLOT_CH):
                            gidx = ((call * 8 + e) * 4 + q) * 2 * SLOT_CH \
                                + half * SLOT_CH + sblock
                            bk = gidx % 8
                            tsb = gidx % 2

                            def tm2(t, e=e, q=q, half=half, sblock=sblock,
                                    widx=widx, wb=wb, gidx=gidx, bk=bk):
                                if half == 0 and sblock == 0:
                                    self.w(t, "A", mk[f"ev1_{(call*8+e)*32 + q*8 + 7}"])
                                if sblock == 0:
                                    self.w(t, f"w2{wb}", mk[f"w2l{widx}"])
                                pm = mk.get(f"yev_{gidx-8}", 0)
                                self.w(t, "V", pm)
                                for hc in range(8):
                                    ins = t.matmul(
                                        self.bank(bk, 0, 512),
                                        hs[:, hc * C + sblock * 128:
                                           hc * C + (sblock + 1) * 128],
                                        w2h[wb][:, hc * 512:(hc + 1) * 512],
                                        start=(hc == 0), stop=(hc == 7),
                                        skip_group_check=True)
                                m = self.i(ins, "P", 1)
                                mk[f"mm2n_{gidx}"] = m
                                if sblock == SLOT_CH - 1:
                                    mk[f"mm2w_{widx}"] = m
                            self.sec("tensor", tm2)

                            def ve2(v, e=e, q=q, half=half, sblock=sblock,
                                    gidx=gidx, bk=bk, tsb=tsb):
                                self.w(v, "P", mk[f"mm2n_{gidx}"])
                                self.w(v, f"ts{tsb}", self.N[f"ts{tsb}"])
                                if q == 0 and half == 0 and sblock == 0:
                                    self.w(v, "G", mk[P_ + f"b2l{e}"])
                                if q == 0:
                                    ins = v.tensor_add(yst[tsb][:, :],
                                                       self.bank(bk, 0, 512),
                                                       b2bc[:, half * 512:(half + 1) * 512])
                                else:
                                    ins = v.tensor_copy(yst[tsb][:, :],
                                                        self.bank(bk, 0, 512))
                                m = self.vi(v, ins, wait=False)
                                mk[f"yev_{gidx}"] = m
                                self._b_evac_tail = m
                                if q == 0 and half == 1 and sblock == SLOT_CH - 1:
                                    mk[P_ + f"b2done{e}"] = m
                            self.sec("vector", ve2)

                            def gy(g, e=e, q=q, half=half, sblock=sblock,
                                   gidx=gidx, tsb=tsb, ts_q0=ts_q0, ts_q1=ts_q1):
                                if _B_NOSTORE:
                                    return
                                self.w(g, "V", mk[f"yev_{gidx}"])
                                if q > 0 and half == 0 and sblock == 0:
                                    self.w(g, "ts0", ts_q0)
                                    self.w(g, "ts1", ts_q1)
                                dst = self.y_d[e * C + sblock * 128:
                                               e * C + (sblock + 1) * 128,
                                               half * 512:(half + 1) * 512]
                                ins = g.dma_start(
                                    out=dst, in_=yst[tsb][:, :],
                                    accum_op=(OP.bypass if (q == 0 or _NO_ACCUM)
                                              else OP.add))
                                self.i(ins, f"ts{tsb}", 16)
                            self.sec("gpsimd", gy)

    def phase_C(self, call, xin, xout):
        nc, mk = self.nc, self.mk
        P_ = f"C{call}_"
        BT = 2
        NG = NT // BT
        TW = BT * 128
        with ExitStack() as es:
            sb = lambda n, sh, dt=F32: es.enter_context(nc.sbuf_tensor(P_ + n, sh, dt))
            xt = [sb(f"xt{b}", [128, BT * D]) for b in range(2)]
            vt2 = [sb(f"vt{b}", [128, BT * D]) for b in range(2)]
            prt = [sb(f"prt{b}", [128, BT * E]) for b in range(2)]
            eit = [sb(f"eit{b}", [128, BT * 4], I32) for b in range(2)]
            adt = [sb(f"adt{b}", [128, BT * 4], I32) for b in range(2)]
            ygs = [[sb(f"yg{j}{b}", [128, BT * D]) for b in range(2)] for j in range(3)]
            xo = [sb(f"xo{b}", [128, BT * D]) for b in range(2)]
            ejf = sb("ejf", [128, BT * 4])
            w8 = sb("w8", [128, BT * E])
            uw = [sb(f"uw{b}", [128, BT * E]) for b in range(2)]
            pjs = sb("pjs", [128, BT * 4])
            scw = sb("scw", [128, BT * 8])
            moe = sb("moe", [128, BT * D])
            tmd = sb("tmd", [128, BT * D])
            tm2 = sb("tm2", [128, BT * D])
            cS = sb("cS", [8, D])
            uwT = sb("uwT", [8, 2 * 128])

            def bc(ap2d, inner):
                return ap2d.unsqueeze(2).broadcast_to((128, ap2d.shape[1], inner))

            def colk(t_, k, j):
                return t_[:, :].rearrange("p (a c) -> p a c", a=BT)[:, :, j]

            def s00(s):
                ins = s.dma_start(out=cS[:, :], in_=self.c_d[:, :])
                mk[P_ + "cl"] = self.i(ins, "L", 16)
            self.sec("sync", s00)

            for i in range(NG):
                b = i % 2
                rows = slice(i * TW, (i + 1) * TW)

                def s0(s, i=i, b=b, rows=rows):
                    if i >= 2:
                        self.w(s, "V", mk[P_ + f"cmb{i-2}"])
                    for o_, i_ in ((prt[b], self.pr_d[rows, :]),
                                   (eit[b], self.eid_d[rows, :]),
                                   (adt[b], self.adr_d[rows, :]),
                                   (xt[b], xin[rows, :]),
                                   (vt2[b], self.v_d[rows, :])):
                        ins = s.dma_start(
                            out=o_[:, :].rearrange("p (a c) -> p a c", a=BT),
                            in_=i_.rearrange("(a p) c -> p a c", p=128))
                        self.i(ins, f"ld{b}", 16)
                    mk[P_ + f"ld{i}"] = self.N[f"ld{b}"]
                self.sec("sync", s0)

                def g0(g, i=i, b=b):
                    self.w(g, f"ld{b}", mk[P_ + f"ld{i}"])
                    if i >= 2:
                        self.w(g, "V", mk[P_ + f"cmb{i-2}"])
                    for st in range(BT):
                        for j in range(3):
                            ins = g.indirect_dma_start(
                                out=ygs[j][b][:, st * D:(st + 1) * D],
                                out_offset=None,
                                in_=self.y_d[:, :],
                                in_offset=IOA(adt[b][:, st * 4 + j:st * 4 + j + 1], 0))
                            self.i(ins, f"yg{b}", 16)
                    mk[P_ + f"yg{i}"] = self.N[f"yg{b}"]
                self.sec("gpsimd", g0)

                def v0(v, i=i, b=b):
                    self.w(v, f"yg{b}", mk[P_ + f"yg{i}"])
                    if i >= 2:
                        self.w(v, f"st{b}", mk[P_ + f"st{i-2}"])
                        self.w(v, "P", mk[P_ + f"cmm{i-2}"])
                    self.vi(v, v.tensor_copy(ejf[:, :], eit[b][:, :]))
                    w83 = w8[:, :].rearrange("p (a c) -> p a c", a=BT)
                    pr3 = prt[b][:, :].rearrange("p (a c) -> p a c", a=BT)
                    for j in range(3):
                        self.vi(v, v.tensor_tensor(
                            w83, self.iota84[:, 0:BT * E]
                            .rearrange("p (a c) -> p a c", a=BT),
                            bc(colk(ejf, BT, j), E), OP.is_equal))
                        self.vi(v, v.tensor_tensor(w83, w83, pr3, OP.mult))
                        self.vi(v, v.tensor_reduce(
                            pjs[:, :].rearrange("p (a c) -> p a c", a=BT)[:, :, j:j + 1],
                            w83, AX.X, OP.add))
                    p0, p1, p2 = (colk(pjs, BT, j) for j in range(3))
                    c0, c1, c2, c3, c4, c5 = (colk(scw, BT, k) for k in range(6))
                    self.vi(v, v.tensor_tensor(c0, p0, p1, OP.min))
                    self.vi(v, v.tensor_tensor(c0, c0, p2, OP.min))
                    self.vi(v, v.tensor_tensor(c3, p2, c0, OP.is_equal))
                    self.vi(v, v.tensor_scalar(c4, c3, -1.0, 1.0, OP.mult, OP.add))
                    self.vi(v, v.tensor_tensor(c5, p1, c0, OP.is_equal))
                    self.vi(v, v.tensor_tensor(c5, c5, c4, OP.mult))
                    self.vi(v, v.tensor_tensor(c1, p0, c0, OP.is_equal))
                    self.vi(v, v.tensor_tensor(c1, c1, c4, OP.mult))
                    self.vi(v, v.tensor_scalar(c2, c5, -1.0, 1.0, OP.mult, OP.add))
                    self.vi(v, v.tensor_tensor(c1, c1, c2, OP.mult))
                    for pj, dj in ((p0, c1), (p1, c5), (p2, c3)):
                        self.vi(v, v.tensor_tensor(c2, pj, dj, OP.mult))
                        self.vi(v, v.tensor_tensor(pj, pj, c2, OP.subtract))
                    self.vi(v, v.tensor_tensor(c0, p0, p1, OP.add))
                    self.vi(v, v.tensor_tensor(c0, c0, p2, OP.add))
                    uw3 = uw[b][:, :].rearrange("p (a c) -> p a c", a=BT)
                    self.vi(v, v.tensor_tensor(uw3, bc(c0, E), bc(c0, E), OP.min),
                            wait=False)
                    for j in range(3):
                        self.vi(v, v.tensor_tensor(
                            w83, self.iota84[:, 0:BT * E]
                            .rearrange("p (a c) -> p a c", a=BT),
                            bc(colk(ejf, BT, j), E), OP.is_equal))
                        self.vi(v, v.tensor_tensor(w83, w83,
                                                   bc(colk(pjs, BT, j), E), OP.mult))
                        self.vi(v, v.tensor_tensor(uw[b][:, :], uw[b][:, :], w8[:, :],
                                                   OP.subtract))
                    mk[P_ + f"cmb_u{i}"] = self.vi(v, v.engine_nop())
                self.sec("vector", v0)

                def ac0(s, i=i, b=b):
                    self.w(s, "V", mk[P_ + f"cmb_u{i}"])
                    for j, dst in ((0, moe), (1, tmd), (2, tm2)):
                        for st in range(BT):
                            wcol = pjs[:, :].rearrange("p (a c) -> p a c",
                                                       a=BT)[:, st, j:j + 1]
                            ins = s.activation(dst[:, st * D:(st + 1) * D],
                                               ygs[j][b][:, st * D:(st + 1) * D],
                                               AF.Identity, scale=wcol)
                            self.i(ins, "A", 1)
                    mk[P_ + f"cmw{i}"] = self.N["A"]
                self.sec("scalar", ac0)

                def v0b(v, i=i):
                    self.w(v, "A", mk[P_ + f"cmw{i}"])
                    self.vi(v, v.tensor_add(moe[:, :], moe[:, :], tmd[:, :]),
                            wait=False)
                    mk[P_ + f"cmb_m{i}"] = self.vi(
                        v, v.tensor_add(moe[:, :], moe[:, :], tm2[:, :]), wait=False)
                self.sec("vector", v0b)

                def t0(t, i=i, b=b):
                    self.w(t, "V", mk[P_ + f"cmb_u{i}"])
                    if i == 0:
                        self.w(t, "L", mk[P_ + "cl"])
                    if i >= 1:
                        self.w(t, "V", mk[P_ + f"cmb{i-1}"])
                    for st in range(BT):
                        ins = t.transpose(self.bank(0, st * 128, 128)[0:E, :],
                                          uw[b][:, st * E:(st + 1) * E],
                                          self.ident[:, :])
                    mk[P_ + f"ctr{i}"] = self.i(ins, "P", 1)
                self.sec("tensor", t0)

                def v1(v, i=i):
                    self.w(v, "P", mk[P_ + f"ctr{i}"])
                    mk[P_ + f"ctc{i}"] = self.vi(
                        v, v.tensor_copy(uwT[0:E, :],
                                         self.bank(0, 0, BT * 128)[0:E, :]))
                self.sec("vector", v1)

                def t1(t, i=i):
                    self.w(t, "V", mk[P_ + f"ctc{i}"])
                    for st in range(BT):
                        for half in range(2):
                            ins = t.matmul(
                                self.bank(1 + st * 2 + half, 0, 512),
                                uwT[0:E, st * 128:(st + 1) * 128],
                                cS[0:E, half * 512:(half + 1) * 512],
                                start=True, stop=True, skip_group_check=True)
                    mk[P_ + f"cmm{i}"] = self.i(ins, "P", 1)
                self.sec("tensor", t1)

                def v2(v, i=i, b=b):
                    self.w(v, "P", mk[P_ + f"cmm{i}"])
                    self.w(v, "V", mk[P_ + f"cmb_m{i}"])
                    for st in range(BT):
                        for half in range(2):
                            self.vi(v, v.tensor_add(
                                moe[:, st * D + half * 512:st * D + (half + 1) * 512],
                                moe[:, st * D + half * 512:st * D + (half + 1) * 512],
                                self.bank(1 + st * 2 + half, 0, 512)), wait=False)
                    self.vi(v, v.tensor_add(tmd[:, :], vt2[b][:, :], moe[:, :]),
                            wait=False)
                    self.vi(v, v.tensor_scalar(tmd[:, :], tmd[:, :], 0.5, None,
                                               OP.mult), wait=False)
                    mk[P_ + f"cmb{i}"] = self.vi(
                        v, v.tensor_add(xo[b][:, :], xt[b][:, :], tmd[:, :]),
                        wait=False)
                self.sec("vector", v2)


                def g1(g, i=i, b=b, rows=rows):
                    self.w(g, "V", mk[P_ + f"cmb{i}"])
                    ins = g.dma_start(out=xout[rows, :]
                                      .rearrange("(a p) c -> p a c", p=128),
                                      in_=xo[b][:, :]
                                      .rearrange("p (a c) -> p a c", a=BT))
                    mk[P_ + f"st{i}"] = self.i(ins, f"st{b}", 16)
                self.sec("gpsimd", g1)

    def phase_D(self):
        nc, mk = self.nc, self.mk
        P_ = "D_"
        with ExitStack() as eo:
            kv_sb = eo.enter_context(nc.sbuf_tensor("D_kv", [128, 520], F32))
            with ExitStack() as ea:
                sb = lambda n, sh, dt=F32: ea.enter_context(nc.sbuf_tensor(P_ + n, sh, dt))
                wq_sb = sb("wq", [128, DC * D])
                wk_sb = sb("wk", [128, DC * D])
                wv_sb = sb("wv", [128, DC * D])
                xt = [sb(f"xt{b}", [128, D]) for b in range(2)]
                v2 = [sb(f"v2{b}", [128, D]) for b in range(2)]
                v2T4 = sb("v2T4", [128, DC * 512])
                kf = sb("kf", [128, D])
                kp = sb("kp", [128, D])
                tmpi = sb("tmpi", [128, D], I32)
                ek = sb("ek", [128, D])
                vt_sb = sb("vts", [128, D])
                vh = sb("vh", [128, NH * 65])
                qs = [sb(f"qs{b}", [128, 512]) for b in range(2)]
                qe = [sb(f"qe{b}", [128, 512]) for b in range(2)]
                qf = [sb(f"qf{b}", [128, 512]) for b in range(2)]
                sc1 = sb("sc1", [128, 8])
                tmp = sb("tmp", [128, D])

                def s0(s):
                    for fc in range(DC):
                        for wsb, wsrc in ((wq_sb, self.wq), (wk_sb, self.wk),
                                          (wv_sb, self.wv)):
                            ins = s.dma_start(out=wsb[:, fc * D:(fc + 1) * D],
                                              in_=wsrc[fc * 128:(fc + 1) * 128, :])
                            self.i(ins, "L", 16)
                    mk["D_wl"] = self.N["L"]
                    for j in range(2):
                        ins = s.dma_start(out=xt[j][:, :],
                                          in_=self.x1_d[j * 128:(j + 1) * 128, :])
                        mk[P_ + f"xl{j}"] = self.i(ins, f"xl{j}", 16)
                self.sec("sync", s0)

                zf = sb("zf", [128, 512])

                def gv(g):
                    ins = g.memset(vh[:, :], 1.0)
                    self.i(ins, "Gc", 1)
                    ins = g.memset(zf[:, :], 0.0)
                    mk[P_ + "vh1"] = self.i(ins, "Gc", 1)
                self.sec("gpsimd", gv)

                def tz(t):
                    self.w(t, "Gc", mk[P_ + "vh1"])
                    t.matmul(self.bank(6, 0, 512), self.ident[:, :], zf[:, :],
                             start=True, stop=True, skip_group_check=True)
                    ins = t.matmul(self.bank(7, 0, 512), self.ident[:, :], zf[:, :],
                                   start=True, stop=True, skip_group_check=True)
                    mk[P_ + "kvz"] = self.i(ins, "P", 1)
                self.sec("tensor", tz)

                for i in range(NT):
                    b = i % 2

                    def v1(v, i=i, b=b):
                        self.w(v, f"xl{b}", mk[P_ + f"xl{i}"])
                        self.vi(v, v.tensor_reduce(sc1[:, 0:1], xt[b][:, :], AX.X, OP.add))
                        self.vi(v, v.tensor_scalar(sc1[:, 0:1], sc1[:, 0:1], 1.0 / D, None, OP.mult))
                        self.vi(v, v.tensor_scalar(v2[b][:, :], xt[b][:, :], sc1[:, 0:1], None,
                                                   OP.subtract), wait=False)
                        self.vi(v, v.tensor_tensor(tmp[:, :], v2[b][:, :], v2[b][:, :], OP.mult),
                                wait=False)
                        mk[P_ + f"ln{i}"] = self.vi(v, v.tensor_reduce(sc1[:, 1:2], tmp[:, :],
                                                                       AX.X, OP.add), wait=False)
                    self.sec("vector", v1)

                    def a1(s, i=i):
                        self.w(s, "V", mk[P_ + f"ln{i}"])
                        ins = s.activation(sc1[:, 2:3], sc1[:, 1:2], AF.Sqrt,
                                           bias=self.cst[:, 0:1], scale=1.0 / D)
                        mk[P_ + f"rs{i}"] = self.i(ins, "A", 1)
                    self.sec("scalar", a1)

                    def v2f(v, i=i, b=b):
                        self.w(v, "A", mk[P_ + f"rs{i}"])
                        self.vi(v, v.reciprocal(sc1[:, 2:3], sc1[:, 2:3]))
                        mk[P_ + f"v2{i}"] = self.vi(v, v.tensor_scalar(v2[b][:, :], v2[b][:, :],
                                                                       sc1[:, 2:3], None, OP.mult),
                                                    wait=False)
                    self.sec("vector", v2f)

                    def t1(t, i=i, b=b):
                        self.w(t, "V", mk[P_ + f"v2{i}"])
                        if i >= 1:
                            self.w(t, "V", mk[P_ + f"kf{i-1}"])
                        for fc in range(DC):
                            ins = t.transpose(self.bank(fc % 2, (fc // 2) * 128, 128),
                                              v2[b][:, fc * 128:(fc + 1) * 128],
                                              self.ident[:, :])
                        mk[P_ + f"vtr{i}"] = self.i(ins, "P", 1)
                    self.sec("tensor", t1)

                    def a3c(s, i=i):
                        self.w(s, "P", mk[P_ + f"vtr{i}"])
                        if i >= 4:
                            self.w(s, "P", mk[P_ + f"qmm{(i//4 - 1)*8 + 7}"])
                        g = i % 4
                        for fc in range(DC):
                            ins = s.activation(
                                v2T4[:, fc * 512 + g * 128:fc * 512 + (g + 1) * 128],
                                self.bank(fc % 2, (fc // 2) * 128, 128), AF.Identity)
                            m = self.i(ins, "A", 1)
                        mk[P_ + f"vTc{i}"] = m
                    self.sec("scalar", a3c)

                    def t2(t, i=i):
                        self.w(t, "A", mk[P_ + f"vTc{i}"])
                        if i == 0:
                            self.w(t, "L", mk["D_wl"])
                        if i >= 1:
                            self.w(t, "A", mk[P_ + f"vev{i-1}"])
                        g = i % 4
                        for half in range(2):
                            for fc in range(DC):
                                ins = t.matmul(
                                    self.bank(half, 0, 512),
                                    v2T4[:, fc * 512 + g * 128:fc * 512 + (g + 1) * 128],
                                    wk_sb[:, fc * D + half * 512:fc * D + (half + 1) * 512],
                                    start=(fc == 0), stop=(fc == DC - 1))
                        mk[P_ + f"kmm{i}"] = self.i(ins, "P", 1)
                        for half in range(2):
                            for fc in range(DC):
                                ins = t.matmul(
                                    self.bank(2 + half, 0, 512),
                                    v2T4[:, fc * 512 + g * 128:fc * 512 + (g + 1) * 128],
                                    wv_sb[:, fc * D + half * 512:fc * D + (half + 1) * 512],
                                    start=(fc == 0), stop=(fc == DC - 1))
                        mk[P_ + f"vmm{i}"] = self.i(ins, "P", 1)
                    self.sec("tensor", t2)

                    def a2(s, i=i):
                        self.w(s, "P", mk[P_ + f"kmm{i}"])
                        if i >= 1:
                            self.w(s, "V", mk[P_ + f"vh{i-1}"])
                        ins = s.activation(ek[:, :], self.psA[:, 0:1024], AF.Exp)
                        mk[P_ + f"ekx{i}"] = self.i(ins, "A", 1)
                        self.w(s, "P", mk[P_ + f"vmm{i}"])
                        ins = s.activation(vt_sb[:, :], self.psA[:, 1024:2048], AF.Identity)
                        mk[P_ + f"vev{i}"] = self.i(ins, "A", 1)
                    self.sec("scalar", a2)

                    def v4(v, i=i):
                        self.w(v, "A", mk[P_ + f"ekx{i}"])
                        if i == 0:
                            self.w(v, "Gc", mk[P_ + "vh1"])
                        self.vi(v, v.tensor_scalar(tmpi[:, :], self.psA[:, 0:1024], 0.0,
                                                   None, OP.is_gt), wait=False)
                        self.vi(v, v.tensor_scalar(kp[:, :], self.psA[:, 0:1024], 1.0,
                                                   None, OP.add), wait=False)
                        mk[P_ + f"kf{i}"] = self.vi(v, v.select(kf[:, :], tmpi[:, :],
                                                                kp[:, :], ek[:, :]), wait=False)
                        self.w(v, "A", mk[P_ + f"vev{i}"])
                        for h in range(NH):
                            m = self.vi(v, v.tensor_copy(vh[:, h * 65:h * 65 + 64],
                                                         vt_sb[:, h * 64:(h + 1) * 64]),
                                        wait=False)
                        mk[P_ + f"vh{i}"] = m
                    self.sec("vector", v4)

                    def t3(t, i=i):
                        self.w(t, "V", mk[P_ + f"kf{i}"])
                        self.w(t, "V", mk[P_ + f"vh{i}"])
                        for h in range(NH):
                            kb = 6 + h // 8
                            rem = h % 8
                            ph, cb = rem % 2, rem // 2
                            ps = self.bank(kb, cb * 65, 65)[ph * 64:ph * 64 + 64, :]
                            ins = t.matmul(ps, kf[:, h * 64:(h + 1) * 64],
                                           vh[:, h * 65:(h + 1) * 65],
                                           start=False, stop=(i == NT - 1),
                                           skip_group_check=True)
                        mk[P_ + f"kv{i}"] = self.i(ins, "P", 1)
                    self.sec("tensor", t3)

                    if i % 4 == 3:
                        gq = i // 4
                        for qc in range(DC):
                            kq = 4 + qc % 2
                            bq = qc % 2
                            idx = gq * 8 + qc

                            def tq(t, gq=gq, qc=qc, kq=kq, idx=idx):
                                if qc < 2:
                                    self.w(t, "A", mk[P_ + f"vTc{gq*4+3}"])
                                if idx >= 2:
                                    self.w(t, "A", mk[P_ + f"qcp{idx-2}"])
                                for fc in range(DC):
                                    ins = t.matmul(
                                        self.bank(kq, 0, 512),
                                        wq_sb[:, fc * D + qc * 128:fc * D + (qc + 1) * 128],
                                        v2T4[:, fc * 512:(fc + 1) * 512],
                                        start=(fc == 0), stop=(fc == DC - 1))
                                mk[P_ + f"qmm{idx}"] = self.i(ins, "P", 1)
                            self.sec("tensor", tq)

                            def aq(s, qc=qc, kq=kq, bq=bq, idx=idx):
                                self.w(s, "P", mk[P_ + f"qmm{idx}"])
                                if idx >= 2:
                                    self.w(s, "V", mk[P_ + f"qfv{idx-2}"])
                                self.ai(s, s.activation(qe[bq][:, :], self.bank(kq, 0, 512),
                                                        AF.Exp))
                                mk[P_ + f"qcp{idx}"] = self.ai(
                                    s, s.activation(qs[bq][:, :], self.bank(kq, 0, 512),
                                                    AF.Identity))
                            self.sec("scalar", aq)

                            def vq(v, bq=bq, idx=idx):
                                self.w(v, "A", mk[P_ + f"qcp{idx}"])
                                if idx >= 2:
                                    self.w(v, f"st{bq}", mk[P_ + f"qst{idx-2}"])
                                self.vi(v, v.tensor_scalar(tmpi[:, 0:512], qs[bq][:, :], 0.0,
                                                           None, OP.is_gt), wait=False)
                                self.vi(v, v.tensor_scalar(qs[bq][:, :], qs[bq][:, :], 1.0,
                                                           None, OP.add), wait=False)
                                mk[P_ + f"qfv{idx}"] = self.vi(
                                    v, v.select(qf[bq][:, :], tmpi[:, 0:512],
                                                qs[bq][:, :], qe[bq][:, :]), wait=False)
                            self.sec("vector", vq)

                            def gq_(g, gq=gq, qc=qc, bq=bq, idx=idx):
                                self.w(g, "V", mk[P_ + f"qfv{idx}"])
                                for tt4 in range(4):
                                    ins = g.dma_start(
                                        out=self.qf_d[gq * 4 + tt4, :, qc, :],
                                        in_=qf[bq][:, tt4 * 128:(tt4 + 1) * 128])
                                    self.i(ins, f"st{bq}", 16)
                                mk[P_ + f"qst{idx}"] = self.N[f"st{bq}"]
                            self.sec("gpsimd", gq_)

                    if i + 2 < NT:
                        def s1(s, i=i, b=b):
                            self.w(s, "V", mk[P_ + f"v2{i}"])
                            ins = s.dma_start(out=xt[b][:, :],
                                              in_=self.x1_d[(i + 2) * 128:(i + 3) * 128, :])
                            mk[P_ + f"xl{i+2}"] = self.i(ins, f"xl{b}", 16)
                        self.sec("sync", s1)

                def vkv(v):
                    self.w(v, "P", mk[P_ + f"kv{NT-1}"])
                    self.vi(v, v.tensor_copy(kv_sb[:, 0:260], self.bank(6, 0, 260)), wait=False)
                    mk[P_ + "kvc"] = self.vi(v, v.tensor_copy(kv_sb[:, 260:520],
                                                              self.bank(7, 0, 260)))
                self.sec("vector", vkv)

                self.barrier()

            with ExitStack() as eb:
                sb = lambda n, sh, dt=F32: eb.enter_context(nc.sbuf_tensor("Db" + n, sh, dt))
                wo_sb = sb("wo", [128, DC * D])
                qfs = [sb(f"qfs{b}", [128, DC * 128]) for b in range(2)]
                x1t = [sb(f"x1t{b}", [128, D]) for b in range(2)]
                cat = sb("cat", [128, D])
                catT = sb("catT", [128, D])
                xa = [sb(f"xa{b}", [128, D]) for b in range(2)]
                rd = sb("rd", [128, NH])

                def s0(s):
                    for fc in range(DC):
                        ins = s.dma_start(out=wo_sb[:, fc * D:(fc + 1) * D],
                                          in_=self.wo[fc * 128:(fc + 1) * 128, :])
                        self.i(ins, "L", 16)
                    mk["Db_wl"] = self.N["L"]
                    for j in range(2):
                        ins = s.dma_start(
                            out=qfs[j][:, :].rearrange("p (a c) -> p a c", a=DC),
                            in_=self.qf_d[j, :, :, :])
                        self.i(ins, f"qf{j}", 16)
                        mk[P_ + f"ql{j}"] = self.N[f"qf{j}"]
                        ins = s.dma_start(out=x1t[j][:, :],
                                          in_=self.x1_d[j * 128:(j + 1) * 128, :])
                        mk[P_ + f"x1l{j}"] = self.i(ins, f"ld{j}", 16)
                self.sec("sync", s0)

                for i in range(NT):
                    b = i % 2

                    def t1(t, i=i, b=b):
                        self.w(t, f"qf{b}", mk[P_ + f"ql{i}"])
                        if i == 0:
                            self.w(t, "V", mk[P_ + "kvc"])
                            self.w(t, "L", mk["Db_wl"])
                        if i >= 1:
                            self.w(t, "V", mk[P_ + f"cm{i-1}"])
                        for hj in range(NH):
                            h = (hj % 4) * 4 + hj // 4
                            lhsT = qfs[b][:, (h // 2) * 128:(h // 2 + 1) * 128]
                            lhsT = lhsT[(h % 2) * 64:(h % 2) * 64 + 64, :]
                            rhs = kv_sb[(h % 2) * 64:(h % 2) * 64 + 64,
                                        (h // 8) * 260 + ((h // 2) % 4) * 65:
                                        (h // 8) * 260 + ((h // 2) % 4) * 65 + 65]
                            ins = t.matmul(self.bank(h // 4, (h % 4) * 65, 65),
                                           lhsT, rhs,
                                           start=(hj < 4), stop=True,
                                           skip_group_check=True)
                        mk[P_ + f"hm{i}"] = self.i(ins, "P", 1)
                    self.sec("tensor", t1)

                    def a1(s, i=i):
                        self.w(s, "P", mk[P_ + f"hm{i}"])
                        if i >= 1:
                            self.w(s, "V", mk[P_ + f"cm{i-1}"])
                        for h in range(NH):
                            ins = s.activation(rd[:, h:h + 1],
                                               self.bank(h // 4, (h % 4) * 65 + 64, 1),
                                               AF.Identity, bias=self.cst[:, 1:2])
                            self.i(ins, "A", 1)
                        mk[P_ + f"rc{i}"] = self.N["A"]
                    self.sec("scalar", a1)

                    def v1(v, i=i):
                        self.w(v, "A", mk[P_ + f"rc{i}"])
                        self.vi(v, v.reciprocal(rd[:, 0:NH], rd[:, 0:NH]))
                        for h in range(NH):
                            m = self.vi(v, v.tensor_scalar(
                                cat[:, h * 64:(h + 1) * 64],
                                self.bank(h // 4, (h % 4) * 65, 64),
                                rd[:, h:h + 1], None, OP.mult), wait=False)
                        mk[P_ + f"cm{i}"] = m
                    self.sec("vector", v1)

                    def t2(t, i=i):
                        self.w(t, "V", mk[P_ + f"cm{i}"])
                        if i >= 1:
                            self.w(t, "V", mk[P_ + f"ctc{i-1}"])
                        for fc in range(DC):
                            ins = t.transpose(self.bank(4 + fc % 2, (fc // 2) * 128, 128),
                                              cat[:, fc * 128:(fc + 1) * 128],
                                              self.ident[:, :])
                        mk[P_ + f"ctr{i}"] = self.i(ins, "P", 1)
                    self.sec("tensor", t2)

                    def v2(v, i=i):
                        self.w(v, "P", mk[P_ + f"ctr{i}"])
                        for fc in range(DC):
                            m = self.vi(v, v.tensor_copy(catT[:, fc * 128:(fc + 1) * 128],
                                                         self.bank(4 + fc % 2,
                                                                   (fc // 2) * 128, 128)),
                                        wait=False)
                        mk[P_ + f"ctc{i}"] = m
                    self.sec("vector", v2)

                    def t3(t, i=i):
                        self.w(t, "V", mk[P_ + f"ctc{i}"])
                        if i >= 1:
                            self.w(t, "V", mk[P_ + f"x2v{i-1}"])
                        for half in range(2):
                            for fc in range(DC):
                                ins = t.matmul(
                                    self.bank(6 + half, 0, 512),
                                    catT[:, fc * 128:(fc + 1) * 128],
                                    wo_sb[:, fc * D + half * 512:fc * D + (half + 1) * 512],
                                    start=(fc == 0), stop=(fc == DC - 1))
                        mk[P_ + f"omm{i}"] = self.i(ins, "P", 1)
                    self.sec("tensor", t3)

                    def v3(v, i=i, b=b):
                        self.w(v, "P", mk[P_ + f"omm{i}"])
                        self.w(v, f"ld{b}", mk[P_ + f"x1l{i}"])
                        if i >= 2:
                            self.w(v, f"st{b}", mk[P_ + f"x2st{i-2}"])
                        mk[P_ + f"x2v{i}"] = self.vi(v, v.tensor_add(xa[b][:, :], x1t[b][:, :],
                                                                     self.psB[:, 1024:2048]),
                                                     wait=False)
                    self.sec("vector", v3)

                    def g1(g, i=i, b=b):
                        self.w(g, "V", mk[P_ + f"x2v{i}"])
                        ins = g.dma_start(out=self.x2_d[i * 128:(i + 1) * 128, :],
                                          in_=xa[b][:, :])
                        mk[P_ + f"x2st{i}"] = self.i(ins, f"st{b}", 16)
                    self.sec("gpsimd", g1)

                    if i + 2 < NT:
                        def s1(s, i=i, b=b):
                            self.w(s, "P", mk[P_ + f"hm{i}"])
                            ins = s.dma_start(
                                out=qfs[b][:, :].rearrange("p (a c) -> p a c", a=DC),
                                in_=self.qf_d[i + 2, :, :, :])
                            self.i(ins, f"qf{b}", 16)
                            mk[P_ + f"ql{i+2}"] = self.N[f"qf{b}"]
                            self.w(s, "V", mk[P_ + f"x2v{i}"])
                            ins = s.dma_start(out=x1t[b][:, :],
                                              in_=self.x1_d[(i + 2) * 128:(i + 3) * 128, :])
                            mk[P_ + f"x1l{i+2}"] = self.i(ins, f"ld{b}", 16)
                        self.sec("sync", s1)

    def build(self):
        nc = self.nc
        with ExitStack() as es:
            for s in _SEMS:
                self.S[s] = es.enter_context(nc.semaphore(s))
                self.N[s] = 0
            sb = lambda n, sh, dt=F32: es.enter_context(nc.sbuf_tensor(n, sh, dt))
            self.ident = sb("ident", [128, 128])
            self.iota8 = sb("iota8", [128, E])
            self.eights = sb("eights", [128, E])
            self.zeros_i = sb("zeros_i", [128, 128], I32)
            self.gamma_bc = sb("gamma_bc", [128, D])
            self.beta_bc = sb("beta_bc", [128, D])
            self.br_bc = sb("br_bc", [128, E])
            self.b1_rows = sb("b1_rows", [128, 256])
            self.b2_rows = sb("b2_rows", [64, 128])
            self.b1_sb = sb("b1_sb", [128, 256])
            self.b2_sb = sb("b2_sb", [128, 64])
            self.rb1_sb = sb("rb1_sb", [128, 256])
            self.wr_sb = sb("wr_sb", [128, DC * E])
            self.ctot_bc = sb("ctot_bc", [128, D])
            self.cst = sb("cst", [128, 4])
            self.UT = sb("UT", [128, 128])
            self.ones1 = sb("ones1", [1, 128])
            self.onesc = sb("onesc", [128, 1])
            self.iota84 = sb("iota84", [128, 32])
            self.eights4 = sb("eights4", [128, 32])
            self.psA = es.enter_context(nc.psum_tensor("psA", [128, 2048], F32))
            self.psB = es.enter_context(nc.psum_tensor("psB", [128, 2048], F32))

            self.block = es.enter_context(nc.Block())

            steps = [
                ("pro", lambda: self.prologue()),
                ("A", lambda: self.phase_A(0, self.x_in, with_b0=True)),
                ("B", lambda: self.phase_B(0)),
                ("C", lambda: self.phase_C(0, self.x_in, self.x1_d)),
                ("D", lambda: self.phase_D()),
                ("A2", lambda: self.phase_A(1, self.x2_d)),
                ("B2", lambda: self.phase_B(1)),
                ("C2", lambda: self.phase_C(1, self.x2_d, self.out)),
            ]
            for name, fn in steps:
                fn()
                self.barrier()
                if self.upto == name:
                    break
        return nc


_CACHE = {}


def build_program(debug=False, upto="all"):
    key = (bool(debug), upto)
    if key not in _CACHE:
        _CACHE[key] = MK(debug=debug, upto=upto).build()
    return _CACHE[key]


def kernel(L0_attn_x, L0_attn_Wq, L0_attn_Wk, L0_attn_Wv, L0_attn_Wo,
           L0_ffn_gamma, L0_ffn_beta, L0_ffn_Wr, L0_ffn_br,
           L0_ffn_W1, L0_ffn_b1, L0_ffn_W2, L0_ffn_b2):
    nc = build_program(debug=False)
    x = np.ascontiguousarray(np.asarray(L0_attn_x, dtype=np.float32))
    com = {
        "wq": np.ascontiguousarray(L0_attn_Wq, dtype=np.float32),
        "wk": np.ascontiguousarray(L0_attn_Wk, dtype=np.float32),
        "wv": np.ascontiguousarray(L0_attn_Wv, dtype=np.float32),
        "wo": np.ascontiguousarray(L0_attn_Wo, dtype=np.float32),
        "wr": np.ascontiguousarray(L0_ffn_Wr, dtype=np.float32),
        "gamma": np.ascontiguousarray(L0_ffn_gamma, dtype=np.float32),
        "beta": np.ascontiguousarray(L0_ffn_beta, dtype=np.float32),
        "brr": np.ascontiguousarray(L0_ffn_br, dtype=np.float32),
        "w1": np.ascontiguousarray(L0_ffn_W1, dtype=np.float32),
        "b1": np.ascontiguousarray(L0_ffn_b1, dtype=np.float32),
        "w2": np.ascontiguousarray(L0_ffn_W2, dtype=np.float32),
        "b2": np.ascontiguousarray(L0_ffn_b2, dtype=np.float32),
    }
    in_maps = [dict(com, x=x[b]) for b in range(8)]
    res = run_bass_kernel_spmd(nc, in_maps, core_ids=list(range(8)))
    return np.stack([np.asarray(res.results[b]["out"]) for b in range(8)], axis=0)


# version 17
# speedup vs baseline: 1.1399x; 1.1141x over previous
import os
import numpy as np
from contextlib import ExitStack
_NO_ACCUM = bool(os.environ.get('MK_NO_ACCUM'))
_B_NE = int(os.environ.get('MK_B_NE', '8'))
_B_NQ = int(os.environ.get('MK_B_NQ', '4'))
_B_GONLY = bool(os.environ.get('MK_B_GONLY'))
_B_MM1ONLY = bool(os.environ.get('MK_B_MM1ONLY'))
_B_NOTR = bool(os.environ.get('MK_B_NOTR'))
_B_NOSTORE = bool(os.environ.get('MK_B_NOSTORE'))

import concourse.bass as bass
import concourse.mybir as mybir
from concourse.bass_utils import run_bass_kernel_spmd

F32 = mybir.dt.float32
I32 = mybir.dt.int32
AX = mybir.AxisListType
OP = mybir.AluOpType
AF = mybir.ActivationFunctionType
IOA = bass.IndirectOffsetOnAxis

T = 4096
D = 1024
HID = 4096
E = 8
C = 1792
NGW = C // 4
SLOT_CH = C // 128
NT = T // 128
DC = D // 128
NH, DH = 16, 64
NSLOT = E * C
LN_EPS = 1e-5
ATT_EPS = 1e-6
BIG = 1e9

_SEMS = (["L", "G", "P", "V", "A", "Gc"]
         + [f"{p}{b}" for b in (0, 1) for p in
            ("xl", "w1", "w2", "ix", "xg", "as", "sc", "ld", "yg", "st", "ts", "qf")])


class MK:
    def __init__(self, debug=False, upto="all"):
        self.debug = debug
        self.upto = upto
        self.nc = nc = bass.Bass("TRN2", target_bir_lowering=False,
                                 detect_race_conditions=False)
        dbg = "ExternalOutput" if debug else "Internal"

        def dt_(name, shape, dtype=F32, kind="Internal"):
            return nc.dram_tensor(name, shape, dtype, kind=kind).ap()

        self.x_in = dt_("x", [T, D], kind="ExternalInput")
        self.wq = dt_("wq", [D, D], kind="ExternalInput")
        self.wk = dt_("wk", [D, D], kind="ExternalInput")
        self.wv = dt_("wv", [D, D], kind="ExternalInput")
        self.wo = dt_("wo", [D, D], kind="ExternalInput")
        self.wr = dt_("wr", [D, E], kind="ExternalInput")
        self.gamma = dt_("gamma", [D], kind="ExternalInput")
        self.beta = dt_("beta", [D], kind="ExternalInput")
        self.br = dt_("brr", [E], kind="ExternalInput")
        self.w1 = dt_("w1", [E, D, HID], kind="ExternalInput")
        self.b1 = dt_("b1", [E, HID], kind="ExternalInput")
        self.w2 = dt_("w2", [E, HID, D], kind="ExternalInput")
        self.b2 = dt_("b2", [E, D], kind="ExternalInput")
        self.out = dt_("out", [T, D], kind="ExternalOutput")

        self.v_d = dt_("v_d", [T, D], kind=dbg)
        self.xn_d = dt_("xn_d", [T, D], kind=dbg)
        self.pr_d = dt_("pr_d", [T, E], kind=dbg)
        self.eid_d = dt_("eid_d", [T, 4], I32, kind=dbg)
        self.adr_d = dt_("adr_d", [T, 4], I32, kind=dbg)
        self.sidx_d = dt_("sidx_d", [NSLOT, 1], I32, kind=dbg)
        self.y_d = dt_("y_d", [NSLOT, D], kind=dbg)
        self.c_d = dt_("c_d", [E, D], kind=dbg)
        self.ct_d = dt_("ct_d", [1, D], kind=dbg)
        self.x1_d = dt_("x1_d", [T, D], kind=dbg)
        self.x2_d = dt_("x2_d", [T, D], kind=dbg)
        self.qf_d = dt_("qf_d", [NT, 128, DC, 128], kind=dbg)

        self.S = {}
        self.N = {}
        self.mk = {}
        self._b_evac_tail = 0

    def w(self, eng, s, v):
        if v > 0:
            eng.wait_ge(self.S[s], v)

    def i(self, instr, s, n):
        self.N[s] += n
        instr.then_inc(self.S[s], n)
        return self.N[s]

    def sec(self, engname, fn):
        getattr(self.block, engname)(fn)

    def vi(self, v, ins, wait=True):
        m = self.i(ins, "V", 1)
        if wait:
            v.wait_ge(self.S["V"], m)
        return m

    def ai(self, s, ins, wait=False):
        m = self.i(ins, "A", 1)
        if wait:
            s.wait_ge(self.S["A"], m)
        return m

    def barrier(self):
        counts = {s: v for s, v in self.N.items() if v > 0}
        for eng in ("sync", "tensor", "vector", "scalar", "gpsimd"):
            def f(e, counts=counts):
                for s, v in counts.items():
                    e.wait_ge(self.S[s], v)
            self.sec(eng, f)

    def bank(self, k, lo=0, n=512):
        ps = self.psA if k < 4 else self.psB
        base = (k % 4) * 512
        return ps[:, base + lo: base + lo + n]

    def prologue(self):
        mk = self.mk

        def g(g):
            ins = g.memset(self.ident[:, :], 0.0)
            m = self.i(ins, "Gc", 1)
            g.wait_ge(self.S["Gc"], m)
            ins = g.affine_select(out=self.ident[:, :], in_=self.ident[:, :],
                                  compare_op=OP.not_equal, fill=1.0, base=0,
                                  pattern=[[-1, 128]], channel_multiplier=1)
            self.i(ins, "Gc", 1)
            ins = g.iota(self.iota8[:, :], pattern=[[1, E]], base=0,
                         channel_multiplier=0, allow_small_or_imprecise_dtypes=True)
            self.i(ins, "Gc", 1)
            ins = g.memset(self.eights[:, :], float(E))
            self.i(ins, "Gc", 1)
            ins = g.memset(self.zeros_i[:, :], 0)
            self.i(ins, "Gc", 1)
            ins = g.memset(self.cst[:, 0:1], LN_EPS)
            self.i(ins, "Gc", 1)
            ins = g.memset(self.cst[:, 1:2], ATT_EPS)
            self.i(ins, "Gc", 1)
            ins = g.memset(self.UT[:, :], 0.0)
            m = self.i(ins, "Gc", 1)
            g.wait_ge(self.S["Gc"], m)
            ins = g.affine_select(out=self.UT[:, :], in_=self.UT[:, :],
                                  compare_op=OP.is_ge, fill=1.0, base=0,
                                  pattern=[[-1, 128]], channel_multiplier=1)
            self.i(ins, "Gc", 1)
            ins = g.memset(self.ones1[:, :], 1.0)
            self.i(ins, "Gc", 1)
            ins = g.memset(self.onesc[:, :], 1.0)
            self.i(ins, "Gc", 1)
            ins = g.iota(self.iota84[:, :], pattern=[[0, 4], [1, E]], base=0,
                         channel_multiplier=0, allow_small_or_imprecise_dtypes=True)
            self.i(ins, "Gc", 1)
            ins = g.memset(self.eights4[:, :], float(E))
            self.i(ins, "Gc", 1)
            i1 = g.dma_start(out=self.gamma_bc[:, :],
                             in_=self.gamma.unsqueeze(0).broadcast_to((128, D)))
            self.i(i1, "G", 16)
            i2 = g.dma_start(out=self.beta_bc[:, :],
                             in_=self.beta.unsqueeze(0).broadcast_to((128, D)))
            self.i(i2, "G", 16)
            i3 = g.dma_start(out=self.br_bc[:, :],
                             in_=self.br.unsqueeze(0).broadcast_to((128, E)))
            self.i(i3, "G", 16)
            b1r = self.b1.rearrange("e (r p) -> (e r) p", p=128)
            i4 = g.dma_start(out=self.b1_rows[:, 0:128], in_=b1r[0:128, :])
            self.i(i4, "G", 16)
            i5 = g.dma_start(out=self.b1_rows[:, 128:256], in_=b1r[128:256, :])
            self.i(i5, "G", 16)
            i6 = g.dma_start(out=self.b2_rows[0:64, :],
                             in_=self.b2.rearrange("e (r p) -> (e r) p", p=128))
            self.i(i6, "G", 16)
            i7 = g.dma_start(out=self.wr_sb[:, :].rearrange("p (c e) -> p c e", c=DC),
                             in_=self.wr.rearrange("(c p) e -> p c e", p=128))
            mk["pro_g"] = self.i(i7, "G", 16)
        self.sec("gpsimd", g)

        def t(t):
            self.w(t, "G", mk["pro_g"])
            t.transpose(self.psA[:, 0:128], self.b1_rows[:, 0:128], self.ident[:, :])
            t.transpose(self.psA[:, 512:640], self.b1_rows[:, 128:256], self.ident[:, :])
            ins = t.transpose(self.psA[:, 1024:1088], self.b2_rows[0:64, :],
                              self.ident[0:64, 0:64])
            mk["pro_t"] = self.i(ins, "P", 1)
        self.sec("tensor", t)

        def v(v):
            self.w(v, "P", mk["pro_t"])
            v.tensor_copy(self.b1_sb[:, 0:128], self.psA[:, 0:128])
            v.tensor_copy(self.b1_sb[:, 128:256], self.psA[:, 512:640])
            ins = v.tensor_copy(self.b2_sb[:, 0:64], self.psA[:, 1024:1088])
            mk["pro_v"] = self.i(ins, "V", 1)
        self.sec("vector", v)

        def s(s):
            self.w(s, "V", mk["pro_v"])
            ins = s.activation(self.rb1_sb[:, :], self.b1_sb[:, :], AF.Relu)
            mk["pro_a"] = self.i(ins, "A", 1)
        self.sec("scalar", s)

    def phase_A(self, call, xin, with_b0=False):
        nc, mk = self.nc, self.mk
        P_ = f"A{call}_"
        BT = 4
        NG = NT // BT
        W = BT * E
        with ExitStack() as es:
            sb = lambda n, sh, dt=F32: es.enter_context(nc.sbuf_tensor(P_ + n, sh, dt))
            if with_b0:
                w2qh = [sb(f"b0w{b}", [128, 4 * D]) for b in range(2)]
                csb = sb("b0c", [128, E])

            def b0_mms(e):
                for q in range(4):
                    for hf in range(2):
                        li = e * 8 + q * 2 + hf

                        def ld(g, e=e, q=q, hf=hf, li=li):
                            if li >= 2:
                                self.w(g, "P", mk[f"b0c_{li-2}"])
                            ins = g.dma_start(
                                out=w2qh[li % 2][:, :]
                                    .rearrange("p (h c) -> p h c", h=4),
                                in_=self.w2[e, q * 1024 + hf * 512:
                                            q * 1024 + (hf + 1) * 512, :]
                                    .rearrange("(h p) c -> p h c", p=128))
                            mk[f"b0l_{li}"] = self.i(ins, f"w2{li % 2}", 16)
                        self.sec("sync", ld)

                        def mm(t, e=e, q=q, hf=hf, li=li):
                            if li == e * 8:
                                self.w(t, "A", mk["pro_a"] if e == 0
                                       else mk[f"b0ev_{e-1}"])
                            self.w(t, f"w2{li % 2}", mk[f"b0l_{li}"])
                            for oc in range(DC):
                                for h4 in range(4):
                                    hq = q * 8 + hf * 4 + h4
                                    ins = t.matmul(
                                        self.bank(2, oc, 1),
                                        w2qh[li % 2][:, h4 * D + oc * 128:
                                                     h4 * D + (oc + 1) * 128],
                                        self.rb1_sb[:, e * 32 + hq:
                                                    e * 32 + hq + 1],
                                        start=(q == 0 and hf == 0 and oc == 0
                                               and h4 == 0),
                                        stop=(q == 3 and hf == 1 and oc == DC - 1
                                              and h4 == 3),
                                        skip_group_check=True)
                            mk[f"b0c_{li}"] = self.i(ins, "P", 1)
                        self.sec("tensor", mm)
                mk[f"b0m_{e}"] = self.N["P"]

            def b0_evac(s, e):
                self.w(s, "P", mk[f"b0m_{e}"])
                if e >= 1:
                    self.w(s, "G", mk[f"b0st_{e-1}"])
                for oc in range(DC):
                    ins = s.activation(csb[:, oc:oc + 1], self.bank(2, oc, 1),
                                       AF.Identity,
                                       bias=self.b2_sb[:, e * 8 + oc:e * 8 + oc + 1])
                    self.i(ins, "A", 1)
                mk[f"b0ev_{e}"] = self.N["A"]

            def b0_store(g, e):
                self.w(g, "A", mk[f"b0ev_{e}"])
                for oc in range(DC):
                    ins = g.dma_start(out=self.c_d[e:e + 1, oc * 128:(oc + 1) * 128],
                                      in_=csb[:, oc:oc + 1])
                    self.i(ins, "G", 16)
                mk[f"b0st_{e}"] = self.N["G"]
            xt = [sb(f"xt{b}", [128, BT * D]) for b in range(2)]
            vt = [sb(f"vt{b}", [128, BT * D]) for b in range(2)]
            xnt = [sb(f"xnt{b}", [128, BT * D]) for b in range(2)]
            xnT = [sb(f"xnT{b}", [128, BT * D]) for b in range(2)]
            tmp = sb("tmp", [128, BT * D])
            sc4 = sb("sc4", [128, 16])
            lg = sb("lg", [128, W])
            prb = [sb(f"prb{b}", [128, W]) for b in range(2)]
            pw = sb("pw", [128, W])
            w8 = sb("w8", [128, W])
            w8i = sb("w8i", [128, W], I32)
            cnd = sb("cnd", [128, W])
            Yt = sb("Yt", [128, W])
            ejs = [sb(f"ejs{b}", [128, BT * 4]) for b in range(2)]
            eidt = [sb(f"eidt{b}", [128, BT * 4], I32) for b in range(2)]
            Pt = sb("Pt", [128, W])
            adrf = [sb(f"adrf{b}", [128, BT * 4]) for b in range(2)]
            adri = [sb(f"adri{b}", [128, BT * 4], I32) for b in range(2)]
            tok = [sb(f"tok{b}", [128, BT], I32) for b in range(2)]
            csrow = sb("csrow", [1, W])
            corr = sb("corr", [1, W])
            base = sb("base", [1, E])

            def bc(ap2d, inner):
                return ap2d.unsqueeze(2).broadcast_to((128, ap2d.shape[1], inner))

            def col4(t_, j):
                return t_[:, :].rearrange("p (a c) -> p a c", a=BT)[:, :, j]

            def col4k(t_, j):
                return t_[:, :].rearrange("p (a c) -> p a c", a=BT)[:, :, j:j + 1]

            def g0(g):
                ins = g.dma_start(
                    out=self.sidx_d.rearrange("(p a) one -> p (a one)", p=128),
                    in_=self.zeros_i[:, 0:NSLOT // 128])
                mk[P_ + "sz"] = self.i(ins, "G", 16)
            self.sec("gpsimd", g0)

            def v00(v):
                ins = v.memset(base[:, :], 0.0)
                mk[P_ + "b0"] = self.vi(v, ins)
            self.sec("vector", v00)

            def s0(s):
                for j in range(2):
                    ins = s.dma_start(
                        out=xt[j][:, :].rearrange("p (a c) -> p a c", a=BT),
                        in_=xin[j * 512:(j + 1) * 512, :]
                            .rearrange("(a p) c -> p a c", p=128))
                    mk[P_ + f"xl{j}"] = self.i(ins, f"xl{j}", 16)
            self.sec("sync", s0)

            for i in range(NG):
                b = i % 2
                x4 = lambda t_: t_[:, :].rearrange("p (a c) -> p a c", a=BT)

                if with_b0 and i >= 1:
                    def ab0(s, i=i):
                        b0_evac(s, i - 1)
                    self.sec("scalar", ab0)

                    def gb0s(g, i=i):
                        b0_store(g, i - 1)
                    self.sec("gpsimd", gb0s)

                def v1(v, i=i, b=b):
                    self.w(v, f"xl{b}", mk[P_ + f"xl{i}"])
                    if i >= 2:
                        self.w(v, f"as{b}", mk[P_ + f"as{i-2}"])
                    self.vi(v, v.tensor_reduce(sc4[:, 0:BT], x4(xt[b]), AX.X, OP.add))
                    self.vi(v, v.tensor_scalar(sc4[:, 0:BT], sc4[:, 0:BT], 1.0 / D,
                                               None, OP.mult))
                    self.vi(v, v.tensor_tensor(x4(vt[b]), x4(xt[b]),
                                               bc(sc4[:, 0:BT], D), OP.subtract),
                            wait=False)
                    self.vi(v, v.tensor_tensor(tmp[:, :], vt[b][:, :], vt[b][:, :],
                                               OP.mult), wait=False)
                    mk[P_ + f"ln1{i}"] = self.vi(
                        v, v.tensor_reduce(sc4[:, 4:4 + BT], x4(tmp), AX.X, OP.add),
                        wait=False)
                self.sec("vector", v1)

                def a1(s, i=i):
                    self.w(s, "V", mk[P_ + f"ln1{i}"])
                    ins = s.activation(sc4[:, 4:4 + BT], sc4[:, 4:4 + BT], AF.Sqrt,
                                       bias=self.cst[:, 0:1], scale=1.0 / D)
                    mk[P_ + f"rs1{i}"] = self.i(ins, "A", 1)
                self.sec("scalar", a1)

                def v2(v, i=i, b=b):
                    self.w(v, "A", mk[P_ + f"rs1{i}"])
                    self.vi(v, v.reciprocal(sc4[:, 4:4 + BT], sc4[:, 4:4 + BT]))
                    self.vi(v, v.tensor_tensor(x4(vt[b]), x4(vt[b]),
                                               bc(sc4[:, 4:4 + BT], D), OP.mult),
                            wait=False)
                    self.vi(v, v.tensor_tensor(sc4[:, 12:12 + BT], sc4[:, 4:4 + BT],
                                               sc4[:, 4:4 + BT], OP.mult))
                    mk[P_ + f"ln2{i}"] = self.vi(
                        v, v.tensor_scalar(sc4[:, 12:12 + BT], sc4[:, 12:12 + BT],
                                           -LN_EPS, 1.0, OP.mult, OP.add))
                self.sec("vector", v2)

                def a2(s, i=i):
                    self.w(s, "V", mk[P_ + f"ln2{i}"])
                    ins = s.activation(sc4[:, 12:12 + BT], sc4[:, 12:12 + BT], AF.Sqrt,
                                       bias=self.cst[:, 0:1], scale=1.0)
                    mk[P_ + f"rs2{i}"] = self.i(ins, "A", 1)
                self.sec("scalar", a2)

                def v3(v, i=i, b=b):
                    self.w(v, "A", mk[P_ + f"rs2{i}"])
                    self.vi(v, v.reciprocal(sc4[:, 12:12 + BT], sc4[:, 12:12 + BT]))
                    self.vi(v, v.tensor_tensor(x4(xnt[b]), x4(vt[b]),
                                               bc(sc4[:, 12:12 + BT], D), OP.mult),
                            wait=False)
                    gam = self.gamma_bc[:, :].unsqueeze(1).broadcast_to((128, BT, D))
                    bet = self.beta_bc[:, :].unsqueeze(1).broadcast_to((128, BT, D))
                    self.vi(v, v.tensor_tensor(x4(xnt[b]), x4(xnt[b]), gam, OP.mult),
                            wait=False)
                    mk[P_ + f"xn{i}"] = self.vi(
                        v, v.tensor_tensor(x4(xnt[b]), x4(xnt[b]), bet, OP.add),
                        wait=False)
                self.sec("vector", v3)

                for st in range(BT):
                    def t1(t, i=i, b=b, st=st):
                        if st == 0:
                            self.w(t, "V", mk[P_ + f"xn{i}"])
                            if i >= 1:
                                self.w(t, "A", mk[P_ + f"xnTc{i-1}_{BT-1}"])
                        else:
                            self.w(t, "A", mk[P_ + f"xnTc{i}_{st-1}"])
                        for fc in range(DC):
                            k = 0 if fc % 2 == 0 else 4
                            ins = t.transpose(self.bank(k, (fc // 2) * 128, 128),
                                              xnt[b][:, (st * DC + fc) * 128:
                                                     (st * DC + fc + 1) * 128],
                                              self.ident[:, :])
                        mk[P_ + f"xtr{i}_{st}"] = self.i(ins, "P", 1)
                    self.sec("tensor", t1)

                    def a4(s, i=i, b=b, st=st):
                        self.w(s, "P", mk[P_ + f"xtr{i}_{st}"])
                        for fc in range(DC):
                            k = 0 if fc % 2 == 0 else 4
                            ins = s.activation(
                                xnT[b][:, (st * DC + fc) * 128:(st * DC + fc + 1) * 128],
                                self.bank(k, (fc // 2) * 128, 128), AF.Identity)
                            m = self.i(ins, "A", 1)
                        mk[P_ + f"xnTc{i}_{st}"] = m
                    self.sec("scalar", a4)

                def t2(t, i=i, b=b):
                    self.w(t, "A", mk[P_ + f"xnTc{i}_{BT-1}"])
                    if i >= 1:
                        self.w(t, "V", mk[P_ + f"lg{i-1}"])
                    for st in range(BT):
                        for fc in range(DC):
                            ins = t.matmul(self.bank(5, st * E, E),
                                           xnT[b][:, (st * DC + fc) * 128:
                                                  (st * DC + fc + 1) * 128],
                                           self.wr_sb[:, fc * E:(fc + 1) * E],
                                           start=(fc == 0), stop=(fc == DC - 1),
                                           skip_group_check=True)
                    mk[P_ + f"rmm{i}"] = self.i(ins, "P", 1)
                self.sec("tensor", t2)

                def v5(v, i=i, b=b):
                    self.w(v, "P", mk[P_ + f"rmm{i}"])
                    brb = self.br_bc[:, :].unsqueeze(1).broadcast_to((128, BT, E))
                    lg3 = lg[:, :].rearrange("p (a c) -> p a c", a=BT)
                    mk[P_ + f"lg{i}"] = self.vi(
                        v, v.tensor_tensor(lg3, self.bank(5, 0, W)
                                           .rearrange("p (a c) -> p a c", a=BT),
                                           brb, OP.add))
                    self.vi(v, v.reduce_max(sc4[:, 0:BT], lg3, AX.X))
                    self.vi(v, v.tensor_tensor(lg3, lg3, bc(sc4[:, 0:BT], E),
                                               OP.subtract))
                    ins = v.engine_nop()
                    mk[P_ + f"nm{i}"] = self.i(ins, "V", 1)
                self.sec("vector", v5)

                def a3(s, i=i, b=b):
                    self.w(s, "V", mk[P_ + f"nm{i}"])
                    if i >= 2:
                        self.w(s, f"as{b}", mk[P_ + f"as{i-2}"])
                    ins = s.activation(prb[b][:, :], lg[:, :], AF.Exp)
                    mk[P_ + f"ex{i}"] = self.i(ins, "A", 1)
                self.sec("scalar", a3)

                def v6(v, i=i, b=b):
                    self.w(v, "A", mk[P_ + f"ex{i}"])
                    p3 = prb[b][:, :].rearrange("p (a c) -> p a c", a=BT)
                    self.vi(v, v.tensor_reduce(sc4[:, 0:BT], p3, AX.X, OP.add))
                    self.vi(v, v.reciprocal(sc4[:, 0:BT], sc4[:, 0:BT]))
                    self.vi(v, v.tensor_tensor(p3, p3, bc(sc4[:, 0:BT], E), OP.mult))
                    self.vi(v, v.tensor_copy(pw[:, :], prb[b][:, :]))
                    pw3 = pw[:, :].rearrange("p (a c) -> p a c", a=BT)
                    w83 = w8[:, :].rearrange("p (a c) -> p a c", a=BT)
                    cn3 = cnd[:, :].rearrange("p (a c) -> p a c", a=BT)
                    for j in range(3):
                        self.vi(v, v.reduce_max(sc4[:, 4:4 + BT], pw3, AX.X))
                        self.vi(v, v.tensor_tensor(
                            w8i[:, :].rearrange("p (a c) -> p a c", a=BT), pw3,
                            bc(sc4[:, 4:4 + BT], E), OP.is_equal))
                        self.vi(v, v.select(cnd[:, :], w8i[:, :], self.iota84[:, :],
                                            self.eights4[:, :]))
                        self.vi(v, v.tensor_reduce(col4k(ejs[b], j), cn3, AX.X,
                                                   OP.min))
                        ejcol = col4(ejs[b], j)
                        self.vi(v, v.tensor_tensor(w83, self.iota84[:, :]
                                                   .rearrange("p (a c) -> p a c", a=BT),
                                                   bc(ejcol, E), OP.is_equal))
                        if j == 0:
                            self.vi(v, v.tensor_copy(Yt[:, :], w8[:, :]))
                        else:
                            self.vi(v, v.tensor_add(Yt[:, :], Yt[:, :], w8[:, :]))
                        self.vi(v, v.scalar_tensor_tensor(pw[:, :], w8[:, :], -BIG,
                                                          pw[:, :], OP.mult, OP.add))
                    self.vi(v, v.memset(col4(ejs[b], 3), 0.0))
                    mk[P_ + f"top{i}"] = self.vi(
                        v, v.tensor_copy(eidt[b][:, :], ejs[b][:, :]), wait=False)
                self.sec("vector", v6)

                def t3(t, i=i, b=b):
                    self.w(t, "V", mk[P_ + f"top{i}"])
                    if i >= 1:
                        self.w(t, "V", mk[P_ + f"csc{i-1}"])
                    for st in range(BT):
                        t.matmul(self.bank(7, st * E, E), self.UT[:, :],
                                 Yt[:, st * E:(st + 1) * E],
                                 start=(st == 0), stop=False, skip_group_check=True)
                    ins = t.matmul(self.bank(6, 0, W)[0:1, :], self.onesc[:, :],
                                   Yt[:, :], start=True, stop=True,
                                   skip_group_check=True)
                    mk[P_ + f"cs{i}"] = self.i(ins, "P", 1)
                self.sec("tensor", t3)

                def v7(v, i=i):
                    self.w(v, "P", mk[P_ + f"cs{i}"])
                    self.vi(v, v.tensor_copy(csrow[:, :], self.bank(6, 0, W)[0:1, :]))
                    self.vi(v, v.tensor_copy(corr[0:1, 0:E], base[0:1, :]))
                    for st in range(1, BT):
                        self.vi(v, v.tensor_add(
                            corr[0:1, st * E:(st + 1) * E],
                            corr[0:1, (st - 1) * E:st * E],
                            csrow[0:1, (st - 1) * E:st * E]))
                    self.vi(v, v.tensor_add(base[0:1, :],
                                            corr[0:1, (BT - 1) * E:BT * E],
                                            csrow[0:1, (BT - 1) * E:BT * E]))
                    ins = v.engine_nop()
                    mk[P_ + f"csc{i}"] = self.i(ins, "V", 1)
                self.sec("vector", v7)

                def t4(t, i=i):
                    self.w(t, "V", mk[P_ + f"csc{i}"])
                    ins = t.matmul(self.bank(7, 0, W), self.ones1[0:1, 0:128],
                                   corr[0:1, :], start=False, stop=True,
                                   skip_group_check=True)
                    mk[P_ + f"pb{i}"] = self.i(ins, "P", 1)
                self.sec("tensor", t4)

                if with_b0 and i < E:
                    b0_mms(i)

                def v8(v, i=i, b=b):
                    self.w(v, "P", mk[P_ + f"pb{i}"])
                    if i >= 2:
                        self.w(v, f"sc{b}", mk[P_ + f"sct{i-2}"])
                    self.vi(v, v.tensor_copy(Pt[:, :], self.bank(7, 0, W)))
                    pt3 = Pt[:, :].rearrange("p (a c) -> p a c", a=BT)
                    w83 = w8[:, :].rearrange("p (a c) -> p a c", a=BT)
                    for j in range(3):
                        ejcol = col4(ejs[b], j)
                        adcol = col4(adrf[b], j)
                        self.vi(v, v.tensor_tensor(w83, self.iota84[:, :]
                                                   .rearrange("p (a c) -> p a c", a=BT),
                                                   bc(ejcol, E), OP.is_equal))
                        self.vi(v, v.tensor_tensor(w83, w83, pt3, OP.mult))
                        self.vi(v, v.tensor_reduce(col4k(adrf[b], j), w83, AX.X,
                                                   OP.add))
                        self.vi(v, v.tensor_scalar(adcol, adcol, float(C - 1), None,
                                                   OP.min))
                        self.vi(v, v.tensor_scalar(tmp[:, 0:BT], ejcol, float(C),
                                                   None, OP.mult))
                        self.vi(v, v.tensor_tensor(adcol, adcol, tmp[:, 0:BT], OP.add))
                    self.vi(v, v.memset(col4(adrf[b], 3), 0.0))
                    mk[P_ + f"p2v{i}"] = self.vi(
                        v, v.tensor_copy(adri[b][:, :], adrf[b][:, :]), wait=False)
                self.sec("vector", v8)

                def g1(g, i=i, b=b):
                    self.w(g, "V", mk[P_ + f"p2v{i}"])
                    if i == 0:
                        self.w(g, "G", mk[P_ + "sz"])
                    if i >= 2:
                        self.w(g, f"sc{b}", mk[P_ + f"sct{i-2}"])
                    ins = g.iota(tok[b][:, :], pattern=[[128, BT]], base=i * 512,
                                 channel_multiplier=1)
                    m = self.i(ins, "Gc", 1)
                    g.wait_ge(self.S["Gc"], m)
                    rows = slice(i * 512, (i + 1) * 512)
                    ins = g.dma_start(out=self.v_d[rows, :]
                                      .rearrange("(a p) c -> p a c", p=128),
                                      in_=vt[b][:, :].rearrange("p (a c) -> p a c", a=BT))
                    self.i(ins, f"as{b}", 16)
                    ins = g.dma_start(out=self.xn_d[rows, :]
                                      .rearrange("(a p) c -> p a c", p=128),
                                      in_=xnt[b][:, :].rearrange("p (a c) -> p a c", a=BT))
                    self.i(ins, f"as{b}", 16)
                    ins = g.dma_start(out=self.pr_d[rows, :]
                                      .rearrange("(a p) c -> p a c", p=128),
                                      in_=prb[b][:, :].rearrange("p (a c) -> p a c", a=BT))
                    self.i(ins, f"as{b}", 16)
                    ins = g.dma_start(out=self.eid_d[rows, :]
                                      .rearrange("(a p) c -> p a c", p=128),
                                      in_=eidt[b][:, :].rearrange("p (a c) -> p a c", a=BT))
                    mk[P_ + f"as{i}"] = self.i(ins, f"as{b}", 16)
                    ins = g.dma_start(out=self.adr_d[rows, :]
                                      .rearrange("(a p) c -> p a c", p=128),
                                      in_=adri[b][:, :].rearrange("p (a c) -> p a c", a=BT))
                    self.i(ins, f"sc{b}", 16)
                    for st in range(BT):
                        for j in range(3):
                            ins = g.indirect_dma_start(
                                out=self.sidx_d[:, :],
                                out_offset=IOA(adri[b][:, st * 4 + j:st * 4 + j + 1], 0),
                                in_=tok[b][:, st:st + 1], in_offset=None)
                            self.i(ins, f"sc{b}", 16)
                    mk[P_ + f"sct{i}"] = self.N[f"sc{b}"]
                self.sec("gpsimd", g1)

                if i + 2 < NG:
                    def s1(s, i=i, b=b):
                        self.w(s, "V", mk[P_ + f"ln1{i}"])
                        ins = s.dma_start(
                            out=xt[b][:, :].rearrange("p (a c) -> p a c", a=BT),
                            in_=xin[(i + 2) * 512:(i + 3) * 512, :]
                                .rearrange("(a p) c -> p a c", p=128))
                        mk[P_ + f"xl{i+2}"] = self.i(ins, f"xl{b}", 16)
                    self.sec("sync", s1)

            if with_b0:
                def ab0f(s):
                    b0_evac(s, E - 1)
                self.sec("scalar", ab0f)

                def gb0f(g):
                    b0_store(g, E - 1)
                self.sec("gpsimd", gb0f)

    def phase_B0(self):
        nc, mk = self.nc, self.mk
        with ExitStack() as es:
            w2q = [es.enter_context(nc.sbuf_tensor(f"B0w2q{b}", [128, E * D], F32))
                   for b in range(2)]
            csb = es.enter_context(nc.sbuf_tensor("B0csb", [128, E], F32))
            cacc = es.enter_context(nc.sbuf_tensor("B0cacc", [128, E], F32))
            NQ = 4
            for e in range(E):
                for q in range(NQ):
                    b = (e * NQ + q) % 2

                    def s0(s, e=e, q=q, b=b):
                        if e * NQ + q >= 2:
                            self.w(s, "P", mk[f"B0mm{e*NQ+q-2}"])
                        ins = s.dma_start(
                            out=w2q[b][:, :].rearrange("p (h c) -> p h c", h=8),
                            in_=self.w2[e, q * 1024:(q + 1) * 1024, :]
                                .rearrange("(h p) c -> p h c", p=128))
                        mk[f"B0l{e*NQ+q}"] = self.i(ins, f"w2{b}", 16)
                    self.sec("sync", s0)

                    def t0(t, e=e, q=q, b=b):
                        self.w(t, f"w2{b}", mk[f"B0l{e*NQ+q}"])
                        if e == 0 and q == 0:
                            self.w(t, "A", mk["pro_a"])
                        if q == 0 and e > 0:
                            self.w(t, "V", mk[f"B0v{e-1}"])
                        for oc in range(DC):
                            for hc in range(8):
                                ins = t.matmul(
                                    self.bank(2, oc, 1),
                                    w2q[b][:, hc * D + oc * 128:hc * D + (oc + 1) * 128],
                                    self.rb1_sb[:, e * 32 + q * 8 + hc:
                                                e * 32 + q * 8 + hc + 1],
                                    start=(q == 0 and oc == 0 and hc == 0),
                                    stop=(q == NQ - 1 and oc == DC - 1 and hc == 7),
                                    skip_group_check=True)
                        mk[f"B0mm{e*NQ+q}"] = self.i(ins, "P", 1)
                    self.sec("tensor", t0)

                def v0(v, e=e):
                    self.w(v, "P", mk[f"B0mm{e*NQ+NQ-1}"])
                    for oc in range(DC):
                        m = self.vi(v, v.tensor_tensor(csb[:, oc:oc + 1], self.bank(2, oc, 1),
                                                       self.b2_sb[:, e * 8 + oc:e * 8 + oc + 1],
                                                       OP.add), wait=False)
                    v.wait_ge(self.S["V"], m)
                    if e == 0:
                        ins = v.tensor_copy(cacc[:, :], csb[:, :])
                    else:
                        ins = v.tensor_add(cacc[:, :], cacc[:, :], csb[:, :])
                    mk[f"B0v{e}"] = self.vi(v, ins)
                self.sec("vector", v0)

                def g0(g, e=e):
                    self.w(g, "V", mk[f"B0v{e}"])
                    for oc in range(DC):
                        ins = g.dma_start(out=self.c_d[e:e + 1, oc * 128:(oc + 1) * 128],
                                          in_=csb[:, oc:oc + 1])
                        self.i(ins, "G", 16)
                    mk[f"B0st{e}"] = self.N["G"]
                self.sec("gpsimd", g0)

            def gf(g):
                self.w(g, "V", mk[f"B0v{E-1}"])
                self.w(g, "G", mk[f"B0st{E-1}"])
                for oc in range(DC):
                    ins = g.dma_start(out=self.ct_d[0:1, oc * 128:(oc + 1) * 128],
                                      in_=cacc[:, oc:oc + 1])
                    self.i(ins, "G", 16)
                g.wait_ge(self.S["G"], self.N["G"])
                ins = g.dma_start(out=self.ctot_bc[:, :],
                                  in_=self.ct_d.broadcast_to((128, D)))
                mk["B0done"] = self.i(ins, "G", 16)
            self.sec("gpsimd", gf)

    def phase_B(self, call):
        nc, mk = self.nc, self.mk
        P_ = f"B{call}_"
        with ExitStack() as es:
            sb = lambda n, sh, dt=F32: es.enter_context(nc.sbuf_tensor(P_ + n, sh, dt))
            xnT = sb("xnT", [128, DC * C])
            hs = sb("hs", [128, 8 * C])
            w1b = [sb(f"w1b{b}", [128, 1024]) for b in range(2)]
            w2h = [sb(f"w2h{b}", [128, 8 * 512]) for b in range(2)]
            xg = [sb(f"xg{b}", [128, D]) for b in range(2)]
            ixb = [sb(f"ixb{b}", [128, 1], I32) for b in range(2)]
            yst = [sb(f"yst{b}", [128, 512]) for b in range(2)]
            b2bc = sb("b2bc", [128, D])

            self._b_evac_tail = 0
            for e in range(min(E, _B_NE)):
                EP = f"{P_}{e}_"

                def gb2(g, e=e):
                    if e > 0:
                        self.w(g, "V", mk[P_ + f"b2done{e-1}"])
                    ins = g.dma_start(out=b2bc[:, :],
                                      in_=self.b2[e:e + 1, :].broadcast_to((128, D)))
                    mk[P_ + f"b2l{e}"] = self.i(ins, "G", 16)
                self.sec("gpsimd", gb2)
                for sc in range(SLOT_CH):
                    b = sc % 2
                    gi = e * SLOT_CH + sc

                    def gg(g, sc=sc, b=b, gi=gi, e=e):
                        if gi >= 2:
                            self.w(g, "P", mk[P_ + f"gxtr{gi-2}"])
                            self.w(g, f"xg{b}", mk[P_ + f"gxl{gi-2}"])
                        ins = g.dma_start(
                            out=ixb[b][:, :],
                            in_=self.sidx_d[e * C + sc * 128:e * C + (sc + 1) * 128, :])
                        ixm = self.i(ins, f"ix{b}", 16)
                        g.wait_ge(self.S[f"ix{b}"], ixm)
                        ins = g.indirect_dma_start(
                            out=xg[b][:, :], out_offset=None,
                            in_=self.xn_d[:, :], in_offset=IOA(ixb[b][:, :], 0))
                        mk[P_ + f"gxl{gi}"] = self.i(ins, f"xg{b}", 16)
                    self.sec("gpsimd", gg)

                    def tt(t, sc=sc, b=b, gi=gi):
                        self.w(t, f"xg{b}", mk[P_ + f"gxl{gi}"])
                        if sc == 0:
                            self.w(t, "V", self._b_evac_tail)
                        if gi >= 1:
                            self.w(t, "V", mk[P_ + f"gxc{gi-1}"])
                        for fc in range(DC):
                            k = 0 if fc % 2 == 0 else 4
                            ins = t.transpose(self.bank(k, (fc // 2) * 128, 128),
                                              xg[b][:, fc * 128:(fc + 1) * 128],
                                              self.ident[:, :])
                        mk[P_ + f"gxtr{gi}"] = self.i(ins, "P", 1)
                    self.sec("tensor", tt)

                    def vv(v, sc=sc, gi=gi):
                        self.w(v, "P", mk[P_ + f"gxtr{gi}"])
                        for fc in range(DC):
                            k = 0 if fc % 2 == 0 else 4
                            m = self.vi(v, v.tensor_copy(
                                xnT[:, fc * C + sc * 128:fc * C + (sc + 1) * 128],
                                self.bank(k, (fc // 2) * 128, 128)), wait=False)
                        mk[P_ + f"gxc{gi}"] = m
                    self.sec("vector", vv)

                for q in range(0 if _B_GONLY else min(4, _B_NQ)):
                    QP = f"{EP}q{q}_"
                    ts_q0, ts_q1 = self.N["ts0"], self.N["ts1"]

                    for hc in range(8):
                        hq = q * 8 + hc
                        b = hc % 2
                        grp = 0 if hc % 2 == 0 else 4
                        wix = (call * 8 + e) * 32 + hq

                        def sw1(s, e=e, hq=hq, b=b, wix=wix):
                            if wix >= 2:
                                self.w(s, "P", mk[f"mm1g_{wix-2}"])
                            ins = s.dma_start(
                                out=w1b[b][:, :].rearrange("p (a c) -> p a c", a=DC),
                                in_=self.w1[e, :, hq * 128:(hq + 1) * 128]
                                    .rearrange("(a p) c -> p a c", p=128))
                            mk[f"w1l{wix}"] = self.i(ins, f"w1{b}", 16)
                        self.sec("sync", sw1)

                        def tm1(t, e=e, q=q, hc=hc, b=b, hq=hq, grp=grp, wix=wix):
                            self.w(t, f"w1{b}", mk[f"w1l{wix}"])
                            if hc == 0:
                                self.w(t, "V", mk[P_ + f"gxc{e*SLOT_CH+SLOT_CH-1}"])
                            if hc < 2:
                                self.w(t, "V", self._b_evac_tail)
                            if wix >= 2:
                                self.w(t, "A", mk[f"ev1_{wix-2}"])
                            for fc in range(DC):
                                for Ng in range(4):
                                    ins = t.matmul(
                                        self.bank(grp + Ng, 0, NGW),
                                        w1b[b][:, fc * 128:(fc + 1) * 128],
                                        xnT[:, fc * C + Ng * NGW:fc * C + (Ng + 1) * NGW],
                                        start=(fc == 0), stop=(fc == DC - 1),
                                        skip_group_check=True)
                            mk[f"mm1g_{wix}"] = self.i(ins, "P", 1)
                        self.sec("tensor", tm1)

                        def am1(s, e=e, hc=hc, hq=hq, grp=grp, wix=wix):
                            self.w(s, "P", mk[f"mm1g_{wix}"])
                            for Ng in range(4):
                                ins = s.activation(
                                    hs[:, hc * C + Ng * NGW:hc * C + (Ng + 1) * NGW],
                                    self.bank(grp + Ng, 0, NGW), AF.Relu,
                                    bias=self.b1_sb[:, e * 32 + hq:e * 32 + hq + 1])
                                m = self.i(ins, "A", 1)
                            mk[f"ev1_{wix}"] = m
                        self.sec("scalar", am1)

                    for half in range(2):
                        widx = ((call * 8 + e) * 4 + q) * 2 + half
                        wb = widx % 2

                        def sw2(s, e=e, q=q, half=half, widx=widx, wb=wb):
                            if widx >= 2:
                                self.w(s, "P", mk[f"mm2w_{widx-2}"])
                            for hc in range(8):
                                ins = s.dma_start(
                                    out=w2h[wb][:, hc * 512:(hc + 1) * 512],
                                    in_=self.w2[e, q * 1024 + hc * 128:
                                                q * 1024 + (hc + 1) * 128,
                                                half * 512:(half + 1) * 512])
                                self.i(ins, f"w2{wb}", 16)
                            mk[f"w2l{widx}"] = self.N[f"w2{wb}"]
                        self.sec("sync", sw2)

                        for sblock in range(SLOT_CH):
                            gidx = ((call * 8 + e) * 4 + q) * 2 * SLOT_CH \
                                + half * SLOT_CH + sblock
                            bk = gidx % 8
                            tsb = gidx % 2

                            def tm2(t, e=e, q=q, half=half, sblock=sblock,
                                    widx=widx, wb=wb, gidx=gidx, bk=bk):
                                if half == 0 and sblock == 0:
                                    self.w(t, "A", mk[f"ev1_{(call*8+e)*32 + q*8 + 7}"])
                                if sblock == 0:
                                    self.w(t, f"w2{wb}", mk[f"w2l{widx}"])
                                pm = mk.get(f"yev_{gidx-8}", 0)
                                self.w(t, "V", pm)
                                for hc in range(8):
                                    ins = t.matmul(
                                        self.bank(bk, 0, 512),
                                        hs[:, hc * C + sblock * 128:
                                           hc * C + (sblock + 1) * 128],
                                        w2h[wb][:, hc * 512:(hc + 1) * 512],
                                        start=(hc == 0), stop=(hc == 7),
                                        skip_group_check=True)
                                m = self.i(ins, "P", 1)
                                mk[f"mm2n_{gidx}"] = m
                                if sblock == SLOT_CH - 1:
                                    mk[f"mm2w_{widx}"] = m
                            self.sec("tensor", tm2)

                            def ve2(v, e=e, q=q, half=half, sblock=sblock,
                                    gidx=gidx, bk=bk, tsb=tsb):
                                self.w(v, "P", mk[f"mm2n_{gidx}"])
                                self.w(v, f"ts{tsb}", self.N[f"ts{tsb}"])
                                if q == 0 and half == 0 and sblock == 0:
                                    self.w(v, "G", mk[P_ + f"b2l{e}"])
                                if q == 0:
                                    ins = v.tensor_add(yst[tsb][:, :],
                                                       self.bank(bk, 0, 512),
                                                       b2bc[:, half * 512:(half + 1) * 512])
                                else:
                                    ins = v.tensor_copy(yst[tsb][:, :],
                                                        self.bank(bk, 0, 512))
                                m = self.vi(v, ins, wait=False)
                                mk[f"yev_{gidx}"] = m
                                self._b_evac_tail = m
                                if q == 0 and half == 1 and sblock == SLOT_CH - 1:
                                    mk[P_ + f"b2done{e}"] = m
                            self.sec("vector", ve2)

                            def gy(g, e=e, q=q, half=half, sblock=sblock,
                                   gidx=gidx, tsb=tsb, ts_q0=ts_q0, ts_q1=ts_q1):
                                if _B_NOSTORE:
                                    return
                                self.w(g, "V", mk[f"yev_{gidx}"])
                                if q > 0 and half == 0 and sblock == 0:
                                    self.w(g, "ts0", ts_q0)
                                    self.w(g, "ts1", ts_q1)
                                dst = self.y_d[e * C + sblock * 128:
                                               e * C + (sblock + 1) * 128,
                                               half * 512:(half + 1) * 512]
                                ins = g.dma_start(
                                    out=dst, in_=yst[tsb][:, :],
                                    accum_op=(OP.bypass if (q == 0 or _NO_ACCUM)
                                              else OP.add))
                                self.i(ins, f"ts{tsb}", 16)
                            self.sec("gpsimd", gy)

    def phase_C(self, call, xin, xout):
        nc, mk = self.nc, self.mk
        P_ = f"C{call}_"
        BT = 2
        NG = NT // BT
        TW = BT * 128
        with ExitStack() as es:
            sb = lambda n, sh, dt=F32: es.enter_context(nc.sbuf_tensor(P_ + n, sh, dt))
            xt = [sb(f"xt{b}", [128, BT * D]) for b in range(2)]
            vt2 = [sb(f"vt{b}", [128, BT * D]) for b in range(2)]
            prt = [sb(f"prt{b}", [128, BT * E]) for b in range(2)]
            eit = [sb(f"eit{b}", [128, BT * 4], I32) for b in range(2)]
            adt = [sb(f"adt{b}", [128, BT * 4], I32) for b in range(2)]
            ygs = [[sb(f"yg{j}{b}", [128, BT * D]) for b in range(2)] for j in range(3)]
            xo = [sb(f"xo{b}", [128, BT * D]) for b in range(2)]
            ejf = sb("ejf", [128, BT * 4])
            w8 = sb("w8", [128, BT * E])
            uw = [sb(f"uw{b}", [128, BT * E]) for b in range(2)]
            pjs = sb("pjs", [128, BT * 4])
            scw = sb("scw", [128, BT * 8])
            moe = sb("moe", [128, BT * D])
            tmd = sb("tmd", [128, BT * D])
            tm2 = sb("tm2", [128, BT * D])
            cS = sb("cS", [8, D])
            uwT = sb("uwT", [8, 2 * 128])

            def bc(ap2d, inner):
                return ap2d.unsqueeze(2).broadcast_to((128, ap2d.shape[1], inner))

            def colk(t_, k, j):
                return t_[:, :].rearrange("p (a c) -> p a c", a=BT)[:, :, j]

            def s00(s):
                ins = s.dma_start(out=cS[:, :], in_=self.c_d[:, :])
                mk[P_ + "cl"] = self.i(ins, "L", 16)
            self.sec("sync", s00)

            for i in range(NG):
                b = i % 2
                rows = slice(i * TW, (i + 1) * TW)

                def s0(s, i=i, b=b, rows=rows):
                    if i >= 2:
                        self.w(s, "V", mk[P_ + f"cmb{i-2}"])
                    for o_, i_ in ((prt[b], self.pr_d[rows, :]),
                                   (eit[b], self.eid_d[rows, :]),
                                   (adt[b], self.adr_d[rows, :]),
                                   (xt[b], xin[rows, :]),
                                   (vt2[b], self.v_d[rows, :])):
                        ins = s.dma_start(
                            out=o_[:, :].rearrange("p (a c) -> p a c", a=BT),
                            in_=i_.rearrange("(a p) c -> p a c", p=128))
                        self.i(ins, f"ld{b}", 16)
                    mk[P_ + f"ld{i}"] = self.N[f"ld{b}"]
                self.sec("sync", s0)

                def g0(g, i=i, b=b):
                    self.w(g, f"ld{b}", mk[P_ + f"ld{i}"])
                    if i >= 2:
                        self.w(g, "V", mk[P_ + f"cmb{i-2}"])
                    for st in range(BT):
                        for j in range(3):
                            ins = g.indirect_dma_start(
                                out=ygs[j][b][:, st * D:(st + 1) * D],
                                out_offset=None,
                                in_=self.y_d[:, :],
                                in_offset=IOA(adt[b][:, st * 4 + j:st * 4 + j + 1], 0))
                            self.i(ins, f"yg{b}", 16)
                    mk[P_ + f"yg{i}"] = self.N[f"yg{b}"]
                self.sec("gpsimd", g0)

                def v0(v, i=i, b=b):
                    self.w(v, f"yg{b}", mk[P_ + f"yg{i}"])
                    if i >= 2:
                        self.w(v, f"st{b}", mk[P_ + f"st{i-2}"])
                        self.w(v, "P", mk[P_ + f"cmm{i-2}"])
                    self.vi(v, v.tensor_copy(ejf[:, :], eit[b][:, :]))
                    w83 = w8[:, :].rearrange("p (a c) -> p a c", a=BT)
                    pr3 = prt[b][:, :].rearrange("p (a c) -> p a c", a=BT)
                    for j in range(3):
                        self.vi(v, v.tensor_tensor(
                            w83, self.iota84[:, 0:BT * E]
                            .rearrange("p (a c) -> p a c", a=BT),
                            bc(colk(ejf, BT, j), E), OP.is_equal))
                        self.vi(v, v.tensor_tensor(w83, w83, pr3, OP.mult))
                        self.vi(v, v.tensor_reduce(
                            pjs[:, :].rearrange("p (a c) -> p a c", a=BT)[:, :, j:j + 1],
                            w83, AX.X, OP.add))
                    p0, p1, p2 = (colk(pjs, BT, j) for j in range(3))
                    c0, c1, c2, c3, c4, c5 = (colk(scw, BT, k) for k in range(6))
                    self.vi(v, v.tensor_tensor(c0, p0, p1, OP.min))
                    self.vi(v, v.tensor_tensor(c0, c0, p2, OP.min))
                    self.vi(v, v.tensor_tensor(c3, p2, c0, OP.is_equal))
                    self.vi(v, v.tensor_scalar(c4, c3, -1.0, 1.0, OP.mult, OP.add))
                    self.vi(v, v.tensor_tensor(c5, p1, c0, OP.is_equal))
                    self.vi(v, v.tensor_tensor(c5, c5, c4, OP.mult))
                    self.vi(v, v.tensor_tensor(c1, p0, c0, OP.is_equal))
                    self.vi(v, v.tensor_tensor(c1, c1, c4, OP.mult))
                    self.vi(v, v.tensor_scalar(c2, c5, -1.0, 1.0, OP.mult, OP.add))
                    self.vi(v, v.tensor_tensor(c1, c1, c2, OP.mult))
                    for pj, dj in ((p0, c1), (p1, c5), (p2, c3)):
                        self.vi(v, v.tensor_tensor(c2, pj, dj, OP.mult))
                        self.vi(v, v.tensor_tensor(pj, pj, c2, OP.subtract))
                    self.vi(v, v.tensor_tensor(c0, p0, p1, OP.add))
                    self.vi(v, v.tensor_tensor(c0, c0, p2, OP.add))
                    uw3 = uw[b][:, :].rearrange("p (a c) -> p a c", a=BT)
                    self.vi(v, v.tensor_tensor(uw3, bc(c0, E), bc(c0, E), OP.min),
                            wait=False)
                    for j in range(3):
                        self.vi(v, v.tensor_tensor(
                            w83, self.iota84[:, 0:BT * E]
                            .rearrange("p (a c) -> p a c", a=BT),
                            bc(colk(ejf, BT, j), E), OP.is_equal))
                        self.vi(v, v.tensor_tensor(w83, w83,
                                                   bc(colk(pjs, BT, j), E), OP.mult))
                        self.vi(v, v.tensor_tensor(uw[b][:, :], uw[b][:, :], w8[:, :],
                                                   OP.subtract))
                    mk[P_ + f"cmb_u{i}"] = self.vi(v, v.engine_nop())
                self.sec("vector", v0)

                def ac0(s, i=i, b=b):
                    self.w(s, "V", mk[P_ + f"cmb_u{i}"])
                    for j, dst in ((0, moe), (1, tmd), (2, tm2)):
                        for st in range(BT):
                            wcol = pjs[:, :].rearrange("p (a c) -> p a c",
                                                       a=BT)[:, st, j:j + 1]
                            ins = s.activation(dst[:, st * D:(st + 1) * D],
                                               ygs[j][b][:, st * D:(st + 1) * D],
                                               AF.Identity, scale=wcol)
                            self.i(ins, "A", 1)
                    mk[P_ + f"cmw{i}"] = self.N["A"]
                self.sec("scalar", ac0)

                def v0b(v, i=i):
                    self.w(v, "A", mk[P_ + f"cmw{i}"])
                    self.vi(v, v.tensor_add(moe[:, :], moe[:, :], tmd[:, :]),
                            wait=False)
                    mk[P_ + f"cmb_m{i}"] = self.vi(
                        v, v.tensor_add(moe[:, :], moe[:, :], tm2[:, :]), wait=False)
                self.sec("vector", v0b)

                def t0(t, i=i, b=b):
                    self.w(t, "V", mk[P_ + f"cmb_u{i}"])
                    if i == 0:
                        self.w(t, "L", mk[P_ + "cl"])
                    if i >= 1:
                        self.w(t, "V", mk[P_ + f"cmb{i-1}"])
                    for st in range(BT):
                        ins = t.transpose(self.bank(0, st * 128, 128)[0:E, :],
                                          uw[b][:, st * E:(st + 1) * E],
                                          self.ident[:, :])
                    mk[P_ + f"ctr{i}"] = self.i(ins, "P", 1)
                self.sec("tensor", t0)

                def v1(v, i=i):
                    self.w(v, "P", mk[P_ + f"ctr{i}"])
                    mk[P_ + f"ctc{i}"] = self.vi(
                        v, v.tensor_copy(uwT[0:E, :],
                                         self.bank(0, 0, BT * 128)[0:E, :]))
                self.sec("vector", v1)

                def t1(t, i=i):
                    self.w(t, "V", mk[P_ + f"ctc{i}"])
                    for st in range(BT):
                        for half in range(2):
                            ins = t.matmul(
                                self.bank(1 + st * 2 + half, 0, 512),
                                uwT[0:E, st * 128:(st + 1) * 128],
                                cS[0:E, half * 512:(half + 1) * 512],
                                start=True, stop=True, skip_group_check=True)
                    mk[P_ + f"cmm{i}"] = self.i(ins, "P", 1)
                self.sec("tensor", t1)

                def v2(v, i=i, b=b):
                    self.w(v, "P", mk[P_ + f"cmm{i}"])
                    self.w(v, "V", mk[P_ + f"cmb_m{i}"])
                    for st in range(BT):
                        for half in range(2):
                            self.vi(v, v.tensor_add(
                                moe[:, st * D + half * 512:st * D + (half + 1) * 512],
                                moe[:, st * D + half * 512:st * D + (half + 1) * 512],
                                self.bank(1 + st * 2 + half, 0, 512)), wait=False)
                    self.vi(v, v.tensor_add(tmd[:, :], vt2[b][:, :], moe[:, :]),
                            wait=False)
                    self.vi(v, v.tensor_scalar(tmd[:, :], tmd[:, :], 0.5, None,
                                               OP.mult), wait=False)
                    mk[P_ + f"cmb{i}"] = self.vi(
                        v, v.tensor_add(xo[b][:, :], xt[b][:, :], tmd[:, :]),
                        wait=False)
                self.sec("vector", v2)


                def g1(g, i=i, b=b, rows=rows):
                    self.w(g, "V", mk[P_ + f"cmb{i}"])
                    ins = g.dma_start(out=xout[rows, :]
                                      .rearrange("(a p) c -> p a c", p=128),
                                      in_=xo[b][:, :]
                                      .rearrange("p (a c) -> p a c", a=BT))
                    mk[P_ + f"st{i}"] = self.i(ins, f"st{b}", 16)
                self.sec("gpsimd", g1)

    def phase_D(self):
        nc, mk = self.nc, self.mk
        P_ = "D_"
        with ExitStack() as eo:
            kv_sb = eo.enter_context(nc.sbuf_tensor("D_kv", [128, 520], F32))
            with ExitStack() as ea:
                sb = lambda n, sh, dt=F32: ea.enter_context(nc.sbuf_tensor(P_ + n, sh, dt))
                wq_sb = sb("wq", [128, DC * D])
                wk_sb = sb("wk", [128, DC * D])
                wv_sb = sb("wv", [128, DC * D])
                xt = [sb(f"xt{b}", [128, D]) for b in range(2)]
                v2 = [sb(f"v2{b}", [128, D]) for b in range(2)]
                v2T4 = sb("v2T4", [128, DC * 512])
                kf = sb("kf", [128, D])
                kp = sb("kp", [128, D])
                tmpi = sb("tmpi", [128, D], I32)
                ek = sb("ek", [128, D])
                vt_sb = sb("vts", [128, D])
                vh = sb("vh", [128, NH * 65])
                qs = [sb(f"qs{b}", [128, 512]) for b in range(2)]
                qe = [sb(f"qe{b}", [128, 512]) for b in range(2)]
                qf = [sb(f"qf{b}", [128, 512]) for b in range(2)]
                sc1 = sb("sc1", [128, 8])
                tmp = sb("tmp", [128, D])

                def s0(s):
                    for fc in range(DC):
                        for wsb, wsrc in ((wq_sb, self.wq), (wk_sb, self.wk),
                                          (wv_sb, self.wv)):
                            ins = s.dma_start(out=wsb[:, fc * D:(fc + 1) * D],
                                              in_=wsrc[fc * 128:(fc + 1) * 128, :])
                            self.i(ins, "L", 16)
                    mk["D_wl"] = self.N["L"]
                    for j in range(2):
                        ins = s.dma_start(out=xt[j][:, :],
                                          in_=self.x1_d[j * 128:(j + 1) * 128, :])
                        mk[P_ + f"xl{j}"] = self.i(ins, f"xl{j}", 16)
                self.sec("sync", s0)

                zf = sb("zf", [128, 512])

                def gv(g):
                    ins = g.memset(vh[:, :], 1.0)
                    self.i(ins, "Gc", 1)
                    ins = g.memset(zf[:, :], 0.0)
                    mk[P_ + "vh1"] = self.i(ins, "Gc", 1)
                self.sec("gpsimd", gv)

                def tz(t):
                    self.w(t, "Gc", mk[P_ + "vh1"])
                    t.matmul(self.bank(6, 0, 512), self.ident[:, :], zf[:, :],
                             start=True, stop=True, skip_group_check=True)
                    ins = t.matmul(self.bank(7, 0, 512), self.ident[:, :], zf[:, :],
                                   start=True, stop=True, skip_group_check=True)
                    mk[P_ + "kvz"] = self.i(ins, "P", 1)
                self.sec("tensor", tz)

                for i in range(NT):
                    b = i % 2

                    def v1(v, i=i, b=b):
                        self.w(v, f"xl{b}", mk[P_ + f"xl{i}"])
                        self.vi(v, v.tensor_reduce(sc1[:, 0:1], xt[b][:, :], AX.X, OP.add))
                        self.vi(v, v.tensor_scalar(sc1[:, 0:1], sc1[:, 0:1], 1.0 / D, None, OP.mult))
                        self.vi(v, v.tensor_scalar(v2[b][:, :], xt[b][:, :], sc1[:, 0:1], None,
                                                   OP.subtract), wait=False)
                        self.vi(v, v.tensor_tensor(tmp[:, :], v2[b][:, :], v2[b][:, :], OP.mult),
                                wait=False)
                        mk[P_ + f"ln{i}"] = self.vi(v, v.tensor_reduce(sc1[:, 1:2], tmp[:, :],
                                                                       AX.X, OP.add), wait=False)
                    self.sec("vector", v1)

                    def a1(s, i=i):
                        self.w(s, "V", mk[P_ + f"ln{i}"])
                        ins = s.activation(sc1[:, 2:3], sc1[:, 1:2], AF.Sqrt,
                                           bias=self.cst[:, 0:1], scale=1.0 / D)
                        mk[P_ + f"rs{i}"] = self.i(ins, "A", 1)
                    self.sec("scalar", a1)

                    def v2f(v, i=i, b=b):
                        self.w(v, "A", mk[P_ + f"rs{i}"])
                        self.vi(v, v.reciprocal(sc1[:, 2:3], sc1[:, 2:3]))
                        mk[P_ + f"v2{i}"] = self.vi(v, v.tensor_scalar(v2[b][:, :], v2[b][:, :],
                                                                       sc1[:, 2:3], None, OP.mult),
                                                    wait=False)
                    self.sec("vector", v2f)

                    def t1(t, i=i, b=b):
                        self.w(t, "V", mk[P_ + f"v2{i}"])
                        if i >= 1:
                            self.w(t, "V", mk[P_ + f"kf{i-1}"])
                        for fc in range(DC):
                            ins = t.transpose(self.bank(fc % 2, (fc // 2) * 128, 128),
                                              v2[b][:, fc * 128:(fc + 1) * 128],
                                              self.ident[:, :])
                        mk[P_ + f"vtr{i}"] = self.i(ins, "P", 1)
                    self.sec("tensor", t1)

                    def a3c(s, i=i):
                        self.w(s, "P", mk[P_ + f"vtr{i}"])
                        if i >= 4:
                            self.w(s, "P", mk[P_ + f"qmm{(i//4 - 1)*8 + 7}"])
                        g = i % 4
                        for fc in range(DC):
                            ins = s.activation(
                                v2T4[:, fc * 512 + g * 128:fc * 512 + (g + 1) * 128],
                                self.bank(fc % 2, (fc // 2) * 128, 128), AF.Identity)
                            m = self.i(ins, "A", 1)
                        mk[P_ + f"vTc{i}"] = m
                    self.sec("scalar", a3c)

                    def t2(t, i=i):
                        self.w(t, "A", mk[P_ + f"vTc{i}"])
                        if i == 0:
                            self.w(t, "L", mk["D_wl"])
                        if i >= 1:
                            self.w(t, "A", mk[P_ + f"vev{i-1}"])
                        g = i % 4
                        for half in range(2):
                            for fc in range(DC):
                                ins = t.matmul(
                                    self.bank(half, 0, 512),
                                    v2T4[:, fc * 512 + g * 128:fc * 512 + (g + 1) * 128],
                                    wk_sb[:, fc * D + half * 512:fc * D + (half + 1) * 512],
                                    start=(fc == 0), stop=(fc == DC - 1))
                        mk[P_ + f"kmm{i}"] = self.i(ins, "P", 1)
                        for half in range(2):
                            for fc in range(DC):
                                ins = t.matmul(
                                    self.bank(2 + half, 0, 512),
                                    v2T4[:, fc * 512 + g * 128:fc * 512 + (g + 1) * 128],
                                    wv_sb[:, fc * D + half * 512:fc * D + (half + 1) * 512],
                                    start=(fc == 0), stop=(fc == DC - 1))
                        mk[P_ + f"vmm{i}"] = self.i(ins, "P", 1)
                    self.sec("tensor", t2)

                    def a2(s, i=i):
                        self.w(s, "P", mk[P_ + f"kmm{i}"])
                        if i >= 1:
                            self.w(s, "V", mk[P_ + f"vh{i-1}"])
                        ins = s.activation(ek[:, :], self.psA[:, 0:1024], AF.Exp)
                        mk[P_ + f"ekx{i}"] = self.i(ins, "A", 1)
                        self.w(s, "P", mk[P_ + f"vmm{i}"])
                        ins = s.activation(vt_sb[:, :], self.psA[:, 1024:2048], AF.Identity)
                        mk[P_ + f"vev{i}"] = self.i(ins, "A", 1)
                    self.sec("scalar", a2)

                    def v4(v, i=i):
                        self.w(v, "A", mk[P_ + f"ekx{i}"])
                        if i == 0:
                            self.w(v, "Gc", mk[P_ + "vh1"])
                        self.vi(v, v.tensor_scalar(tmpi[:, :], self.psA[:, 0:1024], 0.0,
                                                   None, OP.is_gt), wait=False)
                        self.vi(v, v.tensor_scalar(kp[:, :], self.psA[:, 0:1024], 1.0,
                                                   None, OP.add), wait=False)
                        mk[P_ + f"kf{i}"] = self.vi(v, v.select(kf[:, :], tmpi[:, :],
                                                                kp[:, :], ek[:, :]), wait=False)
                        self.w(v, "A", mk[P_ + f"vev{i}"])
                        for h in range(NH):
                            m = self.vi(v, v.tensor_copy(vh[:, h * 65:h * 65 + 64],
                                                         vt_sb[:, h * 64:(h + 1) * 64]),
                                        wait=False)
                        mk[P_ + f"vh{i}"] = m
                    self.sec("vector", v4)

                    def t3(t, i=i):
                        self.w(t, "V", mk[P_ + f"kf{i}"])
                        self.w(t, "V", mk[P_ + f"vh{i}"])
                        for h in range(NH):
                            kb = 6 + h // 8
                            rem = h % 8
                            ph, cb = rem % 2, rem // 2
                            ps = self.bank(kb, cb * 65, 65)[ph * 64:ph * 64 + 64, :]
                            ins = t.matmul(ps, kf[:, h * 64:(h + 1) * 64],
                                           vh[:, h * 65:(h + 1) * 65],
                                           start=False, stop=(i == NT - 1),
                                           skip_group_check=True)
                        mk[P_ + f"kv{i}"] = self.i(ins, "P", 1)
                    self.sec("tensor", t3)

                    if i % 4 == 3:
                        gq = i // 4
                        for qc in range(DC):
                            kq = 4 + qc % 2
                            bq = qc % 2
                            idx = gq * 8 + qc

                            def tq(t, gq=gq, qc=qc, kq=kq, idx=idx):
                                if qc < 2:
                                    self.w(t, "A", mk[P_ + f"vTc{gq*4+3}"])
                                if idx >= 2:
                                    self.w(t, "A", mk[P_ + f"qcp{idx-2}"])
                                for fc in range(DC):
                                    ins = t.matmul(
                                        self.bank(kq, 0, 512),
                                        wq_sb[:, fc * D + qc * 128:fc * D + (qc + 1) * 128],
                                        v2T4[:, fc * 512:(fc + 1) * 512],
                                        start=(fc == 0), stop=(fc == DC - 1))
                                mk[P_ + f"qmm{idx}"] = self.i(ins, "P", 1)
                            self.sec("tensor", tq)

                            def aq(s, qc=qc, kq=kq, bq=bq, idx=idx):
                                self.w(s, "P", mk[P_ + f"qmm{idx}"])
                                if idx >= 2:
                                    self.w(s, "V", mk[P_ + f"qfv{idx-2}"])
                                self.ai(s, s.activation(qe[bq][:, :], self.bank(kq, 0, 512),
                                                        AF.Exp))
                                mk[P_ + f"qcp{idx}"] = self.ai(
                                    s, s.activation(qs[bq][:, :], self.bank(kq, 0, 512),
                                                    AF.Identity))
                            self.sec("scalar", aq)

                            def vq(v, bq=bq, idx=idx):
                                self.w(v, "A", mk[P_ + f"qcp{idx}"])
                                if idx >= 2:
                                    self.w(v, f"st{bq}", mk[P_ + f"qst{idx-2}"])
                                self.vi(v, v.tensor_scalar(tmpi[:, 0:512], qs[bq][:, :], 0.0,
                                                           None, OP.is_gt), wait=False)
                                self.vi(v, v.tensor_scalar(qs[bq][:, :], qs[bq][:, :], 1.0,
                                                           None, OP.add), wait=False)
                                mk[P_ + f"qfv{idx}"] = self.vi(
                                    v, v.select(qf[bq][:, :], tmpi[:, 0:512],
                                                qs[bq][:, :], qe[bq][:, :]), wait=False)
                            self.sec("vector", vq)

                            def gq_(g, gq=gq, qc=qc, bq=bq, idx=idx):
                                self.w(g, "V", mk[P_ + f"qfv{idx}"])
                                for tt4 in range(4):
                                    ins = g.dma_start(
                                        out=self.qf_d[gq * 4 + tt4, :, qc, :],
                                        in_=qf[bq][:, tt4 * 128:(tt4 + 1) * 128])
                                    self.i(ins, f"st{bq}", 16)
                                mk[P_ + f"qst{idx}"] = self.N[f"st{bq}"]
                            self.sec("gpsimd", gq_)

                    if i + 2 < NT:
                        def s1(s, i=i, b=b):
                            self.w(s, "V", mk[P_ + f"v2{i}"])
                            ins = s.dma_start(out=xt[b][:, :],
                                              in_=self.x1_d[(i + 2) * 128:(i + 3) * 128, :])
                            mk[P_ + f"xl{i+2}"] = self.i(ins, f"xl{b}", 16)
                        self.sec("sync", s1)

                def vkv(v):
                    self.w(v, "P", mk[P_ + f"kv{NT-1}"])
                    self.vi(v, v.tensor_copy(kv_sb[:, 0:260], self.bank(6, 0, 260)), wait=False)
                    mk[P_ + "kvc"] = self.vi(v, v.tensor_copy(kv_sb[:, 260:520],
                                                              self.bank(7, 0, 260)))
                self.sec("vector", vkv)

                self.barrier()

            with ExitStack() as eb:
                sb = lambda n, sh, dt=F32: eb.enter_context(nc.sbuf_tensor("Db" + n, sh, dt))
                wo_sb = sb("wo", [128, DC * D])
                qfs = [sb(f"qfs{b}", [128, DC * 128]) for b in range(2)]
                x1t = [sb(f"x1t{b}", [128, D]) for b in range(2)]
                cat = sb("cat", [128, D])
                catT = sb("catT", [128, D])
                xa = [sb(f"xa{b}", [128, D]) for b in range(2)]
                rd = sb("rd", [128, NH])

                def s0(s):
                    for fc in range(DC):
                        ins = s.dma_start(out=wo_sb[:, fc * D:(fc + 1) * D],
                                          in_=self.wo[fc * 128:(fc + 1) * 128, :])
                        self.i(ins, "L", 16)
                    mk["Db_wl"] = self.N["L"]
                    for j in range(2):
                        ins = s.dma_start(
                            out=qfs[j][:, :].rearrange("p (a c) -> p a c", a=DC),
                            in_=self.qf_d[j, :, :, :])
                        self.i(ins, f"qf{j}", 16)
                        mk[P_ + f"ql{j}"] = self.N[f"qf{j}"]
                        ins = s.dma_start(out=x1t[j][:, :],
                                          in_=self.x1_d[j * 128:(j + 1) * 128, :])
                        mk[P_ + f"x1l{j}"] = self.i(ins, f"ld{j}", 16)
                self.sec("sync", s0)

                for i in range(NT):
                    b = i % 2

                    def t1(t, i=i, b=b):
                        self.w(t, f"qf{b}", mk[P_ + f"ql{i}"])
                        if i == 0:
                            self.w(t, "V", mk[P_ + "kvc"])
                            self.w(t, "L", mk["Db_wl"])
                        if i >= 1:
                            self.w(t, "V", mk[P_ + f"cm{i-1}"])
                        for hj in range(NH):
                            h = (hj % 4) * 4 + hj // 4
                            lhsT = qfs[b][:, (h // 2) * 128:(h // 2 + 1) * 128]
                            lhsT = lhsT[(h % 2) * 64:(h % 2) * 64 + 64, :]
                            rhs = kv_sb[(h % 2) * 64:(h % 2) * 64 + 64,
                                        (h // 8) * 260 + ((h // 2) % 4) * 65:
                                        (h // 8) * 260 + ((h // 2) % 4) * 65 + 65]
                            ins = t.matmul(self.bank(h // 4, (h % 4) * 65, 65),
                                           lhsT, rhs,
                                           start=(hj < 4), stop=True,
                                           skip_group_check=True)
                        mk[P_ + f"hm{i}"] = self.i(ins, "P", 1)
                    self.sec("tensor", t1)

                    def a1(s, i=i):
                        self.w(s, "P", mk[P_ + f"hm{i}"])
                        if i >= 1:
                            self.w(s, "V", mk[P_ + f"cm{i-1}"])
                        for h in range(NH):
                            ins = s.activation(rd[:, h:h + 1],
                                               self.bank(h // 4, (h % 4) * 65 + 64, 1),
                                               AF.Identity, bias=self.cst[:, 1:2])
                            self.i(ins, "A", 1)
                        mk[P_ + f"rc{i}"] = self.N["A"]
                    self.sec("scalar", a1)

                    def v1(v, i=i):
                        self.w(v, "A", mk[P_ + f"rc{i}"])
                        self.vi(v, v.reciprocal(rd[:, 0:NH], rd[:, 0:NH]))
                        for h in range(NH):
                            m = self.vi(v, v.tensor_scalar(
                                cat[:, h * 64:(h + 1) * 64],
                                self.bank(h // 4, (h % 4) * 65, 64),
                                rd[:, h:h + 1], None, OP.mult), wait=False)
                        mk[P_ + f"cm{i}"] = m
                    self.sec("vector", v1)

                    def t2(t, i=i):
                        self.w(t, "V", mk[P_ + f"cm{i}"])
                        if i >= 1:
                            self.w(t, "V", mk[P_ + f"ctc{i-1}"])
                        for fc in range(DC):
                            ins = t.transpose(self.bank(4 + fc % 2, (fc // 2) * 128, 128),
                                              cat[:, fc * 128:(fc + 1) * 128],
                                              self.ident[:, :])
                        mk[P_ + f"ctr{i}"] = self.i(ins, "P", 1)
                    self.sec("tensor", t2)

                    def v2(v, i=i):
                        self.w(v, "P", mk[P_ + f"ctr{i}"])
                        for fc in range(DC):
                            m = self.vi(v, v.tensor_copy(catT[:, fc * 128:(fc + 1) * 128],
                                                         self.bank(4 + fc % 2,
                                                                   (fc // 2) * 128, 128)),
                                        wait=False)
                        mk[P_ + f"ctc{i}"] = m
                    self.sec("vector", v2)

                    def t3(t, i=i):
                        self.w(t, "V", mk[P_ + f"ctc{i}"])
                        if i >= 1:
                            self.w(t, "V", mk[P_ + f"x2v{i-1}"])
                        for half in range(2):
                            for fc in range(DC):
                                ins = t.matmul(
                                    self.bank(6 + half, 0, 512),
                                    catT[:, fc * 128:(fc + 1) * 128],
                                    wo_sb[:, fc * D + half * 512:fc * D + (half + 1) * 512],
                                    start=(fc == 0), stop=(fc == DC - 1))
                        mk[P_ + f"omm{i}"] = self.i(ins, "P", 1)
                    self.sec("tensor", t3)

                    def v3(v, i=i, b=b):
                        self.w(v, "P", mk[P_ + f"omm{i}"])
                        self.w(v, f"ld{b}", mk[P_ + f"x1l{i}"])
                        if i >= 2:
                            self.w(v, f"st{b}", mk[P_ + f"x2st{i-2}"])
                        mk[P_ + f"x2v{i}"] = self.vi(v, v.tensor_add(xa[b][:, :], x1t[b][:, :],
                                                                     self.psB[:, 1024:2048]),
                                                     wait=False)
                    self.sec("vector", v3)

                    def g1(g, i=i, b=b):
                        self.w(g, "V", mk[P_ + f"x2v{i}"])
                        ins = g.dma_start(out=self.x2_d[i * 128:(i + 1) * 128, :],
                                          in_=xa[b][:, :])
                        mk[P_ + f"x2st{i}"] = self.i(ins, f"st{b}", 16)
                    self.sec("gpsimd", g1)

                    if i + 2 < NT:
                        def s1(s, i=i, b=b):
                            self.w(s, "P", mk[P_ + f"hm{i}"])
                            ins = s.dma_start(
                                out=qfs[b][:, :].rearrange("p (a c) -> p a c", a=DC),
                                in_=self.qf_d[i + 2, :, :, :])
                            self.i(ins, f"qf{b}", 16)
                            mk[P_ + f"ql{i+2}"] = self.N[f"qf{b}"]
                            self.w(s, "V", mk[P_ + f"x2v{i}"])
                            ins = s.dma_start(out=x1t[b][:, :],
                                              in_=self.x1_d[(i + 2) * 128:(i + 3) * 128, :])
                            mk[P_ + f"x1l{i+2}"] = self.i(ins, f"ld{b}", 16)
                        self.sec("sync", s1)

    def build(self):
        nc = self.nc
        with ExitStack() as es:
            for s in _SEMS:
                self.S[s] = es.enter_context(nc.semaphore(s))
                self.N[s] = 0
            sb = lambda n, sh, dt=F32: es.enter_context(nc.sbuf_tensor(n, sh, dt))
            self.ident = sb("ident", [128, 128])
            self.iota8 = sb("iota8", [128, E])
            self.eights = sb("eights", [128, E])
            self.zeros_i = sb("zeros_i", [128, 128], I32)
            self.gamma_bc = sb("gamma_bc", [128, D])
            self.beta_bc = sb("beta_bc", [128, D])
            self.br_bc = sb("br_bc", [128, E])
            self.b1_rows = sb("b1_rows", [128, 256])
            self.b2_rows = sb("b2_rows", [64, 128])
            self.b1_sb = sb("b1_sb", [128, 256])
            self.b2_sb = sb("b2_sb", [128, 64])
            self.rb1_sb = sb("rb1_sb", [128, 256])
            self.wr_sb = sb("wr_sb", [128, DC * E])
            self.ctot_bc = sb("ctot_bc", [128, D])
            self.cst = sb("cst", [128, 4])
            self.UT = sb("UT", [128, 128])
            self.ones1 = sb("ones1", [1, 128])
            self.onesc = sb("onesc", [128, 1])
            self.iota84 = sb("iota84", [128, 32])
            self.eights4 = sb("eights4", [128, 32])
            self.psA = es.enter_context(nc.psum_tensor("psA", [128, 2048], F32))
            self.psB = es.enter_context(nc.psum_tensor("psB", [128, 2048], F32))

            self.block = es.enter_context(nc.Block())

            steps = [
                ("pro", lambda: self.prologue()),
                ("A", lambda: self.phase_A(0, self.x_in, with_b0=True)),
                ("B", lambda: self.phase_B(0)),
                ("C", lambda: self.phase_C(0, self.x_in, self.x1_d)),
                ("D", lambda: self.phase_D()),
                ("A2", lambda: self.phase_A(1, self.x2_d)),
                ("B2", lambda: self.phase_B(1)),
                ("C2", lambda: self.phase_C(1, self.x2_d, self.out)),
            ]
            for name, fn in steps:
                fn()
                self.barrier()
                if self.upto == name:
                    break
        return nc


_CACHE = {}


def build_program(debug=False, upto="all"):
    key = (bool(debug), upto)
    if key not in _CACHE:
        _CACHE[key] = MK(debug=debug, upto=upto).build()
    return _CACHE[key]


def kernel(L0_attn_x, L0_attn_Wq, L0_attn_Wk, L0_attn_Wv, L0_attn_Wo,
           L0_ffn_gamma, L0_ffn_beta, L0_ffn_Wr, L0_ffn_br,
           L0_ffn_W1, L0_ffn_b1, L0_ffn_W2, L0_ffn_b2):
    nc = build_program(debug=False)
    x = np.ascontiguousarray(np.asarray(L0_attn_x, dtype=np.float32))
    com = {
        "wq": np.ascontiguousarray(L0_attn_Wq, dtype=np.float32),
        "wk": np.ascontiguousarray(L0_attn_Wk, dtype=np.float32),
        "wv": np.ascontiguousarray(L0_attn_Wv, dtype=np.float32),
        "wo": np.ascontiguousarray(L0_attn_Wo, dtype=np.float32),
        "wr": np.ascontiguousarray(L0_ffn_Wr, dtype=np.float32),
        "gamma": np.ascontiguousarray(L0_ffn_gamma, dtype=np.float32),
        "beta": np.ascontiguousarray(L0_ffn_beta, dtype=np.float32),
        "brr": np.ascontiguousarray(L0_ffn_br, dtype=np.float32),
        "w1": np.ascontiguousarray(L0_ffn_W1, dtype=np.float32),
        "b1": np.ascontiguousarray(L0_ffn_b1, dtype=np.float32),
        "w2": np.ascontiguousarray(L0_ffn_W2, dtype=np.float32),
        "b2": np.ascontiguousarray(L0_ffn_b2, dtype=np.float32),
    }
    in_maps = [dict(com, x=x[b]) for b in range(8)]
    res = run_bass_kernel_spmd(nc, in_maps, core_ids=list(range(8)))
    return np.stack([np.asarray(res.results[b]["out"]) for b in range(8)], axis=0)
